# version 12
# speedup vs baseline: 1.2846x; 1.0041x over previous
import numpy as np
import ml_dtypes
from contextlib import ExitStack

import concourse.bass as bass
import concourse.mybir as mybir
from concourse.bass_utils import run_bass_kernel_spmd

F32 = mybir.dt.float32
BF16 = mybir.dt.bfloat16
AF = mybir.ActivationFunctionType
ALU = mybir.AluOpType
AX = mybir.AxisListType
BF = ml_dtypes.bfloat16

NCORES = 8
SEQ = 8192
DM = 2048
TPC = SEQ // NCORES

ENGS = ("tensor", "vector", "scalar", "gpsimd", "sync")


class Prog:

    SEMOF = {"tensor": "pe", "vector": "dve", "scalar": "act", "gpsimd": "gp"}

    def __init__(self, nc, ctx, semnames):
        self.nc = nc
        self.ctx = ctx
        self.q = {e: [] for e in ENGS}
        names = set(semnames) | set(self.SEMOF.values())
        self.h = {n: ctx.enter_context(nc.semaphore(n)) for n in names}
        self.c = {n: 0 for n in names}
        self.rings = {}
        self.pending = {}

    def arm_barrier(self, waits):
        for e_ in ENGS:
            self.pending[e_] = list(waits)

    def sem(self, name):
        if name not in self.h:
            self.h[name] = self.ctx.enter_context(self.nc.semaphore(name))
            self.c[name] = 0
        return name

    def emit(self, eng, fn, waits=(), inc=None, inck=1, gpser=False):
        waits = [(n, v) for (n, v) in waits if v is not None]
        pend = self.pending.pop(eng, None)
        if pend:
            waits = pend + waits
        is_dma = (inck == 16)
        esem = self.SEMOF.get(eng)
        if esem is not None and not is_dma:
            assert inc is None or inc == esem, (eng, inc)
            prev = self.c[esem]
            self.c[esem] += 1
            self.q[eng].append((
                [(self.h[n], v) for (n, v) in waits],
                fn, self.h[esem], 1, prev))
            return self.c[esem]
        if inc is not None:
            self.c[inc] += inck
        self.q[eng].append((
            [(self.h[n], v) for (n, v) in waits],
            fn, self.h[inc] if inc else None, inck, None))
        return self.c[inc] if inc else None

    def dma(self, eng, fn, ring, depth, waits=()):
        i = self.rings.get(ring, 0)
        self.rings[ring] = i + 1
        n = self.sem(f"{ring}{i % depth}")
        c = self.emit(eng, fn, waits=waits, inc=n, inck=16)
        return (n, c)

    def v(self, name):
        return self.c[name]

    def replay(self, block):
        for eng in ENGS:
            if not self.q[eng]:
                continue
            queue = self.q[eng]
            esem = self.h.get(self.SEMOF.get(eng))

            def body(e, queue=queue, esem=esem):
                for (waits, fn, sem, inck, prev) in queue:
                    for (h, val) in waits:
                        e.wait_ge(h, val)
                    if prev is not None and prev > 0:
                        e.wait_ge(esem, prev)
                    instr = fn(e)
                    if sem is not None:
                        if instr is None:
                            instr = e.engine_nop() if hasattr(e, "engine_nop") else e.nop()
                        instr.then_inc(sem, inck)
            getattr(block, eng)(body)



F32 = mybir.dt.float32
BF16 = mybir.dt.bfloat16
AF = mybir.ActivationFunctionType
ALU = mybir.AluOpType
AX = mybir.AxisListType

SEQ = 8192
NSEG = 8
TS = 1024
NG = 8
NCOLT = 12
PH = 99


def build_launch1(Prog, dbg=False):
    nc = bass.Bass("TRN2", target_bir_lowering=False, debug=False)
    hT_h = nc.dram_tensor("hidT", [2048, SEQ], F32, kind="ExternalInput")
    wc_h = nc.dram_tensor("Wcat", [2048, 1544], F32, kind="ExternalInput")
    cw_h = nc.dram_tensor("cw", [128, 32], F32, kind="ExternalInput")
    dtb_h = nc.dram_tensor("dtb", [4, 1], F32, kind="ExternalInput")
    alog_h = nc.dram_tensor("alog", [4, 1], F32, kind="ExternalInput")
    nw_h = nc.dram_tensor("nwbc", [128, 128], F32, kind="ExternalInput")
    idn_h = nc.dram_tensor("idnb", [128, 128], BF16, kind="ExternalInput")
    msk_h = nc.dram_tensor("masks", [128, 256], F32, kind="ExternalInput")
    nc_gout_handle = nc.dram_tensor("gout", [SEQ, 512], BF16, kind="ExternalOutput")
    gout_o = nc_gout_handle.ap()
    pkgd = nc.dram_tensor("pkgd", [28, TS], F32).ap()
    rowd = nc.dram_tensor("rowd", [3, 4 * TS], F32).ap()

    with ExitStack() as ctx:
        e = ctx.enter_context
        p = Prog(nc, ctx, [])
        wsb = e(nc.sbuf_tensor("wsb", [128, 16 * 1544], BF16))
        hb = e(nc.sbuf_tensor("hb", [128, 16 * 512], BF16))
        mr = e(nc.sbuf_tensor("mr", [128, 8 * 516], F32))
        cs = e(nc.sbuf_tensor("cs", [128, 512], F32))
        qs = e(nc.sbuf_tensor("qs", [128, 2 * TS], BF16))
        ks = e(nc.sbuf_tensor("ks", [128, 2 * TS], BF16))
        vs = e(nc.sbuf_tensor("vs", [128, 4 * TS], BF16))
        zs = e(nc.sbuf_tensor("zs", [128, 4 * TS], BF16))
        sq = e(nc.sbuf_tensor("sq", [128, 512], BF16))
        ba = e(nc.sbuf_tensor("ba", [8, 2 * 512], F32))
        gr = e(nc.sbuf_tensor("gr", [4, TS], F32))
        gc = e(nc.sbuf_tensor("gc", [4, TS], F32))
        br = e(nc.sbuf_tensor("br", [4, TS], F32))
        lnk = e(nc.sbuf_tensor("lnk", [4, TS], F32))
        lnq = e(nc.sbuf_tensor("lnq", [4, TS], F32))
        r1 = e(nc.sbuf_tensor("r1", [4, TS], F32))
        r2 = e(nc.sbuf_tensor("r2", [4, TS], F32))
        r3 = e(nc.sbuf_tensor("r3", [4, TS], F32))
        r4 = e(nc.sbuf_tensor("r4", [4, TS], F32))
        b1r = e(nc.sbuf_tensor("b1r", [1, 4 * TS], F32))
        egls = e(nc.sbuf_tensor("egls", [128, 64], F32))
        pkg28 = e(nc.sbuf_tensor("pkg28", [28, TS], F32))
        colsb = e(nc.sbuf_tensor("colsb", [128, 28], F32))
        s2q4 = e(nc.sbuf_tensor("s2q4", [4, TS], F32))
        s2k4 = e(nc.sbuf_tensor("s2k4", [4, TS], F32))
        qtm = e(nc.sbuf_tensor("qtm", [128, 2 * 128], BF16))
        ktm = e(nc.sbuf_tensor("ktm", [128, 2 * 128], BF16))
        vtm = e(nc.sbuf_tensor("vtm", [128, 4 * 128], BF16))
        ztm = e(nc.sbuf_tensor("ztm", [128, 4 * 128], BF16))
        asb = e(nc.sbuf_tensor("asb", [128, 4 * 128], BF16))
        atsb = e(nc.sbuf_tensor("atsb", [128, 4 * 128], BF16))
        antb = e(nc.sbuf_tensor("antb", [128, 4 * 128], BF16))
        ttsb = e(nc.sbuf_tensor("ttsb", [128, 4 * 128], BF16))
        ttf = e(nc.sbuf_tensor("ttf", [128, 4 * 128], F32))
        xsb = e(nc.sbuf_tensor("xsb", [128, 256], BF16))
        usb = e(nc.sbuf_tensor("usb", [128, 4 * 128], F32))
        wfm = e(nc.sbuf_tensor("wfm", [128, 4 * 128], BF16))
        kdt = e(nc.sbuf_tensor("kdt", [128, 128], BF16))
        vnb = e(nc.sbuf_tensor("vnb", [64, 128], BF16))
        sscol = e(nc.sbuf_tensor("sscol", [128, 2], F32))
        l2c = e(nc.sbuf_tensor("l2c", [4, 2], F32))
        Sf = e(nc.sbuf_tensor("Sf", [128, 4 * 128], F32))
        Sb = e(nc.sbuf_tensor("Sb", [128, 4 * 128], BF16))
        goutb = e(nc.sbuf_tensor("goutb", [128, 512], BF16))
        cwsb = e(nc.sbuf_tensor("cwsb", [128, 32], F32))
        dtb = e(nc.sbuf_tensor("dtbS", [4, 1], F32))
        nalog = e(nc.sbuf_tensor("nalog", [4, 1], F32))
        nwbc = e(nc.sbuf_tensor("nwbcS", [128, 128], F32))
        idnb = e(nc.sbuf_tensor("idnbS", [128, 128], BF16))
        msl = e(nc.sbuf_tensor("mslS", [128, 128], F32))
        mui = e(nc.sbuf_tensor("muiS", [128, 128], F32))
        idnf = e(nc.sbuf_tensor("idnf", [128, 128], F32))
        onecb = e(nc.sbuf_tensor("onecb", [128, 1], BF16))
        oner = e(nc.sbuf_tensor("oner", [1, 128], F32))
        sel0 = e(nc.sbuf_tensor("sel0", [128, 4], BF16))
        sel1 = e(nc.sbuf_tensor("sel1", [128, 4], BF16))
        e6c = e(nc.sbuf_tensor("e6c", [128, 1], F32))
        one4 = e(nc.sbuf_tensor("one4", [4, 1], F32))
        ps0 = e(nc.psum_tensor("ps0", [128, 512], F32))
        ps1 = e(nc.psum_tensor("ps1", [128, 512], F32))
        ps2 = e(nc.psum_tensor("ps2", [128, 512], F32))
        ps3 = e(nc.psum_tensor("ps3", [128, 512], F32))
        ps4 = e(nc.psum_tensor("ps4", [128, 512], F32))
        ps5 = e(nc.psum_tensor("ps5", [128, 512], F32))
        ps6 = e(nc.psum_tensor("ps6", [128, 1024], BF16))
        ps7 = e(nc.psum_tensor("ps7", [128, 512], F32))

        p.emit("gpsimd", lambda g: g.memset(onecb[:, :], 1.0), gpser=True)
        p.emit("gpsimd", lambda g: g.memset(oner[:1, :], 1.0), gpser=True)
        p.emit("gpsimd", lambda g: g.memset(e6c[:, :], 1e-6), gpser=True)
        p.emit("gpsimd", lambda g: g.memset(one4[:4, :], 1.0), gpser=True)
        p.emit("gpsimd", lambda g: g.memset(sel0[:, :], 0.0), gpser=True)
        p.emit("gpsimd", lambda g: g.memset(sel1[:, :], 0.0), gpser=True)
        p.emit("gpsimd", lambda g: g.memset(sel0[:, 0:2], 1.0), gpser=True)
        p.emit("gpsimd", lambda g: g.memset(sel1[:, 2:4], 1.0), gpser=True)
        p.emit("gpsimd", lambda g: g.memset(mr[:, :], 0.0), gpser=True)
        p.emit("gpsimd", lambda g: g.memset(Sf[:, :], 0.0), gpser=True)
        p.emit("gpsimd", lambda g: g.memset(Sb[:, :], 0.0), gpser=True)
        init_gp = p.v("gp")
        for kt in range(16):
            p.dma("gpsimd", lambda g, kt=kt: g.dma_start(
                out=wsb[:, kt * 1544:(kt + 1) * 1544],
                in_=wc_h.ap()[kt * 128:(kt + 1) * 128, :]), "lw", 4)
        wld = [(n, p.c[n]) for n in ("lw0", "lw1", "lw2", "lw3")]
        cw_ld = p.dma("sync", lambda sy: sy.dma_start(out=cwsb[:, :],
                                                      in_=cw_h.ap()[:, :]), "lc", 1)
        dtb_ld = p.dma("sync", lambda sy: sy.dma_start(out=dtb[:4, :],
                                                       in_=dtb_h.ap()[:, :]), "lc2", 1)
        al_ld = p.dma("sync", lambda sy: sy.dma_start(out=nalog[:4, :],
                                                      in_=alog_h.ap()[:, :]), "lc3", 1)
        nw_ld = p.dma("sync", lambda sy: sy.dma_start(out=nwbc[:, :],
                                                      in_=nw_h.ap()[:, :]), "lc4", 1)
        id_ld = p.dma("sync", lambda sy: sy.dma_start(out=idnb[:, :],
                                                      in_=idn_h.ap()[:, :]), "lc5", 1)
        mk_ld = p.dma("sync", lambda sy: sy.dma_start(out=msl[:, :],
                                                      in_=msk_h.ap()[:, 0:128]),
                      "lc6", 1)
        mk2_ld = p.dma("sync", lambda sy: sy.dma_start(out=mui[:, :],
                                                       in_=msk_h.ap()[:, 128:256]),
                       "lc7", 1)
        c_ea = p.emit("scalar", lambda a: a.activation(
            out=nalog[:4, :], in_=nalog[:4, :], func=AF.Exp), waits=[al_ld],
            inc="act")
        c_na = p.emit("vector", lambda v: v.tensor_scalar(
            nalog[:4, :], nalog[:4, :], -1.0, None, ALU.mult),
            waits=[("act", c_ea)], inc="dve")
        c_idf = p.emit("vector", lambda v: v.tensor_copy(idnf[:, :], idnb[:, :]),
                       waits=[id_ld], inc="dve")

        hb_free = None
        seg_scan_done = None
        prev_pkgT = None
        st_gout = {}
        grp_ps01_free = None
        grp_ps4_free = None
        grp_ps3_free = None
        prev_k1 = None
        prevs = {}
        tr_i = [0]
        trev = {}
        kdc = e(nc.sbuf_tensor("kdc", [64, 128], BF16))
        ss_ev = [None]
        antc = e(nc.sbuf_tensor("antc", [64, 64], BF16))
        qec = e(nc.sbuf_tensor("qec", [64, 1], F32))

        for s in range(NSEG):
            ev_ft = [None] * 13
            for hf in range(2):
                base_t = s * TS + hf * 512
                hld = [None] * 16
                for kt in range(16):
                    w_h = [("pe", hb_free)] if hb_free is not None else []
                    hld[kt] = p.dma("gpsimd", lambda g2, kt=kt, base_t=base_t:
                                    g2.dma_start(
                                        out=hb[:, kt * 512:(kt + 1) * 512],
                                        in_=hT_h.ap()[kt * 128:(kt + 1) * 128,
                                                      base_t:base_t + 512]),
                                    "lh", 4, waits=w_h)
                for ct in range(13):
                    pp = [ps0, ps1][ct % 2]
                    if ct < 12:
                        def mmp(t, ct=ct, pp=pp):
                            r = None
                            for kt in range(16):
                                r = t.matmul(pp[:, :],
                                             wsb[:, kt * 1544 + ct * 128:
                                                 kt * 1544 + ct * 128 + 128],
                                             hb[:, kt * 512:(kt + 1) * 512],
                                             start=(kt == 0), stop=(kt == 15))
                            return r
                    else:
                        pp = ps2

                        def mmp(t, pp=pp):
                            r = None
                            for kt in range(16):
                                r = t.matmul(pp[0:8, :],
                                             wsb[:, kt * 1544 + 1536:
                                                 kt * 1544 + 1544],
                                             hb[:, kt * 512:(kt + 1) * 512],
                                             start=(kt == 0), stop=(kt == 15))
                            return r
                    w_mm = [hld[12], hld[13], hld[14], hld[15]]
                    if (s, hf, ct) == (0, 0, 0):
                        w_mm += wld
                    if (hf, ct) == (0, 0) and s > 0:
                        w_mm.append(("dve", seg_scan_done))
                    if ct >= 2:
                        w_mm.append(("dve", ev_ft[ct - 2]))
                    elif ev_ft[10 + ct] is not None:
                        w_mm.append(("dve", ev_ft[10 + ct]))
                    mmc = p.emit("tensor", mmp, waits=w_mm, inc="pe")
                    if ct == 12:
                        hb_free = mmc
                    if ct < 8:
                        ft = ct
                        w_ev = [("pe", mmc)]
                        if (s, hf) == (0, 0):
                            w_ev += [("gp", init_gp), cw_ld]
                        p.emit("vector", lambda v, ft=ft, pp=pp: v.tensor_copy(
                            mr[:, ft * 516 + 3:ft * 516 + 515], pp[:, :]),
                            waits=w_ev)
                        p.emit("vector", lambda v, ft=ft: v.tensor_scalar(
                            cs[:, :], mr[:, ft * 516 + 3:ft * 516 + 515],
                            cwsb[:, ft * 4 + 3:ft * 4 + 4], None, ALU.mult))
                        for j in range(3):
                            p.emit("vector", lambda v, ft=ft, j=j:
                                   v.scalar_tensor_tensor(
                                       cs[:, :], mr[:, ft * 516 + j:ft * 516 + j + 512],
                                       cwsb[:, ft * 4 + j:ft * 4 + j + 1], cs[:, :],
                                       ALU.mult, ALU.add))
                        c_halo = p.emit("vector", lambda v, ft=ft: v.tensor_copy(
                            mr[:, ft * 516:ft * 516 + 3],
                            mr[:, ft * 516 + 512:ft * 516 + 515]), inc="dve")
                        ev_ft[ct] = c_halo
                        dst = [qs, qs, ks, ks, vs, vs, vs, vs][ct]
                        hd = [0, 1, 0, 1, 0, 1, 2, 3][ct]
                        c_si = p.emit("scalar", lambda a, dst=dst, hd=hd, hf=hf:
                                      a.activation(
                                          out=dst[:, hd * TS + hf * 512:
                                                  hd * TS + hf * 512 + 512],
                                          in_=cs[:, :], func=AF.Silu),
                                      waits=[("dve", c_halo)], inc="act")
                        if ct < 4:
                            c_sq = p.emit("scalar", lambda a, dst=dst, hd=hd, hf=hf:
                                          a.activation(
                                              out=sq[:, 0:512],
                                              in_=dst[:, hd * TS + hf * 512:
                                                      hd * TS + hf * 512 + 512],
                                              func=AF.Square), inc="act")
                            sel = [sel0, sel1][hd]
                            psq = ps7
                            w_cm = [("act", c_sq)]
                            if ct in (0, 2) and ss_ev[0] is not None:
                                w_cm.append(("dve", ss_ev[0]))
                            c_m = p.emit("tensor", lambda t, sel=sel, ct=ct: t.matmul(
                                psq[0:4, :], sel[:, :], sq[:, 0:512],
                                start=(ct in (0, 2)), stop=(ct in (1, 3))),
                                waits=w_cm, inc="pe")
                            if ct in (1, 3):
                                d4 = [s2q4, s2k4][ct // 2]
                                ss_ev[0] = p.emit("vector",
                                                  lambda v, d4=d4, hf=hf: v.tensor_copy(
                                                      d4[:4, hf * 512:hf * 512 + 512],
                                                      psq[0:4, :]),
                                                  waits=[("pe", c_m)] +
                                                  ([("act", prevs["q1"])]
                                                   if "q1" in prevs else []),
                                                  inc="dve")
                        ev_ft[ct] = p.emit("vector", lambda v: None,
                                           waits=[("act", c_si)], inc="dve")
                    elif ct < 12:
                        hd = ct - 8
                        c_si = p.emit("scalar", lambda a, hd=hd, hf=hf, pp=pp:
                                      a.activation(
                                          out=zs[:, hd * TS + hf * 512:
                                                  hd * TS + hf * 512 + 512],
                                          in_=pp[:, :], func=AF.Silu),
                                      waits=[("pe", mmc)], inc="act")
                        ev_ft[ct] = p.emit("vector", lambda v: None,
                                           waits=[("act", c_si)], inc="dve")
                    else:
                        ev_ft[ct] = p.emit("vector", lambda v, hf=hf: v.tensor_copy(
                            ba[0:8, hf * 512:hf * 512 + 512], ps2[0:8, :]),
                            waits=[("pe", mmc)] +
                            ([("act", prevs["be"]), prevs["b1"]]
                             if "be" in prevs else []), inc="dve")
            p1_done = p.v("dve")

            if PH < 2:
                continue
            c_b1 = p.dma("sync", lambda sy: sy.dma_start(
                out=rowd[0:1, :].rearrange("a (p t) -> (a p) t", p=4),
                in_=ba[4:8, :]), "rb", 1,
                waits=[("dve", p1_done)] + ([prevs["rl0"]] if "rl0" in prevs
                                            else []))
            c_b2 = p.dma("sync", lambda sy: sy.dma_start(
                out=gr[:4, :], in_=rowd[0:1, :].rearrange(
                    "a (p t) -> (a p) t", p=4)), "rb2", 1,
                waits=[c_b1] + ([("dve", prevs["rows_dve"])]
                               if "rows_dve" in prevs else []))
            c_be = p.emit("scalar", lambda a: a.activation(
                out=br[:4, :], in_=ba[0:4, :], func=AF.Sigmoid),
                waits=[("dve", p1_done)] +
                ([prevs["st7"], ("dve", prevs["rows_dve"])]
                 if "st7" in prevs else []), inc="act")
            c_lb = p.emit("scalar", lambda a: a.activation(
                out=r1[:4, :], in_=br[:4, :], func=AF.Ln),
                waits=[prevs["st6"]] if "st6" in prevs else [], inc="act")
            c_ga = p.emit("vector", lambda v: v.tensor_scalar(
                gr[:4, :], gr[:4, :], dtb[0:4, 0:1], None, ALU.add),
                waits=[c_b2, dtb_ld], inc="dve")
            c_ab = p.emit("scalar", lambda a: a.activation(
                out=r2[:4, :], in_=gr[:4, :], func=AF.Abs),
                waits=[("dve", c_ga)] + ([prevs["st4"]] if "st4" in prevs
                                         else []), inc="act")
            c_ex = p.emit("scalar", lambda a: a.activation(
                out=r2[:4, :], in_=r2[:4, :], func=AF.Exp, scale=-1.0), inc="act")
            c_ln = p.emit("scalar", lambda a: a.activation(
                out=r2[:4, :], in_=r2[:4, :], func=AF.Ln, bias=one4[0:4, 0:1]),
                inc="act")
            p.emit("vector", lambda v: v.tensor_scalar(
                r3[:4, :], gr[:4, :], 0.0, None, ALU.max),
                waits=[("act", c_ln)] + ([prevs["sb1b"], prevs["st2"]]
                                         if "sb1b" in prevs else []))
            p.emit("vector", lambda v: v.tensor_tensor(
                r3[:4, :], r3[:4, :], r2[:4, :], ALU.add))
            c_g = p.emit("vector", lambda v: v.tensor_scalar(
                gr[:4, :], r3[:4, :], nalog[0:4, 0:1], None, ALU.mult),
                waits=[("dve", c_na)], inc="dve")
            p.emit("vector", lambda v: v.tensor_scalar(
                r2[:4, 0:64], gr[:4, 0:64], 0.0, None, ALU.mult),
                waits=[("dve", c_g)])
            for ch in range(16):
                p.emit("vector", lambda v, ch=ch: v.tensor_tensor_scan(
                    gc[:4, ch * 64:(ch + 1) * 64], gr[:4, ch * 64:(ch + 1) * 64],
                    r2[:4, 0:64], 0.0, ALU.add, ALU.add),
                    waits=[prevs["sb0"]] if (ch == 0 and "sb0" in prevs) else [])
            c_gc = p.emit("vector", lambda v: v.tensor_copy(gc[:4, 0:1], gc[:4, 0:1]),
                          inc="dve")
            c_k1 = p.emit("scalar", lambda a: a.activation(
                out=lnk[:4, :], in_=s2k4[:4, :], func=AF.Ln, bias=e6c[0:4, 0:1]),
                waits=[("dve", p1_done)], inc="act")
            c_q1 = p.emit("scalar", lambda a: a.activation(
                out=lnq[:4, :], in_=s2q4[:4, :], func=AF.Ln, bias=e6c[0:4, 0:1]),
                inc="act")
            p.emit("vector", lambda v: v.tensor_scalar(
                lnk[:4, :], lnk[:4, :], -0.5, None, ALU.mult),
                waits=[("act", c_q1)])
            c_lk = p.emit("vector", lambda v: v.tensor_scalar(
                lnq[:4, :], lnq[:4, :], -0.5, None, ALU.mult), inc="dve")
            if s == 0:
                p.emit("gpsimd", lambda g: g.memset(l2c[:4, 0:1], 2.0), gpser=True)
                c_l2a = p.emit("scalar", lambda a: a.activation(
                    out=l2c[:4, 1:2], in_=l2c[:4, 0:1], func=AF.Ln),
                    waits=[("gp", p.v("gp"))], inc="act")
                c_l2b = p.emit("vector", lambda v: v.tensor_scalar(
                    l2c[:4, 1:2], l2c[:4, 1:2], -3.5, None, ALU.mult),
                    waits=[("act", c_l2a)], inc="dve")
            p.emit("vector", lambda v: v.tensor_tensor(
                r2[:4, :], gc[:4, :], lnk[:4, :], ALU.add),
                waits=[("dve", c_lk), ("dve", c_gc), ("act", c_lb)])
            c_p1 = p.emit("vector", lambda v: v.tensor_tensor(
                r2[:4, :], r2[:4, :], r1[:4, :], ALU.add), inc="dve")
            c_p2 = p.emit("vector", lambda v: v.scalar_tensor_tensor(
                r3[:4, :], gc[:4, :], -1.0, lnk[:4, :], ALU.mult, ALU.add),
                inc="dve")
            p.emit("vector", lambda v: v.tensor_tensor(
                r4[:4, :], gc[:4, :], lnq[:4, :], ALU.add),
                waits=[prevs["st5"], prevs["sb2"]] if "st5" in prevs else [])
            c_p3 = p.emit("vector", lambda v: v.tensor_scalar(
                r4[:4, :], r4[:4, :], l2c[0:4, 1:2], None, ALU.add),
                inc="dve")
            for ch in range(16):
                p.emit("vector", lambda v, ch=ch: v.tensor_scalar(
                    r1[:4, ch * 64:(ch + 1) * 64], gc[:4, ch * 64:(ch + 1) * 64],
                    gc[0:4, ch * 64 + 63:ch * 64 + 64], -1.0,
                    ALU.subtract, ALU.mult),
                    waits=[("dve", c_p1)] if ch == 0 else [])
            c_kd = p.emit("vector", lambda v: v.tensor_tensor(
                r1[:4, :], r1[:4, :], lnk[:4, :], ALU.add), inc="dve")
            sb2 = p.dma("scalar", lambda sq: sq.dma_start(
                out=rowd[2:3, :].rearrange("a (p t) -> (a p) t", p=4),
                in_=r4[:4, :]), "rb4", 1,
                waits=[("dve", c_p3)] + ([prevs["rl_p3"]] if "rl_p3" in prevs
                                         else []))
            sb1b = p.dma("scalar", lambda sq: sq.dma_start(
                out=rowd[1:2, :].rearrange("a (p t) -> (a p) t", p=4),
                in_=r3[:4, :]), "rb5", 1,
                waits=[("dve", c_p2)] + ([prevs["rl_p2"]] if "rl_p2" in prevs
                                         else []))
            sb0 = p.dma("scalar", lambda sq: sq.dma_start(
                out=rowd[0:1, :].rearrange("a (p t) -> (a p) t", p=4),
                in_=gc[:4, :]), "rb6", 1,
                waits=[("dve", c_gc), c_b2])
            rl0 = p.dma("scalar", lambda sq: sq.dma_start(
                out=b1r[:1, :], in_=rowd[0:1, :]), "rb7", 1,
                waits=[sb0] + ([("pe", prev_k1)] if prev_k1 is not None else []))
            st1 = p.dma("sync", lambda sy: sy.dma_start(
                out=pkgd[0:4, :], in_=r2[:4, :]), "pk", 8, waits=[("dve", c_p1)])
            st2 = p.dma("sync", lambda sy: sy.dma_start(
                out=pkgd[4:8, :], in_=r3[:4, :]), "pk", 8, waits=[("dve", c_p2)])
            st3 = p.dma("sync", lambda sy: sy.dma_start(
                out=pkgd[8:12, :], in_=r4[:4, :]), "pk", 8, waits=[("dve", c_p3)])
            c_e1 = p.emit("scalar", lambda a: a.activation(
                out=r2[:4, :], in_=r2[:4, :], func=AF.Exp), waits=[st1], inc="act")
            st4 = p.dma("sync", lambda sy: sy.dma_start(
                out=pkgd[12:16, :], in_=r2[:4, :]), "pk", 8, waits=[("act", c_e1)])
            c_e2 = p.emit("scalar", lambda a: a.activation(
                out=r4[:4, :], in_=r4[:4, :], func=AF.Exp), waits=[st3, sb2],
                inc="act")
            st5 = p.dma("sync", lambda sy: sy.dma_start(
                out=pkgd[16:20, :], in_=r4[:4, :]), "pk", 8, waits=[("act", c_e2)])
            c_e3 = p.emit("scalar", lambda a: a.activation(
                out=r1[:4, :], in_=r1[:4, :], func=AF.Exp),
                waits=[("dve", c_kd)], inc="act")
            st6 = p.dma("sync", lambda sy: sy.dma_start(
                out=pkgd[20:24, :], in_=r1[:4, :]), "pk", 8, waits=[("act", c_e3)])
            st7 = p.dma("sync", lambda sy: sy.dma_start(
                out=pkgd[24:28, :], in_=br[:4, :]), "pk", 8, waits=[("act", c_be)])
            for v_i in range(4):
                rhs = bass.AP(b1r, v_i * TS + 63, [[4 * TS, 1], [64, 16]])
                c_eg = p.emit("tensor", lambda t, rhs=rhs, v_i=v_i: t.matmul(
                    ps7[:, 16 * v_i:16 * v_i + 16], oner[:1, :], rhs,
                    start=True, stop=True),
                    waits=([rl0, ("dve", p.v("dve"))] +
                           ([("act", prevs["ege"])] if "ege" in prevs else []))
                    if v_i == 0 else [],
                    inc="pe")
            c_ege = p.emit("scalar", lambda a: a.activation(
                out=egls[:, :], in_=ps7[:, 0:64], func=AF.Exp),
                waits=[("pe", c_eg)], inc="act")
            prevs["ege"] = c_ege
            pk_ld = p.dma("sync", lambda sy: sy.dma_start(
                out=pkg28[:, :], in_=pkgd[:, :]), "pk2", 1,
                waits=[st1, st2, st3, st4, st5, st6, st7] +
                ([("dve", prev_pkgT)] if prev_pkgT is not None else []))
            prevs.update(st2=st2, st4=st4, st5=st5, st6=st6, st7=st7,
                         sb0=sb0, sb1b=sb1b, sb2=sb2, rl0=rl0,
                         q1=c_q1, be=c_be, b1=c_b1, rows_dve=p.v("dve"))

            for g in range(NG):
                gb = g * 128
                if PH < 31:
                    continue
                w_t = [pk_ld, id_ld]
                if prev_pkgT is not None:
                    w_t.append(("dve", prev_pkgT))
                if grp_ps4_free is not None:
                    w_t.append(("dve", grp_ps4_free))
                c_pt = p.emit("tensor", lambda t, gb=gb: t.transpose(
                    ps4[:, 0:28], pkg28[:, gb:gb + 128], idnf[0:28, 0:28]),
                    waits=w_t + [("dve", c_idf)], inc="pe")
                c_ct = p.emit("vector", lambda v: v.tensor_copy(
                    colsb[:, :], ps4[:, 0:28]), waits=[("pe", c_pt)], inc="dve")
                prev_pkgT = c_ct
                if PH < 32:
                    continue

                items = []
                for (name, src, n_h, dstt) in [("q", qs, 2, qtm), ("k", ks, 2, ktm),
                                               ("v", vs, 4, vtm), ("z", zs, 4, ztm)]:
                    for hd in range(n_h):
                        items.append((src, hd, dstt))
                for b0 in (0, 8):
                    batch = items[b0:b0 + 8]
                    for bi, (src, hd, dstt) in enumerate(batch):
                        slot = bi
                        w_tr = []
                        if bi == 0:
                            w_tr.append(("dve", max(
                                [v for v in trev.values() if v is not None]
                                + [p.v("dve") if b0 else 0])))
                            if tr_i[0] == 0:
                                w_tr.append(id_ld)
                        tr_i[0] += 1
                        c_tr = p.emit("tensor", lambda t, src=src, hd=hd, gb=gb,
                                      slot=slot: t.transpose(
                                          ps6[:, slot * 128:slot * 128 + 128],
                                          src[:, hd * TS + gb:hd * TS + gb + 128],
                                          idnb[:, :]), waits=w_tr, inc="pe")
                    pe_b = p.v("pe")
                    for bi, (src, hd, dstt) in enumerate(batch):
                        slot = bi
                        c_te = p.emit("vector", lambda v, dstt=dstt, hd=hd,
                                      slot=slot: v.tensor_copy(
                                          dstt[:, hd * 128:hd * 128 + 128],
                                          ps6[:, slot * 128:slot * 128 + 128]),
                                      waits=[("pe", pe_b)] if bi == 0 else [],
                                      inc="dve")
                        trev[slot] = c_te
                tr_done = p.v("dve")
                if PH < 33:
                    continue

                w_g = [("dve", grp_ps01_free)] if grp_ps01_free else []
                for v_i in range(4):
                    kh = v_i // 2
                    p.emit("tensor", lambda t, v_i=v_i, kh=kh, gb=gb: t.matmul(
                        ps0[:, v_i * 128:v_i * 128 + 128],
                        ks[:, kh * TS + gb:kh * TS + gb + 128],
                        ks[:, kh * TS + gb:kh * TS + gb + 128],
                        start=True, stop=True), waits=w_g if v_i == 0 else [],
                        inc="pe")
                    p.emit("tensor", lambda t, v_i=v_i, kh=kh, gb=gb: t.matmul(
                        ps1[:, v_i * 128:v_i * 128 + 128],
                        ks[:, kh * TS + gb:kh * TS + gb + 128],
                        qs[:, kh * TS + gb:kh * TS + gb + 128],
                        start=True, stop=True), inc="pe")
                gram_pe = p.v("pe")
                if PH < 40:
                    continue

                w_rlA = [sb1b, ("act", c_ege)]
                if prev_k1 is not None:
                    w_rlA.append(("pe", prev_k1))
                rl_p2 = p.dma("scalar", lambda sq: sq.dma_start(
                    out=b1r[:1, :], in_=rowd[1:2, :]), "rbp", 2, waits=w_rlA)
                prevs["rl_p2"] = rl_p2
                w_k1 = [rl_p2, ("dve", c_ct)]
                if grp_ps4_free is not None:
                    w_k1.append(("dve", grp_ps4_free))
                rhsA = bass.AP(b1r, gb, [[4 * TS, 1], [TS, 4], [1, 128]])
                c_k1a = p.emit("tensor", lambda t, rhsA=rhsA: t.matmul(
                    ps4[:, :], oner[:1, :], rhsA, start=True, stop=True),
                    waits=w_k1, inc="pe")
                a_done = [None] * 4
                for v_i in range(4):
                    c_e = p.emit("vector", lambda v, v_i=v_i: v.tensor_scalar(
                        cs[:, 128:256], ps4[:, v_i * 128:v_i * 128 + 128],
                        colsb[:, v_i:v_i + 1], 20.0, ALU.add, ALU.min),
                        waits=[("pe", c_k1a)], inc="dve")
                    c_da = p.emit("scalar", lambda a: a.activation(
                        out=cs[:, 128:256], in_=cs[:, 128:256], func=AF.Exp),
                        waits=[("dve", c_e)], inc="act")
                    p.emit("vector", lambda v, v_i=v_i: v.tensor_tensor(
                        cs[:, 0:128], ps0[:, v_i * 128:v_i * 128 + 128],
                        cs[:, 128:256], ALU.mult),
                        waits=[("act", c_da), ("pe", gram_pe)] +
                        ([mk_ld] if (s, g, v_i) == (0, 0, 0) else []))
                    c_A = p.emit("vector", lambda v, v_i=v_i: v.tensor_tensor(
                        asb[:, v_i * 128:v_i * 128 + 128], cs[:, 0:128],
                        msl[:, :], ALU.mult), inc="dve")
                    c_at = p.emit("tensor", lambda t, v_i=v_i: t.transpose(
                        ps6[:, v_i * 128:v_i * 128 + 128],
                        asb[:, v_i * 128:v_i * 128 + 128], idnb[:, :]),
                        waits=[("dve", c_A), ("dve", trev_all(trev))], inc="pe")
                    a_done[v_i] = p.emit("vector", lambda v, v_i=v_i: v.tensor_copy(
                        atsb[:, v_i * 128:v_i * 128 + 128],
                        ps6[:, v_i * 128:v_i * 128 + 128]),
                        waits=[("pe", c_at)], inc="dve")
                    trev[v_i] = a_done[v_i]
                rl_p3 = p.dma("scalar", lambda sq: sq.dma_start(
                    out=b1r[:1, :], in_=rowd[2:3, :]), "rbp", 2,
                    waits=[sb2, ("pe", c_k1a)])
                prevs["rl_p3"] = rl_p3
                rhsQ = bass.AP(b1r, gb, [[4 * TS, 1], [TS, 4], [1, 128]])
                c_k1b = p.emit("tensor", lambda t, rhsQ=rhsQ: t.matmul(
                    ps4[:, :], oner[:1, :], rhsQ, start=True, stop=True),
                    waits=[rl_p3, ("dve", p.v("dve"))], inc="pe")
                prev_k1 = c_k1b
                ant_done = [None] * 4
                for v_i in range(4):
                    c_e = p.emit("vector", lambda v, v_i=v_i: v.tensor_scalar(
                        cs[:, 128:256], ps4[:, v_i * 128:v_i * 128 + 128],
                        colsb[:, 4 + v_i:5 + v_i], 20.0, ALU.add, ALU.min),
                        waits=[("pe", c_k1b)], inc="dve")
                    c_da = p.emit("scalar", lambda a: a.activation(
                        out=cs[:, 128:256], in_=cs[:, 128:256], func=AF.Exp),
                        waits=[("dve", c_e)], inc="act")
                    p.emit("vector", lambda v, v_i=v_i: v.tensor_tensor(
                        cs[:, 0:128], ps1[:, v_i * 128:v_i * 128 + 128],
                        cs[:, 128:256], ALU.mult),
                        waits=[("act", c_da)] +
                        ([mk2_ld] if (s, g, v_i) == (0, 0, 0) else []))
                    ant_done[v_i] = p.emit("vector", lambda v, v_i=v_i:
                                           v.tensor_tensor(
                                               antb[:, v_i * 128:v_i * 128 + 128],
                                               cs[:, 0:128], mui[:, :], ALU.mult),
                                           inc="dve")
                dec_done = p.v("dve")
                if PH < 50:
                    continue

                for v_i in range(4):
                    p.emit("vector", lambda v, v_i=v_i: v.tensor_copy(
                        ttf[:, v_i * 128:v_i * 128 + 128],
                        atsb[:, v_i * 128:v_i * 128 + 128]),
                        waits=[("dve", dec_done)] if v_i == 0 else [])
                    p.emit("vector", lambda v, v_i=v_i: v.tensor_tensor(
                        ttf[:, v_i * 128:v_i * 128 + 128],
                        ttf[:, v_i * 128:v_i * 128 + 128], idnf[:, :], ALU.add))
                c_tb0 = p.emit("vector", lambda v: v.tensor_copy(
                    ttsb[:, :], ttf[:, :]), inc="dve")
                for it in range(5):
                    for v_i in range(4):
                        w_sq = []
                        if v_i == 0:
                            w_sq.append(("dve", p.v("dve")))
                            if grp_ps3_free is not None and it == 0:
                                w_sq.append(("dve", grp_ps3_free))
                        p.emit("tensor", lambda t, v_i=v_i: t.matmul(
                            ps1[:, v_i * 128:v_i * 128 + 128],
                            atsb[:, v_i * 128:v_i * 128 + 128],
                            asb[:, v_i * 128:v_i * 128 + 128],
                            start=True, stop=True), waits=w_sq, inc="pe")
                        p.emit("tensor", lambda t, v_i=v_i: t.matmul(
                            ps2[:, v_i * 128:v_i * 128 + 128],
                            asb[:, v_i * 128:v_i * 128 + 128],
                            atsb[:, v_i * 128:v_i * 128 + 128],
                            start=True, stop=True), inc="pe")
                        p.emit("tensor", lambda t, v_i=v_i: t.matmul(
                            ps3[:, v_i * 128:v_i * 128 + 128],
                            asb[:, v_i * 128:v_i * 128 + 128],
                            ttsb[:, v_i * 128:v_i * 128 + 128],
                            start=True, stop=True), inc="pe")
                    sq_pe = p.v("pe")
                    p.emit("vector", lambda v: v.tensor_tensor(
                        ttf[:, :], ttf[:, :], ps3[:, :], ALU.add),
                        waits=[("pe", sq_pe)])
                    c_b2e = p.emit("vector", lambda v: v.tensor_copy(
                        asb[:, :], ps1[:, :]), inc="dve")
                    c_bt2e = p.emit("vector", lambda v: v.tensor_copy(
                        atsb[:, :], ps2[:, :]), inc="dve")
                    p.emit("vector", lambda v: v.tensor_copy(
                        ttsb[:, :], ttf[:, :]), inc="dve")
                c_ttf = p.v("dve")
                grp_ps3_free = c_ttf
                if PH < 60:
                    continue

                for v_i in range(4):
                    kh = v_i // 2
                    p.emit("vector", lambda v, v_i=v_i: v.tensor_scalar(
                        xsb[:, 0:128], vtm[:, v_i * 128:v_i * 128 + 128],
                        colsb[:, 24 + v_i:25 + v_i], None, ALU.mult),
                        waits=[("dve", tr_done), ("dve", c_ttf)])
                    c_x = p.emit("vector", lambda v, v_i=v_i, kh=kh: v.tensor_scalar(
                        xsb[:, 128:256], ktm[:, kh * 128:kh * 128 + 128],
                        colsb[:, 12 + v_i:13 + v_i], None, ALU.mult), inc="dve")
                    uo = (v_i % 2) * 256
                    c_uw = p.emit("tensor", lambda t, v_i=v_i, uo=uo: t.matmul(
                        ps4[:, uo:uo + 256], ttsb[:, v_i * 128:v_i * 128 + 128],
                        xsb[:, :], start=True, stop=True),
                        waits=[("dve", c_x)], inc="pe")
                    c_u = p.emit("vector", lambda v, v_i=v_i, uo=uo: v.tensor_copy(
                        usb[:, v_i * 128:v_i * 128 + 128], ps4[:, uo:uo + 128]),
                        waits=[("pe", c_uw)], inc="dve")
                    c_w = p.emit("vector", lambda v, uo=uo: v.tensor_copy(
                        xsb[:, 0:128], ps4[:, uo + 128:uo + 256]), inc="dve")
                    c_wt = p.emit("tensor", lambda t, v_i=v_i: t.transpose(
                        ps6[:, v_i * 128:v_i * 128 + 128], xsb[:, 0:128], idnb[:, :]),
                        waits=[("dve", c_w), ("dve", trev_all(trev))], inc="pe")
                    c_wf = p.emit("vector", lambda v, v_i=v_i: v.tensor_copy(
                        wfm[:, v_i * 128:v_i * 128 + 128],
                        ps6[:, v_i * 128:v_i * 128 + 128]),
                        waits=[("pe", c_wt)], inc="dve")
                    trev[v_i] = c_wf
                    c_kdt = p.emit("vector", lambda v, v_i=v_i, kh=kh: v.tensor_scalar(
                        kdt[:, :], ktm[:, kh * 128:kh * 128 + 128],
                        colsb[:, 20 + v_i:21 + v_i], None, ALU.mult), inc="dve")
                    for c2 in range(2):
                        chg = g * 2 + c2
                        t0c = c2 * 64
                        c_cp1 = p.emit("vector", lambda v, t0c=t0c: v.tensor_copy(
                            kdc[0:64, :], kdt[t0c:t0c + 64, :]),
                            waits=[("dve", c_kdt)])
                        c_cp2 = p.emit("vector", lambda v, v_i=v_i, t0c=t0c:
                                       v.tensor_copy(
                                           antc[0:64, :],
                                           antb[t0c:t0c + 64,
                                                v_i * 128 + t0c:v_i * 128 + t0c + 64]),
                                       waits=[("dve", ant_done[v_i])])
                        c_cp3 = p.emit("vector", lambda v, v_i=v_i, t0c=t0c:
                                       v.tensor_copy(
                                           qec[0:64, :],
                                           colsb[t0c:t0c + 64, 16 + v_i:17 + v_i]),
                                       inc="dve")
                        c_m1 = p.emit("tensor", lambda t, v_i=v_i, t0c=t0c: t.matmul(
                            ps5[0:64, 0:128],
                            wfm[:, v_i * 128 + t0c:v_i * 128 + t0c + 64],
                            Sb[:, v_i * 128:v_i * 128 + 128],
                            start=True, stop=True),
                            waits=[("dve", c_wf), ("dve", p.v("dve"))], inc="pe")
                        c_vn = p.emit("vector", lambda v, v_i=v_i, t0c=t0c:
                                      v.tensor_tensor(
                                          mr[0:64, 3615:3743],
                                          usb[t0c:t0c + 64,
                                              v_i * 128:v_i * 128 + 128],
                                          ps5[0:64, 0:128], ALU.subtract),
                                      waits=[("pe", c_m1)], inc="dve")
                        c_vb = p.emit("vector", lambda v: v.tensor_copy(
                            vnb[0:64, :], mr[0:64, 3615:3743]), inc="dve")
                        c_m2 = p.emit("tensor", lambda t: t.matmul(
                            ps5[0:64, 128:256], antc[0:64, :],
                            vnb[0:64, :], start=True, stop=True),
                            waits=[("dve", c_vb)], inc="pe")
                        c_av = p.emit("vector", lambda v: v.tensor_copy(
                            mr[0:64, 3743:3871], ps5[0:64, 128:256]),
                            waits=[("pe", c_m2)], inc="dve")
                        c_m3 = p.emit("tensor", lambda t, v_i=v_i, kh=kh, gb=gb,
                                      t0c=t0c: t.matmul(
                                          ps5[0:64, 256:384],
                                          qs[:, kh * TS + gb + t0c:
                                             kh * TS + gb + t0c + 64],
                                          Sb[:, v_i * 128:v_i * 128 + 128],
                                          start=True, stop=True),
                                      waits=[("dve", c_av)], inc="pe")
                        c_oc = p.emit("vector", lambda v:
                                      v.scalar_tensor_tensor(
                                          mr[0:64, 3871:3999], ps5[0:64, 256:384],
                                          qec[0:64, 0:1],
                                          mr[0:64, 3743:3871], ALU.mult, ALU.add),
                                      waits=[("pe", c_m3), ("dve", c_av)],
                                      inc="dve")
                        c_m4 = p.emit("tensor", lambda t: t.matmul(
                            ps5[:, 384:512], kdc[0:64, :], vnb[0:64, :],
                            start=True, stop=True),
                            waits=[("dve", c_oc)], inc="pe")
                        c_su = p.emit("vector", lambda v, v_i=v_i, chg=chg:
                                      v.scalar_tensor_tensor(
                                          Sf[:, v_i * 128:v_i * 128 + 128],
                                          Sf[:, v_i * 128:v_i * 128 + 128],
                                          egls[:, v_i * 16 + chg:v_i * 16 + chg + 1],
                                          ps5[:, 384:512], ALU.mult, ALU.add),
                                      waits=[("pe", c_m4)], inc="dve")
                        c_sb = p.emit("vector", lambda v, v_i=v_i: v.tensor_copy(
                            Sb[:, v_i * 128:v_i * 128 + 128],
                            Sf[:, v_i * 128:v_i * 128 + 128]),
                            waits=[("pe", c_m3)], inc="dve")
                        if PH < 70:
                            continue
                        c_ss = p.emit("scalar", lambda a: a.activation(
                            out=cs[0:64, 0:128], in_=mr[0:64, 3871:3999], func=AF.Square,
                            accum_out=sscol[0:64, 0:1]),
                            waits=[("dve", c_oc)], inc="act")
                        c_r1 = p.emit("scalar", lambda a: a.activation(
                            out=sscol[0:64, 1:2], in_=sscol[0:64, 0:1], func=AF.Ln,
                            scale=1.0 / 128.0, bias=e6c[0:64, 0:1]), inc="act")
                        c_r2 = p.emit("scalar", lambda a: a.activation(
                            out=sscol[0:64, 1:2], in_=sscol[0:64, 1:2], func=AF.Exp,
                            scale=-0.5), inc="act")
                        p.emit("vector", lambda v: v.tensor_scalar(
                            mr[0:64, 3871:3999], mr[0:64, 3871:3999], sscol[0:64, 1:2], None,
                            ALU.mult), waits=[("act", c_r2)])
                        p.emit("vector", lambda v: v.tensor_tensor(
                            mr[0:64, 3871:3999], mr[0:64, 3871:3999], nwbc[0:64, :], ALU.mult),
                            waits=[nw_ld] if (s, g, v_i, c2) == (0, 0, 0, 0) else [])
                        w_gt = []
                        if (v_i, c2) == (0, 0):
                            prevk = (s * 8 + g - 1)
                            if prevk >= 0:
                                w_gt.append(st_gout[prevk % 16])
                        p.emit("vector", lambda v, v_i=v_i, t0c=t0c: v.tensor_copy(
                            vnb[0:64, :], ztm[t0c:t0c + 64,
                                              v_i * 128:v_i * 128 + 128]),
                            waits=w_gt)
                        c_gt = p.emit("vector", lambda v, v_i=v_i, t0c=t0c, g=g:
                                      v.tensor_tensor(
                                          goutb[t0c:t0c + 64,
                                                v_i * 128:v_i * 128 + 128],
                                          mr[0:64, 3871:3999], vnb[0:64, :],
                                          ALU.mult), inc="dve")
                grp_ps01_free = p.v("dve")
                grp_ps4_free = p.v("dve")
                oap = bass.AP(nc_gout_handle, (s * TS + g * 128) * 512,
                              [[512, 128], [1, 512]])
                st_gout[(s * 8 + g) % 16] = p.dma(
                    "sync", lambda sy, oap=oap: sy.dma_start(
                        out=oap, in_=goutb[:, 0:512]),
                    "go", 1, waits=[("dve", p.v("dve"))])
            seg_scan_done = p.v("dve")
        allw = [(n, p.c[n]) for n in list(p.h.keys())]
        p.emit("sync", lambda sy: None, waits=allw)
        with nc.Block() as block:
            p.replay(block)
    return nc


def trev_all(trev):
    vals = [v for v in trev.values() if v is not None]
    return max(vals) if vals else None



F32 = mybir.dt.float32
BF16 = mybir.dt.bfloat16
AF = mybir.ActivationFunctionType
ALU = mybir.AluOpType
AX = mybir.AxisListType

DM = 2048
TO = 1024


def build_launch2(Prog, dbg=False):
    nc = bass.Bass("TRN2", target_bir_lowering=False, debug=False)
    gT_h = nc.dram_tensor("gT", [4096, TO], BF16, kind="ExternalInput")
    hT_h = nc.dram_tensor("hT", [DM, TO], F32, kind="ExternalInput")
    Wout_h = nc.dram_tensor("Wout", [4096, DM], F32, kind="ExternalInput")
    w1T_h = nc.dram_tensor("w1T", [DM, 5632], F32, kind="ExternalInput")
    w2T_h = nc.dram_tensor("w2T", [DM, 5632], F32, kind="ExternalInput")
    w3T_h = nc.dram_tensor("w3T", [5632, DM], F32, kind="ExternalInput")
    Wkv_h = nc.dram_tensor("Wkv", [DM, DM], F32, kind="ExternalInput")
    Wq_h = nc.dram_tensor("Wq", [DM, DM], F32, kind="ExternalInput")
    sc_h = nc.dram_tensor("sc", [128, TO], F32, kind="ExternalInput")
    x2T_o = nc.dram_tensor("x2T", [DM, TO], F32, kind="ExternalOutput").ap()
    kroT_o = nc.dram_tensor("kroT", [1024, TO], F32, kind="ExternalOutput").ap()
    vT_o = nc.dram_tensor("vT", [1024, TO], F32, kind="ExternalOutput").ap()
    qroT_o = nc.dram_tensor("qroT", [DM, TO], F32, kind="ExternalOutput").ap()
    kind = "ExternalOutput" if dbg else "Internal"
    if dbg:
        hdbg = nc.dram_tensor("hdbg", [44 * 128, TO], BF16, kind="ExternalOutput").ap()
        thrd = nc.dram_tensor("thrd", [128, 8], F32, kind="ExternalOutput").ap()
    x1d = nc.dram_tensor("x1d", [DM, TO], F32, kind=kind).ap()
    x0d = nc.dram_tensor("x0d", [DM, TO], F32, kind=kind).ap()
    gT, hT = gT_h.ap(), hT_h.ap()
    w1T, w2T, w3T = w1T_h.ap(), w2T_h.ap(), w3T_h.ap()

    with ExitStack() as ctx:
        e = ctx.enter_context
        p = Prog(nc, ctx, ["din", "dw", "pe", "dve", "act", "gp", "st"])

        x1b = e(nc.sbuf_tensor("x1b", [128, 16 * TO], BF16))
        wstr = e(nc.sbuf_tensor("wstr", [128, 4 * 512], F32))
        tpos = e(nc.sbuf_tensor("tpos", [128, 512], F32))
        tneg = e(nc.sbuf_tensor("tneg", [128, 512], F32))
        rowm = e(nc.sbuf_tensor("rowm", [1, TO], F32))
        rowv = e(nc.sbuf_tensor("rowv", [1, TO], F32))
        rowr = e(nc.sbuf_tensor("rowr", [1, TO], F32))
        gcol = e(nc.sbuf_tensor("gcol", [128, 8], F32))
        thr = e(nc.sbuf_tensor("thr", [128, 8], F32))
        onec = e(nc.sbuf_tensor("onec", [128, 1], F32))
        onecb = e(nc.sbuf_tensor("onecb", [128, 1], BF16))
        epsc = e(nc.sbuf_tensor("epsc", [1, 1], F32))
        grow = e(nc.sbuf_tensor("grow", [1, 4 * 512], F32))
        oner = e(nc.sbuf_tensor("oner", [1, 128], F32))
        sqs = e(nc.sbuf_tensor("sqs", [128, 2 * 512], BF16))
        sig = e(nc.sbuf_tensor("sig", [128, 2 * 512], BF16))
        hidb = e(nc.sbuf_tensor("hidb", [128, 4 * 512], F32))
        x0buf = e(nc.sbuf_tensor("x0buf", [128, 4 * 512], F32))
        ps = [e(nc.psum_tensor(f"ps{i}", [128, 512], F32)) for i in range(4)]
        psr = [e(nc.psum_tensor(f"psr{i}", [1, 512], F32)) for i in range(4)]

        p.emit("gpsimd", lambda g: g.memset(onec[:, :], 1.0), gpser=True)
        p.emit("gpsimd", lambda g: g.memset(onecb[:, :], 1.0), gpser=True)
        p.emit("gpsimd", lambda g: g.memset(epsc[:1, :], 1e-5), gpser=True)
        p.emit("gpsimd", lambda g: g.memset(gcol[:, :], 0.0), gpser=True)
        p.emit("gpsimd", lambda g: g.memset(grow[:1, :], 0.0), gpser=True)
        p.emit("gpsimd", lambda g: g.memset(oner[:1, :], 1.0), gpser=True)
        p.emit("gpsimd", lambda g: g.memset(rowm[:1, 0:8], 0.0), gpser=True)
        init_gp = p.v("gp")

        plan_g = []
        for w_i, (dram, n_pt, n_ft) in enumerate([(w1T, 16, 11), (w2T, 16, 11),
                                                  (w3T, 44, 4)]):
            for pt in range(n_pt):
                for ft in range(n_ft):
                    plan_g.append((dram, pt, ft, w_i))
        red_done = [None] * len(plan_g)
        seen = set()
        for i, (dram, pt, ft, w_i) in enumerate(plan_g):
            sl = i % 4
            waits = [red_done[i - 4]] if i >= 4 else []
            dc = p.dma("scalar", lambda sq, dram=dram, pt=pt, ft=ft, sl=sl: sq.dma_start(
                out=wstr[:, sl * 512:(sl + 1) * 512],
                in_=dram[pt * 128:(pt + 1) * 128, ft * 512:(ft + 1) * 512]),
                "gw", 4, waits=waits)
            eng = "vector"
            semn = "dve" if eng == "vector" else "gp"
            col = w_i if eng == "vector" else 3 + w_i
            tcol = 6 if eng == "vector" else 7
            first = (eng, w_i) not in seen
            seen.add((eng, w_i))
            if eng == "vector":
                if first:
                    c = p.emit(eng, lambda ee, sl=sl, col=col: ee.tensor_reduce(
                        gcol[:, col:col + 1], wstr[:, sl * 512:(sl + 1) * 512],
                        AX.X, ALU.add, apply_absolute_value=True),
                        waits=[dc, ("gp", init_gp)], inc=semn)
                else:
                    p.emit(eng, lambda ee, sl=sl, tcol=tcol: ee.tensor_reduce(
                        gcol[:, tcol:tcol + 1], wstr[:, sl * 512:(sl + 1) * 512],
                        AX.X, ALU.add, apply_absolute_value=True), waits=[dc])
                    c = p.emit(eng, lambda ee, col=col, tcol=tcol: ee.tensor_tensor(
                        gcol[:, col:col + 1], gcol[:, col:col + 1],
                        gcol[:, tcol:tcol + 1], ALU.add), inc=semn)
            else:
                p.emit(eng, lambda ee, sl=sl: ee.tensor_reduce(
                    grow[0:1, 3 * 512:3 * 512 + 512], wstr[:, sl * 512:(sl + 1) * 512],
                    AX.C, ALU.add, apply_absolute_value=True), waits=[dc],
                    gpser=True)
                c = p.emit(eng, lambda ee, w_i=w_i: ee.tensor_tensor(
                    grow[0:1, w_i * 512:w_i * 512 + 512],
                    grow[0:1, w_i * 512:w_i * 512 + 512],
                    grow[0:1, 3 * 512:3 * 512 + 512], ALU.add), gpser=True)
            red_done[i] = (semn, c)
        gam_gp0 = p.v("gp")
        for w_i in range(3):
            p.emit("vector", lambda v, w_i=w_i: v.tensor_reduce(
                gcol[0:1, 3 + w_i:4 + w_i], grow[0:1, w_i * 512:w_i * 512 + 512],
                AX.X, ALU.add), waits=[("gp", gam_gp0)] if w_i == 0 else [],
                inc="dve")
        gam_dve, gam_gp = p.v("dve"), p.v("gp")
        c_pe = p.emit("tensor", lambda t: t.matmul(psr[0][:1, 0:8], onec[:, :],
                                                   gcol[:, 0:8], start=True, stop=True),
                      waits=[("dve", gam_dve), ("gp", gam_gp)], inc="pe")
        p.emit("vector", lambda v: v.tensor_copy(rowv[:1, 0:8], psr[0][:1, 0:8]),
               waits=[("pe", c_pe)])
        p.emit("vector", lambda v: v.tensor_tensor(rowm[:1, 0:3], rowv[:1, 0:3],
                                                   rowv[:1, 3:6], ALU.add))
        p.emit("vector", lambda v: v.tensor_scalar(rowm[:1, 0:3], rowm[:1, 0:3],
                                                   0.5 / (5632.0 * 2048.0), 0.5e-8,
                                                   ALU.mult, ALU.add))
        c_dve = p.emit("vector", lambda v: v.tensor_scalar(
            rowm[:1, 4:7], rowm[:1, 0:3], -1.0, None, ALU.mult), inc="dve")
        c_pe = p.emit("tensor", lambda t: t.matmul(ps[0][:, 0:8], oner[:1, :],
                                                   rowm[:1, 0:8], start=True, stop=True),
                      waits=[("dve", c_dve)], inc="pe")
        thr_done = p.emit("vector", lambda v: v.tensor_copy(thr[:, 0:8], ps[0][:, 0:8]),
                          waits=[("pe", c_pe)], inc="dve")

        stats_hist = []

        def stats_tile(i, ct, th, xsl, wait_ev, first_wait=None):
            if ct == 0 and th == 0:
                stats_hist.clear()
            w_sq = [("dve", wait_ev)]
            if i >= 2:
                w_sq.append(("pe", stats_hist[i - 2]))
            sq_c = p.emit("scalar", lambda a, xsl=xsl, i=i: a.activation(
                out=sqs[:, (i % 2) * 512:(i % 2) * 512 + 512],
                in_=x0buf[:, xsl:xsl + 512], func=AF.Square),
                waits=w_sq, inc="act")

            w_mm = [("act", sq_c)]
            if i <= 1 and first_wait is not None:
                w_mm.append(("dve", first_wait))
            p.emit("tensor", lambda t, th=th, ct=ct, xsl=xsl: t.matmul(
                psr[th][:1, :], onec[:, :], x0buf[:, xsl:xsl + 512],
                start=(ct == 0), stop=(ct == 15)), waits=w_mm, inc="pe")
            stats_hist.append(p.emit("tensor", lambda t, th=th, ct=ct, i=i: t.matmul(
                psr[2 + th][:1, :], onecb[:, :],
                sqs[:, (i % 2) * 512:(i % 2) * 512 + 512],
                start=(ct == 0), stop=(ct == 15)), inc="pe"))
            return p.dma("sync", lambda sy, ct=ct, th=th, xsl=xsl: sy.dma_start(
                out=x0d[ct * 128:(ct + 1) * 128, th * 512:th * 512 + 512],
                in_=x0buf[:, xsl:xsl + 512]),
                "so", 4, waits=[("dve", wait_ev)])

        def ln_rows(eps):
            stats_pe = p.v("pe")
            p.emit("vector", lambda v: v.tensor_scalar(rowm[:1, 0:512], psr[0][:1, :],
                                                       1.0 / DM, None, ALU.mult),
                   waits=[("pe", stats_pe)])
            p.emit("vector", lambda v: v.tensor_scalar(rowm[:1, 512:1024], psr[1][:1, :],
                                                       1.0 / DM, None, ALU.mult))
            p.emit("vector", lambda v: v.tensor_scalar(rowv[:1, 0:512], psr[2][:1, :],
                                                       1.0 / DM, None, ALU.mult))
            p.emit("vector", lambda v: v.tensor_scalar(rowv[:1, 512:1024], psr[3][:1, :],
                                                       1.0 / DM, None, ALU.mult))
            p.emit("vector", lambda v: v.tensor_tensor(grow[0:1, 0:TO], rowm[:1, :],
                                                       rowm[:1, :], ALU.mult))
            c_d = p.emit("vector", lambda v: v.tensor_tensor(rowv[:1, :], rowv[:1, :],
                                                             grow[0:1, 0:TO], ALU.subtract),
                         inc="dve")
            p.emit("scalar", lambda a: a.activation(
                out=rowr[:1, :], in_=rowv[:1, :], func=AF.Ln, bias=epsc[0:1, 0:1]),
                waits=[("dve", c_d)])
            return p.emit("scalar", lambda a: a.activation(
                out=rowr[:1, :], in_=rowr[:1, :], func=AF.Exp, scale=-0.5),
                inc="act"), c_d

        def ln_normalize(c_a, wait_st, dst_b16, out_dram):
            prev = [None] * 32
            last_dve = None
            for th in range(2):
                p.emit("tensor", lambda t, th=th: t.matmul(
                    ps[2][:, :], oner[:1, :], rowm[:1, th * 512:th * 512 + 512],
                    start=True, stop=True), waits=[("act", c_a)] +
                    ([("dve", last_dve)] if th == 1 else []), inc="pe")
                c_bc = p.emit("tensor", lambda t, th=th: t.matmul(
                    ps[3][:, :], oner[:1, :], rowr[:1, th * 512:th * 512 + 512],
                    start=True, stop=True), inc="pe")
                for ct in range(16):
                    i = th * 16 + ct
                    xsl = (i % 2) * 512
                    w_rl = list(wait_st) if i == 0 else []
                    if i >= 2:
                        w_rl += [("dve", prev[i - 2][0]), prev[i - 2][1]]
                    rl = p.dma("scalar", lambda sq, ct=ct, th=th, xsl=xsl: sq.dma_start(
                        out=x0buf[:, xsl:xsl + 512],
                        in_=x0d[ct * 128:(ct + 1) * 128, th * 512:th * 512 + 512]),
                        "rl", 2, waits=w_rl)
                    p.emit("vector", lambda v, xsl=xsl: v.tensor_tensor(
                        tpos[:, :], x0buf[:, xsl:xsl + 512], ps[2][:, :], ALU.subtract),
                        waits=[rl] + ([("pe", c_bc)] if ct == 0 else []))
                    p.emit("vector", lambda v, xsl=xsl: v.tensor_tensor(
                        x0buf[:, xsl:xsl + 512], tpos[:, :], ps[3][:, :], ALU.mult))
                    c_n = p.emit("vector", lambda v, ct=ct, th=th, xsl=xsl: v.tensor_copy(
                        dst_b16[:, ct * TO + th * 512:ct * TO + th * 512 + 512],
                        x0buf[:, xsl:xsl + 512]), inc="dve")
                    c_o = p.dma("sync", lambda sy, ct=ct, th=th, xsl=xsl: sy.dma_start(
                        out=out_dram[ct * 128:(ct + 1) * 128, th * 512:th * 512 + 512],
                        in_=x0buf[:, xsl:xsl + 512]),
                        "xo", 2, waits=[("dve", c_n)])
                    prev[i] = (c_n, c_o)
                last_dve = p.v("dve")
            return p.v("dve"), [prev[30][1], prev[31][1]]

        stA = ExitStack()
        gsb = stA.enter_context(nc.sbuf_tensor("gsb", [128, 32 * TO], BF16))
        wpan = stA.enter_context(nc.sbuf_tensor("wpan", [128, 2 * 4096], BF16))
        for kt in range(32):
            p.emit("gpsimd", lambda g, kt=kt: g.dma_start(
                out=gsb[:, kt * TO:(kt + 1) * TO], in_=gT[kt * 128:(kt + 1) * 128, :]),
                inc="din", inck=16)
        gsb_ld = p.v("din")

        evac_done = [None] * 16
        stA_st = [None] * 32
        for ct in range(16):
            po = (ct % 2) * 4096
            ap = bass.AP(Wout_h, ct * 128, [[DM, 128], [128 * DM, 32], [1, 128]])
            w_pan = [("pe", mmc)] if ct >= 2 else []
            pan_c = p.dma("gpsimd", lambda g, ap=ap, po=po: g.dma_start(
                out=wpan[:, po:po + 4096], in_=ap), "pa", 2, waits=w_pan)
            hid_c = [None, None]
            for th in range(2):
                hsl = ((ct * 2 + th) % 4) * 512
                w_h = [("dve", evac_done[ct - 2])] if ct >= 2 else []
                hid_c[th] = p.dma("sync", lambda sy, ct=ct, th=th, hsl=hsl: sy.dma_start(
                    out=hidb[:, hsl:hsl + 512],
                    in_=hT[ct * 128:(ct + 1) * 128, th * 512:th * 512 + 512]),
                    "ph", 4, waits=w_h)
            mm_waits = [pan_c, ("din", gsb_ld if ct == 0 else None),
                        ("dve", evac_done[ct - 1] if ct >= 1 else thr_done)]

            def mm_th(th):
                def f(t, th=th, po=po):
                    r = None
                    for kt in range(32):
                        r = t.matmul(ps[th][:, :],
                                     wpan[:, po + kt * 128:po + kt * 128 + 128],
                                     gsb[:, kt * TO + th * 512:kt * TO + th * 512 + 512],
                                     start=(kt == 0), stop=(kt == 31))
                    return r
                return f
            mmc_th = [p.emit("tensor", mm_th(0), waits=mm_waits, inc="pe"),
                      p.emit("tensor", mm_th(1), inc="pe")]
            mmc = mmc_th[1]
            for th in range(2):
                i = ct * 2 + th
                xsl = (i % 4) * 512
                w_ev = [("pe", mmc_th[th]), hid_c[th]]
                if i >= 4:
                    w_ev.append(stA_st[i - 4])
                c_ev = p.emit("vector", lambda v, th=th, xsl=xsl, i=i: v.tensor_tensor(
                    x0buf[:, xsl:xsl + 512], ps[th][:, :],
                    hidb[:, (i % 4) * 512:(i % 4) * 512 + 512], ALU.add),
                    waits=w_ev, inc="dve")
                stA_st[i] = stats_tile(i, ct, th, xsl, c_ev, first_wait=thr_done)
            evac_done[ct] = p.v("dve")
        stA.close()
        c_a1, ln1_cd = ln_rows(1e-5)
        ln1_dve, ln1_st = ln_normalize(c_a1, stA_st[28:32], x1b, x1d)

        stBH = ExitStack()
        hsb = stBH.enter_context(nc.sbuf_tensor("hsb", [128, 44 * TO], BF16))
        stB1 = ExitStack()
        tern = stB1.enter_context(nc.sbuf_tensor("tern", [128, 32 * 512], BF16))

        tern_c = {}
        jt_last_mm = None
        for jt in range(11):
            for u in range(32):
                kt, w_i = u // 2, u % 2
                gi = jt * 32 + u
                sl = gi % 4
                dram = w1T if w_i == 0 else w2T
                waits = []
                if gi == 0:
                    waits = [("dve", gam_dve), ("gp", gam_gp)] + ln1_st
                if gi >= 4:
                    waits.append(("gp", tern_c[gi - 4]))
                dc = p.dma("sync", lambda sy, dram=dram, kt=kt, jt=jt, sl=sl:
                           sy.dma_start(out=wstr[:, sl * 512:(sl + 1) * 512],
                                        in_=dram[kt * 128:(kt + 1) * 128,
                                                 jt * 512:(jt + 1) * 512]),
                           "gw", 4, waits=waits)
                gw = [dc]
                if gi == 0:
                    gw += [("dve", thr_done)] + ln1_st
                if jt > 0 and u == 0:
                    gw.append(("pe", jt_last_mm))
                ts_slot = u
                p.emit("gpsimd", lambda g, sl=sl, w_i=w_i: g.tensor_scalar(
                    tpos[:, :], wstr[:, sl * 512:(sl + 1) * 512], thr[:, w_i:w_i + 1],
                    None, ALU.is_gt), waits=gw, gpser=True)
                p.emit("gpsimd", lambda g, sl=sl, w_i=w_i: g.tensor_scalar(
                    tneg[:, :], wstr[:, sl * 512:(sl + 1) * 512],
                    thr[:, 4 + w_i:5 + w_i], None, ALU.is_lt), gpser=True)
                tern_c[gi] = p.emit("gpsimd", lambda g, ts_slot=ts_slot:
                                    g.tensor_tensor(
                                        tern[:, ts_slot * 512:(ts_slot + 1) * 512],
                                        tneg[:, :], tpos[:, :], ALU.subtract), gpser=True)
            tern_all = p.v("gp")
            for jj in range(4):
                w_mm = [("gp", tern_all), ("dve", p.v("dve") if (jt + jj) > 0 else None)]
                mmh_c = {}
                for w_i in range(2):
                    for th in range(2):
                        def mmh(t, w_i=w_i, th=th, jj=jj):
                            pp = ps[w_i * 2 + th]
                            r = None
                            for kt in range(16):
                                ts_slot = kt * 2 + w_i
                                r = t.matmul(pp[:, :],
                                             tern[:, ts_slot * 512 + jj * 128:
                                                  ts_slot * 512 + jj * 128 + 128],
                                             x1b[:, kt * TO + th * 512:
                                                 kt * TO + th * 512 + 512],
                                             start=(kt == 0), stop=(kt == 15))
                            return r
                        mmh_c[(w_i, th)] = p.emit(
                            "tensor", mmh,
                            waits=w_mm if (w_i, th) == (0, 0) else [], inc="pe")
                jt_last_mm = mmh_c[(1, 1)]
                sig_c = {}
                for th in range(2):
                    sig_c[th] = p.emit("scalar", lambda a, th=th: a.activation(
                        out=sig[:, th * 512:th * 512 + 512], in_=ps[th][:, :],
                        func=AF.Sigmoid, scale=-1.0),
                        waits=[("pe", mmh_c[(0, th)])], inc="act")
                jcol = jt * 4 + jj
                for th in range(2):
                    p.emit("vector", lambda v, jcol=jcol, th=th: v.scalar_tensor_tensor(
                        hsb[:, jcol * TO + th * 512:jcol * TO + th * 512 + 512],
                        ps[2 + th][:, :], -1.0, sig[:, th * 512:th * 512 + 512],
                        ALU.mult, ALU.mult),
                        waits=[("act", sig_c[th]), ("pe", mmh_c[(1, th)])],
                        inc="dve" if th == 1 else None)
        h_all_dve = p.v("dve")
        if dbg:
            for j in range(44):
                p.dma("sync", lambda sy, j=j: sy.dma_start(
                    out=hdbg[j * 128:(j + 1) * 128, :], in_=hsb[:, j * TO:(j + 1) * TO]),
                    "hdb", 2, waits=[("dve", h_all_dve)] if j == 0 else [])
            p.dma("sync", lambda sy: sy.dma_start(out=thrd[:, :], in_=thr[:, 0:8]),
                  "hdb", 2)
        stB1.close()

        stB2 = ExitStack()
        w3pan = stB2.enter_context(nc.sbuf_tensor("w3pan", [128, 44 * 128], F32))
        t3pan = stB2.enter_context(nc.sbuf_tensor("t3pan", [128, 44 * 128], BF16))
        ln2_mm = [None] * 16
        stB_st = [None] * 32
        evh = [None] * 32
        for ct in range(16):
            ap = bass.AP(w3T_h, ct * 128, [[DM, 128], [128 * DM, 44], [1, 128]])
            w_pan = [("dve", h_all_dve)] if ct == 0 else [("pe", ln2_mm[ct - 1])]
            pan_c = p.dma("scalar", lambda sq, ap=ap: sq.dma_start(
                out=w3pan[:, :], in_=ap), "w3", 1, waits=w_pan)
            for b in range(11):
                p.emit("gpsimd", lambda g, b=b: g.tensor_scalar(
                    tpos[:, :], w3pan[:, b * 512:b * 512 + 512],
                    thr[:, 2:3], None, ALU.is_gt),
                    waits=[pan_c] if b == 0 else [], gpser=True)
                p.emit("gpsimd", lambda g, b=b: g.tensor_scalar(
                    tneg[:, :], w3pan[:, b * 512:b * 512 + 512],
                    thr[:, 6:7], None, ALU.is_lt), gpser=True)
                p.emit("gpsimd", lambda g, b=b: g.tensor_tensor(
                    t3pan[:, b * 512:b * 512 + 512], tneg[:, :], tpos[:, :],
                    ALU.subtract), gpser=True)
            tern3 = p.v("gp")
            x1l = [None, None]
            for th in range(2):
                i = ct * 2 + th
                hsl = (i % 4) * 512
                w_h = [("dve", evh[i - 4])] if ct >= 2 else [("dve", h_all_dve)]
                x1l[th] = p.dma("sync", lambda sy, ct=ct, th=th, hsl=hsl: sy.dma_start(
                    out=hidb[:, hsl:hsl + 512],
                    in_=x1d[ct * 128:(ct + 1) * 128, th * 512:th * 512 + 512]),
                    "px", 4, waits=w_h)

            def mmy_th(th):
                def f(t, th=th):
                    r = None
                    for j in range(44):
                        r = t.matmul(ps[th][:, :], t3pan[:, j * 128:j * 128 + 128],
                                     hsb[:, j * TO + th * 512:j * TO + th * 512 + 512],
                                     start=(j == 0), stop=(j == 43))
                    return r
                return f
            mmy_c = [p.emit("tensor", mmy_th(0),
                            waits=[("gp", tern3),
                                   ("dve", evh[2 * ct - 1] if ct >= 1 else None)],
                            inc="pe"),
                     p.emit("tensor", mmy_th(1), inc="pe")]
            ln2_mm[ct] = mmy_c[1]
            for th in range(2):
                i = ct * 2 + th
                xsl = (i % 4) * 512
                w_ev = [("pe", mmy_c[th]), x1l[th]]
                if i >= 4:
                    w_ev.append(stB_st[i - 4])
                c_ev = p.emit("vector", lambda v, th=th, xsl=xsl, i=i: v.tensor_tensor(
                    x0buf[:, xsl:xsl + 512],
                    hidb[:, (i % 4) * 512:(i % 4) * 512 + 512],
                    ps[th][:, :], ALU.subtract), waits=w_ev, inc="dve")
                evh[i] = c_ev
                stB_st[i] = stats_tile(i, ct, th, xsl, c_ev, first_wait=ln1_cd)
        stB2.close()
        stBH.close()
        c_a2, ln2_cd = ln_rows(1e-5)
        x2b = e(nc.sbuf_tensor("x2b", [128, 16 * TO], BF16))
        x2_dve, x2_st = ln_normalize(c_a2, stB_st[28:32], x2b, x2T_o)

        stC = ExitStack()
        kvpan = stC.enter_context(nc.sbuf_tensor("kvpan", [128, 2 * 2048], BF16))
        kvout = stC.enter_context(nc.sbuf_tensor("kvout", [128, 2 * TO], F32))
        xqb = stC.enter_context(nc.sbuf_tensor("xqb", [128, 16 * TO], BF16))
        scs = stC.enter_context(nc.sbuf_tensor("scs", [128, TO], F32))
        scs_ld = p.dma("sync", lambda sy: sy.dma_start(out=scs[:, :],
                                                        in_=sc_h.ap()[:, :]), "scl", 1)

        cx_hist = []
        for i in range(32):
            ct, th = i // 2, i % 2
            sl = ct * TO + th * 512
            w0 = [("dve", x2_dve)] if i == 0 else []
            if i >= 2:
                w0.append(("pe", cx_hist[i - 2]))
            c_ab = p.emit("scalar", lambda a, sl=sl, i=i: a.activation(
                out=sqs[:, (i % 2) * 512:(i % 2) * 512 + 512], in_=x2b[:, sl:sl + 512],
                func=AF.Abs), waits=w0, inc="act")
            cx_hist.append(p.emit("tensor", lambda t, th=th, ct=ct, i=i: t.matmul(
                psr[th][:1, :], onecb[:, :], sqs[:, (i % 2) * 512:(i % 2) * 512 + 512],
                start=(ct == 0), stop=(ct == 15)),
                waits=[("act", c_ab)] + ([("dve", ln2_cd)] if i <= 1 else []), inc="pe"))
        cx_pe = p.v("pe")
        p.emit("scalar", lambda a: a.activation(
            out=rowm[:1, 0:512], in_=psr[0][:1, :], func=AF.Sigmoid),
            waits=[("pe", cx_pe)], inc="act")
        c_cx = p.emit("scalar", lambda a: a.activation(
            out=rowm[:1, 512:1024], in_=psr[1][:1, :], func=AF.Sigmoid), inc="act")
        last = None
        for th in range(2):
            c_bc = p.emit("tensor", lambda t, th=th: t.matmul(
                ps[2][:, :], oner[:1, :], rowm[:1, th * 512:th * 512 + 512],
                start=True, stop=True), waits=[("act", c_cx)] +
                ([("dve", last)] if th == 1 else []), inc="pe")
            for ct in range(16):
                sl = ct * TO + th * 512
                p.emit("vector", lambda v, sl=sl: v.tensor_tensor(
                    xqb[:, sl:sl + 512], x2b[:, sl:sl + 512], ps[2][:, :], ALU.mult),
                    waits=[("pe", c_bc)] if ct == 0 else [],
                    inc="dve" if ct == 15 else None)
            last = p.v("dve")
        xq_done = p.v("dve")

        def rope_block(pp, th, ob, w_first):
            p.emit("vector", lambda v: v.tensor_tensor(
                tpos[0:64, 0:512], pp[0:64, :],
                scs[64:128, th * 512:th * 512 + 512], ALU.mult), waits=w_first)
            p.emit("vector", lambda v: v.tensor_tensor(
                x0buf[0:64, 0:512], pp[64:128, :],
                scs[0:64, th * 512:th * 512 + 512], ALU.mult))
            p.emit("vector", lambda v: v.tensor_tensor(
                kvout[0:64, ob + th * 512:ob + th * 512 + 512],
                tpos[0:64, 0:512], x0buf[0:64, 0:512], ALU.subtract))
            p.emit("vector", lambda v: v.tensor_tensor(
                tpos[0:64, 0:512], pp[0:64, :],
                scs[0:64, th * 512:th * 512 + 512], ALU.mult))
            p.emit("vector", lambda v: v.tensor_tensor(
                x0buf[0:64, 0:512], pp[64:128, :],
                scs[64:128, th * 512:th * 512 + 512], ALU.mult))
            return p.emit("vector", lambda v: v.tensor_tensor(
                kvout[64:128, ob + th * 512:ob + th * 512 + 512],
                tpos[0:64, 0:512], x0buf[0:64, 0:512], ALU.add), inc="dve")

        def proj_pass(W_h, rhs, rope_cts, outs, wait0):
            prev = [None] * 16
            stc = [None] * 16
            for ct in range(16):
                po = (ct % 2) * 2048
                ap = bass.AP(W_h, ct * 128, [[DM, 128], [128 * DM, 16], [1, 128]])
                w_pan = [("dve", wait0)] if ct < 2 else [("dve", prev[ct - 2])]
                pan_c = p.dma("gpsimd", lambda g, ap=ap, po=po: g.dma_start(
                    out=kvpan[:, po:po + 2048], in_=ap), "kp", 2, waits=w_pan)

                def mmp_th(th):
                    def f(t, th=th, po=po, rhs=rhs):
                        r = None
                        for kt in range(16):
                            r = t.matmul(ps[th][:, :],
                                         kvpan[:, po + kt * 128:po + kt * 128 + 128],
                                         rhs[:, kt * TO + th * 512:
                                             kt * TO + th * 512 + 512],
                                         start=(kt == 0), stop=(kt == 15))
                        return r
                    return f
                mmc_t = [p.emit("tensor", mmp_th(0),
                                waits=[pan_c,
                                       ("dve", prev[ct - 1] if ct >= 1 else wait0)],
                                inc="pe"),
                         p.emit("tensor", mmp_th(1), inc="pe")]
                mmc = mmc_t[1]
                ob = (ct % 2) * TO
                for th in range(2):
                    pp = ps[th]
                    w_first = [("pe", mmc_t[th])]
                    if th == 0 and ct >= 2:
                        w_first.append(stc[ct - 2])
                    if ct in rope_cts:
                        c_last = rope_block(pp, th, ob, w_first)
                    else:
                        c_last = p.emit("vector", lambda v, pp=pp, th=th, ob=ob:
                                        v.tensor_copy(
                                            kvout[:, ob + th * 512:ob + th * 512 + 512],
                                            pp[:, :]), waits=w_first, inc="dve")
                prev[ct] = c_last
                oo, ro = outs(ct)
                stc[ct] = p.dma("sync", lambda sy, oo=oo, ro=ro, ob=ob: sy.dma_start(
                    out=oo[ro:ro + 128, :], in_=kvout[:, ob:ob + TO]),
                    "ko", 2, waits=[("dve", prev[ct])])
            return prev, stc

        p.emit("vector", lambda v: None, waits=[scs_ld] + x2_st)
        prev_kv, st_kv = proj_pass(
            Wkv_h, x2b, set(range(8)),
            lambda ct: (kroT_o, ct * 128) if ct < 8 else (vT_o, (ct - 8) * 128),
            x2_dve)
        p.emit("vector", lambda v: None, waits=[st_kv[14], st_kv[15]])
        prev_q, st_q = proj_pass(
            Wq_h, xqb, set(range(16)),
            lambda ct: (qroT_o, ct * 128),
            p.v("dve"))
        stC.close()

        allw = [(n, p.c[n]) for n in list(p.h.keys())
                if n not in ("pe", "dve", "act", "gp")]
        p.emit("sync", lambda sy: None, waits=allw)

        with nc.Block() as block:
            p.replay(block)
    return nc



F32 = mybir.dt.float32
BF16 = mybir.dt.bfloat16
AF = mybir.ActivationFunctionType
ALU = mybir.AluOpType
AX = mybir.AxisListType

DM = 2048
TO = 1024
SEQ = 8192
SCALE = float(1.0 / np.sqrt(np.float64(128.0)))
STRADDLE = {0: list(range(0, 7)) + [62, 63], 1: list(range(1, 11))}
WRAP = {(0, 62), (0, 63)}
SSLOT = {}
for _q in (0, 1):
    for _j in STRADDLE[_q]:
        SSLOT[(_q, _j)] = len(SSLOT)
NSL = len(SSLOT)


def build_launch3(Prog, dbg=False):
    nc = bass.Bass("TRN2", target_bir_lowering=False, debug=False)
    q_h = nc.dram_tensor("qroT", [DM, TO], F32, kind="ExternalInput")
    kr_h = nc.dram_tensor("kr", [1024, SEQ], F32, kind="ExternalInput")
    vr_h = nc.dram_tensor("vr", [SEQ, 1024], F32, kind="ExternalInput")
    x2_h = nc.dram_tensor("x2T", [DM, TO], F32, kind="ExternalInput")
    Wo_h = nc.dram_tensor("Wo", [DM, DM], F32, kind="ExternalInput")
    bo_h = nc.dram_tensor("bo2", [DM, 1], F32, kind="ExternalInput")
    ir_h = nc.dram_tensor("irow", [1, TO], F32, kind="ExternalInput")
    sj_h = nc.dram_tensor("SJ", [128, 128], F32, kind="ExternalInput")
    sg_h = nc.dram_tensor("SG", [1, 2 * SEQ], F32, kind="ExternalInput")
    pen_h = nc.dram_tensor("PEND3", [128, NSL], F32, kind="ExternalInput")
    u0_h = nc.dram_tensor("u0", [128, 1], F32, kind="ExternalInput")
    hr_h = nc.dram_tensor("hrow", [1, 16], F32, kind="ExternalInput")
    cst_h = nc.dram_tensor("cstc", [1, 4], F32, kind="ExternalInput")
    x3T_o = nc.dram_tensor("x3T", [DM, TO], F32, kind="ExternalOutput").ap()
    amx_o = nc.dram_tensor("amx", [1, 1], F32, kind="ExternalOutput").ap()
    x0d = nc.dram_tensor("x0d", [DM, TO], F32,
                         kind="ExternalOutput" if dbg else "Internal").ap()

    with ExitStack() as ctx:
        e = ctx.enter_context
        p = Prog(nc, ctx, [])
        qb = e(nc.sbuf_tensor("qb", [128, 4 * TO], BF16))
        kf = e(nc.sbuf_tensor("kf", [128, SEQ], BF16))
        vt = e(nc.sbuf_tensor("vt", [128, SEQ], BF16))
        outT = e(nc.sbuf_tensor("outT", [128, 16 * TO], BF16))
        d3sb = e(nc.sbuf_tensor("d3sb", [128, NSL * 512], F32))
        sgf = e(nc.sbuf_tensor("sgf", [1, SEQ], F32))
        sj = e(nc.sbuf_tensor("sj", [128, 128], F32))
        t0 = e(nc.sbuf_tensor("t0", [128, 512], F32))
        t1 = e(nc.sbuf_tensor("t1", [128, 512], F32))
        pT = e(nc.sbuf_tensor("pT", [128, 2 * 512], BF16))
        sqk = e(nc.sbuf_tensor("sqk", [128, 2 * 512], BF16))
        bcol = e(nc.sbuf_tensor("bcol", [128, 128], F32))
        ccol = e(nc.sbuf_tensor("ccol", [128, 12], F32))
        mxa = e(nc.sbuf_tensor("mxa", [128, 2], F32))
        rowm = e(nc.sbuf_tensor("rowm", [1, TO], F32))
        rowv = e(nc.sbuf_tensor("rowv", [1, TO], F32))
        rowr = e(nc.sbuf_tensor("rowr", [1, TO], F32))
        rowg = e(nc.sbuf_tensor("rowg", [1, TO], F32))
        srow = e(nc.sbuf_tensor("srow", [1, 64], F32))
        Rrow = e(nc.sbuf_tensor("Rrow", [1, TO], F32))
        irow = e(nc.sbuf_tensor("irowS", [1, TO], F32))
        hrow = e(nc.sbuf_tensor("hrowS", [1, 16], F32))
        slo = e(nc.sbuf_tensor("slo", [1, 16], F32))
        cstc = e(nc.sbuf_tensor("cstcS", [1, 4], F32))
        onec = e(nc.sbuf_tensor("onec", [128, 1], F32))
        onecb = e(nc.sbuf_tensor("onecb", [128, 1], BF16))
        oner = e(nc.sbuf_tensor("oner", [1, 128], F32))
        epsc = e(nc.sbuf_tensor("epsc", [1, 1], F32))
        o64 = e(nc.sbuf_tensor("o64", [128, 64], F32))
        idn = e(nc.sbuf_tensor("idn", [128, 128], F32))
        wpan = e(nc.sbuf_tensor("wpan", [128, 2 * 2048], BF16))
        x2b = e(nc.sbuf_tensor("x2rb", [128, 4 * 512], F32))
        x0buf = e(nc.sbuf_tensor("x0buf", [128, 4 * 512], F32))
        ps = [e(nc.psum_tensor(f"ps{i}", [128, 512], F32)) for i in range(4)]
        psr = [e(nc.psum_tensor(f"psr{i}", [1, 512], F32)) for i in range(4)]

        p.emit("gpsimd", lambda g: g.memset(onec[:, :], 1.0), gpser=True)
        p.emit("gpsimd", lambda g: g.memset(onecb[:, :], 1.0), gpser=True)
        p.emit("gpsimd", lambda g: g.memset(oner[:1, :], 1.0), gpser=True)
        p.emit("gpsimd", lambda g: g.memset(epsc[:1, :], 1e-5), gpser=True)
        p.emit("gpsimd", lambda g: g.memset(o64[:, :], 1.0), gpser=True)
        p.emit("gpsimd", lambda g: g.memset(mxa[:, 0:2], 0.0), gpser=True)
        p.emit("gpsimd", lambda g: g.iota(idn[:, :], [[-1, 128]], base=0,
                                          channel_multiplier=1,
                                          allow_small_or_imprecise_dtypes=True),
               gpser=True)
        p.emit("gpsimd", lambda g: g.iota(t1[:, :], [[-1, 512]], base=0,
                                          channel_multiplier=1,
                                          allow_small_or_imprecise_dtypes=True),
               gpser=True)
        init_gp = p.v("gp")
        c_idn = p.emit("vector", lambda v: v.tensor_scalar(
            idn[:, :], idn[:, :], 0.0, None, ALU.is_equal), waits=[("gp", init_gp)],
            inc="dve")

        sj_ld = p.dma("sync", lambda sy: sy.dma_start(out=sj[:, :],
                                                      in_=sj_h.ap()[:, :]), "ls0", 1)
        ir_ld = p.dma("sync", lambda sy: sy.dma_start(out=irow[:1, :],
                                                      in_=ir_h.ap()[:, :]), "ls2", 1)
        hr_ld = p.dma("sync", lambda sy: sy.dma_start(out=hrow[:1, :],
                                                      in_=hr_h.ap()[:, :]), "ls3", 1)
        cs_ld = p.dma("sync", lambda sy: sy.dma_start(out=cstc[:1, :],
                                                      in_=cst_h.ap()[:, :]), "ls4", 1)
        u0_ld = p.dma("sync", lambda sy: sy.dma_start(out=ccol[:, 3:4],
                                                      in_=u0_h.ap()[:, :]), "ls5", 1)
        pend3 = e(nc.sbuf_tensor("pend3", [128, NSL], F32))
        pen_ld = p.dma("sync", lambda sy: sy.dma_start(out=pend3[:, :],
                                                       in_=pen_h.ap()[:, :]), "ls6", 1)

        c_l2 = p.emit("scalar", lambda a: a.activation(
            out=cstc[:1, 1:2], in_=cstc[:1, 0:1], func=AF.Ln), waits=[cs_ld],
            inc="act")
        c_sl = p.emit("vector", lambda v: v.tensor_scalar(
            slo[:1, :], hrow[:1, :], cstc[0:1, 1:2], None, ALU.mult),
            waits=[("act", c_l2), hr_ld], inc="dve")
        c_slo = p.emit("scalar", lambda a: a.activation(
            out=slo[:1, :], in_=slo[:1, :], func=AF.Exp), waits=[("dve", c_sl)],
            inc="act")

        for (qc, jt), sl in SSLOT.items():
            off = float(128 * jt - 512 * qc)
            if (qc, jt) in WRAP:
                p.emit("vector", lambda v, sl=sl, off=off: v.tensor_scalar(
                    d3sb[:, sl * 512:sl * 512 + 512], t1[:, :], -1.0, 8192.0 - off,
                    ALU.mult, ALU.add), waits=[("gp", init_gp)] if sl == 0 else [])
            else:
                p.emit("vector", lambda v, sl=sl, off=off: v.tensor_scalar(
                    d3sb[:, sl * 512:sl * 512 + 512], t1[:, :], off, None,
                    ALU.add), waits=[("gp", init_gp)] if sl == 0 else [])
                c_ab = p.emit("scalar", lambda a, sl=sl: a.activation(
                    out=d3sb[:, sl * 512:sl * 512 + 512],
                    in_=d3sb[:, sl * 512:sl * 512 + 512], func=AF.Abs),
                    waits=[("dve", p.v("dve"))], inc="act")
                p.emit("vector", lambda v: None, waits=[("act", c_ab)])
            p.emit("vector", lambda v, sl=sl: v.tensor_scalar(
                t0[:, :], d3sb[:, sl * 512:sl * 512 + 512], 256.5, None, ALU.is_le))
            p.emit("vector", lambda v, sl=sl: v.scalar_tensor_tensor(
                d3sb[:, sl * 512:sl * 512 + 512], t0[:, :], 1e12,
                d3sb[:, sl * 512:sl * 512 + 512], ALU.mult, ALU.add))
        for sl in range(NSL):
            p.emit("vector", lambda v, sl=sl: v.tensor_scalar(
                d3sb[:, sl * 512:sl * 512 + 512], d3sb[:, sl * 512:sl * 512 + 512],
                pend3[:, sl:sl + 1], None, ALU.add),
                waits=[pen_ld] if sl == 0 else [])
        d3_done = p.emit("vector", lambda v: v.tensor_copy(t0[:, 0:1], t0[:, 0:1]),
                         inc="dve")

        psr0_free = p.v("dve")
        prev_exp = None
        prev_evac = None
        prev_sum = None
        grp_done = None
        mq_done = {}
        q_ld = {}
        kn_last = None
        mq_last = None
        vse_prev = None
        out_written = {}
        head_pe_done = [None] * 8

        for kvh in range(8):
            for qq in range(2):
                qh = 2 * kvh + qq
                slq = qh % 4
                w_q = []
                if qh >= 4:
                    w_q.append(("pe", head_pe_done[kvh - 2]))
                q_ld[qh] = p.dma("gpsimd", lambda g, qh=qh, slq=slq: g.dma_start(
                    out=qb[:, slq * TO:(slq + 1) * TO],
                    in_=q_h.ap()[qh * 128:(qh + 1) * 128, :]), "lq", 4, waits=w_q)
            w_kv = [("pe", head_pe_done[kvh - 1])] if kvh >= 1 else []
            kld = p.dma("gpsimd", lambda g, kvh=kvh: g.dma_start(
                out=kf[:, :], in_=kr_h.ap()[kvh * 128:(kvh + 1) * 128, :]),
                "lk", 1, waits=w_kv)
            vap = bass.AP(vr_h, kvh * 128, [[1024, 128], [128 * 1024, 64], [1, 128]])
            vld = p.dma("gpsimd", lambda g, vap=vap: g.dma_start(
                out=vt[:, :], in_=vap), "lv", 1, waits=w_kv)

            def rowmax(src_tiles, dst_cell, waits0):
                nonlocal psr0_free
                last = None
                for u, (ap_in, n) in enumerate(src_tiles):
                    sqc = p.emit("scalar", lambda a, ap_in=ap_in, u=u: a.activation(
                        out=sqk[:, (u % 2) * 512:(u % 2) * 512 + 512], in_=ap_in,
                        func=AF.Square),
                        waits=(waits0 if u == 0 else []) +
                        ([("pe", mm_h[u - 2])] if u >= 2 else []), inc="act")
                    if u == 0:
                        mm_h = [None] * len(src_tiles)
                    mm_h[u] = p.emit("tensor", lambda t, u=u: t.matmul(
                        psr[0][:1, :], onecb[:, :],
                        sqk[:, (u % 2) * 512:(u % 2) * 512 + 512],
                        start=True, stop=True),
                        waits=[("act", sqc), ("dve", psr0_free)], inc="pe")
                    p.emit("vector", lambda v: v.tensor_reduce(
                        srow[:1, 16:17], psr[0][:1, :], AX.X, ALU.max),
                        waits=[("pe", mm_h[u])])
                    if u == 0:
                        last = p.emit("vector", lambda v, dst_cell=dst_cell:
                                      v.tensor_copy(srow[:1, dst_cell:dst_cell + 1],
                                                    srow[:1, 16:17]), inc="dve")
                    else:
                        last = p.emit("vector", lambda v, dst_cell=dst_cell:
                                      v.tensor_tensor(
                                          srow[:1, dst_cell:dst_cell + 1],
                                          srow[:1, dst_cell:dst_cell + 1],
                                          srow[:1, 16:17], ALU.max), inc="dve")
                    psr0_free = last
                return last

            for qq in range(2):
                qh = 2 * kvh + qq
                slq = qh % 4
                tiles = [(qb[:, slq * TO + th * 512:slq * TO + th * 512 + 512], th)
                         for th in range(2)]
                mq_done[qh] = rowmax(tiles, 32 + qh, [q_ld[qh]])
            ktiles = [(kf[:, cn * 512:cn * 512 + 512], cn) for cn in range(16)]
            kn_last = rowmax(ktiles, 1, [kld])

            def vsmm(t):
                r = None
                for jt in range(64):
                    r = t.matmul(ps[3][:, 0:1],
                                 vt[:, jt * 128:jt * 128 + 128],
                                 onecb[:, :], start=(jt == 0), stop=(jt == 63))
                return r
            w_vs = [vld]
            if vse_prev is not None:
                w_vs.append(("dve", vse_prev))
            if grp_done is not None:
                w_vs.append(("dve", grp_done))
            c_vs = p.emit("tensor", vsmm, waits=w_vs, inc="pe")
            c_vse = p.emit("vector", lambda v: v.tensor_scalar(
                ccol[:, 1:2], ps[3][:, 0:1], 1.0 / SEQ, None, ALU.mult),
                waits=[("pe", c_vs), u0_ld], inc="dve")
            p.emit("vector", lambda v: v.tensor_tensor(
                ccol[:, 4:5], ccol[:, 1:2], ccol[:, 3:4], ALU.mult))
            vse_prev = p.emit("vector", lambda v: v.tensor_scalar(
                ccol[:, 5:6], ccol[:, 3:4], -1.0, 1.0, ALU.mult, ALU.add), inc="dve")

            for qc in range(2):
                w_sg = [("pe", p.v("pe"))]
                sg_ld = p.dma("sync", lambda sy, qc=qc: sy.dma_start(
                    out=sgf[:1, :], in_=sg_h.ap()[0:1, qc * SEQ:(qc + 1) * SEQ]),
                    "lsg", 1, waits=w_sg)
                for qq in range(2):
                    qh = 2 * kvh + qq
                    slq = qh % 4
                    p.emit("vector", lambda v, qh=qh: v.tensor_tensor(
                        srow[:1, 2:3], srow[:1, 32 + qh:33 + qh], srow[:1, 1:2],
                        ALU.mult),
                        waits=[("dve", kn_last), ("dve", mq_done[qh]),
                               ("act", c_slo)])
                    c_q1 = p.v("dve")
                    c_q2 = p.emit("scalar", lambda a: a.activation(
                        out=srow[:1, 3:4], in_=srow[:1, 2:3], func=AF.Sqrt),
                        waits=[("dve", c_q1)], inc="act")
                    p.emit("vector", lambda v, qh=qh: v.tensor_scalar(
                        srow[:1, 4:5], slo[0:1, qh:qh + 1], 257.0, None, ALU.mult),
                        waits=[("act", c_q2)])
                    p.emit("vector", lambda v: v.tensor_scalar(
                        srow[:1, 5:6], srow[:1, 3:4], -SCALE, None, ALU.mult))
                    p.emit("vector", lambda v: v.tensor_tensor(
                        srow[:1, 6:7], srow[:1, 4:5], srow[:1, 5:6], ALU.add))
                    p.emit("vector", lambda v, qh=qh: v.tensor_scalar(
                        srow[:1, 7:8], slo[0:1, qh:qh + 1], 1.0 / SCALE, None,
                        ALU.mult))
                    p.emit("vector", lambda v, qh=qh: v.tensor_scalar(
                        srow[:1, 8:9], slo[0:1, qh:qh + 1], -1.0, None, ALU.mult))
                    p.emit("vector", lambda v, qh=qh: v.tensor_copy(
                        srow[:1, 9:10], slo[0:1, qh:qh + 1]))
                    c_R = p.emit("vector", lambda v: v.tensor_scalar(
                        Rrow[:1, :], irow[:1, :], srow[0:1, 7:8], None, ALU.mult),
                        waits=[ir_ld], inc="dve")
                    w_cb = [("dve", c_R)]
                    if grp_done is not None:
                        w_cb.append(("dve", grp_done))
                    c_cb2 = p.emit("tensor", lambda t: t.matmul(
                        ps[3][:, 1:5], oner[:1, :], srow[:1, 6:10], start=True,
                        stop=True), waits=w_cb, inc="pe")
                    c_cbe = p.emit("vector", lambda v: v.tensor_copy(
                        ccol[:, 6:10], ps[3][:, 1:5]), waits=[("pe", c_cb2)],
                        inc="dve")
                    c_bcl = p.emit("vector", lambda v: v.tensor_scalar(
                        bcol[:, :], sj[:, :], ccol[:, 9:10], ccol[:, 6:7],
                        ALU.mult, ALU.add), waits=[sj_ld], inc="dve")

                    sc_mm = [None] * 64
                    pT_done = [None] * 64
                    cons_pv = [None] * 64
                    cons_on = [None] * 64
                    for jt in range(64):
                        sl = jt % 2
                        is_str = jt in STRADDLE[qc]
                        w_sc = [sg_ld] if jt == 0 else []
                        if jt == 0:
                            w_sc += [("dve", c_bcl), ("dve", d3_done)]
                        if jt >= 2:
                            w_sc.append(("act", pT_done[jt - 2]))
                        elif prev_exp is not None:
                            w_sc.append(("act", prev_exp))

                        def scmm(t, jt=jt, sl=sl, slq=slq, qc=qc, is_str=is_str):
                            m1 = t.matmul(
                                ps[sl][:, :],
                                kf[:, jt * 128:jt * 128 + 128],
                                qb[:, slq * TO + qc * 512:slq * TO + qc * 512 + 512],
                                start=True, stop=is_str)
                            if is_str:
                                return m1
                            return t.matmul(
                                ps[sl][:, :],
                                sgf[0:1, jt * 128:jt * 128 + 128],
                                Rrow[0:1, qc * 512:qc * 512 + 512],
                                start=False, stop=True)
                        sc_mm[jt] = p.emit("tensor", scmm, waits=w_sc, inc="pe")

                        w_pt = [("pe", sc_mm[jt])]
                        if jt >= 2:
                            w_pt += [("pe", cons_on[jt - 2])]
                        if not is_str:
                            pT_done[jt] = p.emit(
                                "scalar", lambda a, sl=sl, qc=qc, jt=jt: a.activation(
                                    out=pT[:, sl * 512:sl * 512 + 512],
                                    in_=ps[sl][:, :], func=AF.Exp,
                                    bias=bcol[:, qc * 64 + jt:qc * 64 + jt + 1],
                                    scale=SCALE), waits=w_pt, inc="act")
                        else:
                            ssl = SSLOT[(qc, jt)]
                            p.emit("vector", lambda v, ssl=ssl: v.tensor_scalar(
                                t0[:, :], d3sb[:, ssl * 512:ssl * 512 + 512],
                                ccol[:, 8:9], ccol[:, 6:7], ALU.mult, ALU.add),
                                waits=[("pe", sc_mm[jt])])
                            c_ex = p.emit("vector",
                                          lambda v, sl=sl: v.scalar_tensor_tensor(
                                              t1[:, :], ps[sl][:, :], SCALE,
                                              t0[:, :], ALU.mult, ALU.add),
                                          waits=w_pt[1:], inc="dve")
                            pT_done[jt] = p.emit(
                                "scalar", lambda a, sl=sl: a.activation(
                                    out=pT[:, sl * 512:sl * 512 + 512],
                                    in_=t1[:, :], func=AF.Exp),
                                waits=[("dve", c_ex)], inc="act")

                        w_cons = [("act", pT_done[jt])]
                        if jt == 0:
                            w_cons.append(vld)
                            if prev_evac is not None:
                                w_cons.append(("act", prev_evac))
                        cons_pv[jt] = p.emit("tensor", lambda t, sl=sl, jt=jt: t.matmul(
                            ps[2][:, :], vt[:, jt * 128:jt * 128 + 128],
                            pT[:, sl * 512:sl * 512 + 512],
                            start=(jt == 0), stop=(jt == 63)), waits=w_cons, inc="pe")
                        w_on = []
                        if jt == 0 and prev_sum is not None:
                            w_on.append(("dve", prev_sum))
                        cons_on[jt] = p.emit("tensor", lambda t, sl=sl, jt=jt: t.matmul(
                            psr[1][:1, :], onecb[:, :],
                            pT[:, sl * 512:sl * 512 + 512],
                            start=(jt == 0), stop=(jt == 63)), waits=w_on, inc="pe")
                    prev_exp = pT_done[63]
                    c_sum = p.emit("vector", lambda v: v.tensor_copy(
                        rowg[:1, 0:512], psr[1][:1, :]),
                        waits=[("pe", cons_on[63])], inc="dve")
                    prev_sum = p.emit("vector", lambda v: v.reciprocal(
                        rowg[:1, 0:512], rowg[:1, 0:512]), inc="dve")
                    c_nb = p.emit("tensor", lambda t: t.matmul(
                        ps[3][:, :], oner[:1, :], rowg[:1, 0:512], start=True,
                        stop=True), waits=[("dve", prev_sum)], inc="pe")
                    prev_evac = p.emit("scalar", lambda a: a.activation(
                        out=t1[:, :], in_=ps[2][:, :], func=AF.Identity),
                        waits=[("pe", cons_pv[63])], inc="act")
                    c_out = p.emit("vector", lambda v, qh=qh, qc=qc: v.tensor_tensor(
                        outT[:, qh * TO + qc * 512:qh * TO + qc * 512 + 512],
                        t1[:, :], ps[3][:, :], ALU.mult),
                        waits=[("act", prev_evac), ("pe", c_nb)], inc="dve")
                    if qc == 0:
                        p.emit("vector", lambda v, qh=qh: v.tensor_scalar(
                            t0[:, 0:64], outT[:, qh * TO:qh * TO + 64],
                            ccol[:, 5:6], None, ALU.mult),
                            waits=[("dve", vse_prev)])
                        p.emit("vector", lambda v: v.tensor_scalar(
                            t0[:, 64:128], o64[:, 0:64], ccol[:, 4:5], None,
                            ALU.mult))
                        c_out = p.emit("vector", lambda v, qh=qh: v.tensor_tensor(
                            outT[:, qh * TO:qh * TO + 64], t0[:, 0:64],
                            t0[:, 64:128], ALU.add), inc="dve")
                    grp_done = p.v("dve")
                    out_written[qh] = grp_done
            head_pe_done[kvh] = p.v("pe")
        att_done = p.v("dve")
        att_act = p.v("act")

        stats_hist = []
        sqc_hist = [None] * 32
        sth = [None] * 32
        evd = [None] * 32
        wo_mm = [None] * 32
        for ct in range(16):
            po = (ct % 2) * 2048
            ap = bass.AP(Wo_h, ct * 128, [[DM, 128], [128 * DM, 16], [1, 128]])
            w_pan = [("pe", wo_mm[2 * ct - 3])] if ct >= 2 else []
            pan_c = p.dma("gpsimd", lambda g, ap=ap, po=po: g.dma_start(
                out=wpan[:, po:po + 2048], in_=ap), "wp", 2, waits=w_pan)
            bo_c = p.dma("sync", lambda sy, ct=ct: sy.dma_start(
                out=ccol[:, 0:1], in_=bo_h.ap()[ct * 128:(ct + 1) * 128, :]),
                "bo", 1, waits=[("dve", evd[2 * ct - 1] if ct >= 1 else att_done)])
            for th in range(2):
                i = ct * 2 + th
                x2_c = p.dma("sync", lambda sy, ct=ct, th=th, i=i: sy.dma_start(
                    out=x2b[:, (i % 4) * 512:(i % 4) * 512 + 512],
                    in_=x2_h.ap()[ct * 128:(ct + 1) * 128, th * 512:th * 512 + 512]),
                    "x2", 4, waits=[("dve", evd[i - 4])] if i >= 4 else [])

                def mmw(t, po=po, th=th):
                    r = None
                    for kt in range(16):
                        r = t.matmul(ps[th][:, :],
                                     wpan[:, po + kt * 128:po + kt * 128 + 128],
                                     outT[:, kt * TO + th * 512:kt * TO + th * 512 + 512],
                                     start=(kt == 0), stop=(kt == 15))
                    return r
                w_mm = [pan_c, ("dve", evd[i - 2] if i >= 2 else att_done)]
                wo_mm[i] = p.emit("tensor", mmw, waits=w_mm, inc="pe")
                w_ev = [("pe", wo_mm[i]), x2_c, bo_c]
                if i >= 4:
                    w_ev.append(sth[i - 4])
                if i >= 2:
                    w_ev.append(("act", sqc_hist[i - 2]))
                elif att_act:
                    w_ev.append(("act", att_act))
                p.emit("vector", lambda v, th=th: v.tensor_scalar(
                    t0[:, :], ps[th][:, :], ccol[:, 0:1], None, ALU.add), waits=w_ev)
                c_ev = p.emit("vector", lambda v, i=i: v.tensor_tensor(
                    x0buf[:, (i % 4) * 512:(i % 4) * 512 + 512], t0[:, :],
                    x2b[:, (i % 4) * 512:(i % 4) * 512 + 512], ALU.add), inc="dve")
                evd[i] = c_ev
                sqc_hist[i] = p.emit("scalar", lambda a, i=i: a.activation(
                    out=sqk[:, (i % 2) * 512:(i % 2) * 512 + 512],
                    in_=x0buf[:, (i % 4) * 512:(i % 4) * 512 + 512], func=AF.Square),
                    waits=[("dve", c_ev)] +
                    ([("pe", stats_hist[i - 2])] if i >= 2 else []), inc="act")
                p.emit("tensor", lambda t, th=th, ct=ct, i=i: t.matmul(
                    psr[th][:1, :], onec[:, :],
                    x0buf[:, (i % 4) * 512:(i % 4) * 512 + 512],
                    start=(ct == 0), stop=(ct == 15)),
                    waits=[("act", sqc_hist[i])] +
                    ([("dve", att_done)] if i <= 1 else []), inc="pe")
                stats_hist.append(p.emit("tensor", lambda t, th=th, ct=ct, i=i: t.matmul(
                    psr[2 + th][:1, :], onecb[:, :],
                    sqk[:, (i % 2) * 512:(i % 2) * 512 + 512],
                    start=(ct == 0), stop=(ct == 15)), inc="pe"))
                sth[i] = p.dma("sync", lambda sy, ct=ct, th=th, i=i: sy.dma_start(
                    out=x0d[ct * 128:(ct + 1) * 128, th * 512:th * 512 + 512],
                    in_=x0buf[:, (i % 4) * 512:(i % 4) * 512 + 512]), "so", 4,
                    waits=[("dve", c_ev)])
        stats_pe = p.v("pe")

        p.emit("vector", lambda v: v.tensor_scalar(rowm[:1, 0:512], psr[0][:1, :],
                                                   1.0 / DM, None, ALU.mult),
               waits=[("pe", stats_pe)])
        p.emit("vector", lambda v: v.tensor_scalar(rowm[:1, 512:1024], psr[1][:1, :],
                                                   1.0 / DM, None, ALU.mult))
        p.emit("vector", lambda v: v.tensor_scalar(rowv[:1, 0:512], psr[2][:1, :],
                                                   1.0 / DM, None, ALU.mult))
        p.emit("vector", lambda v: v.tensor_scalar(rowv[:1, 512:1024], psr[3][:1, :],
                                                   1.0 / DM, None, ALU.mult))
        p.emit("vector", lambda v: v.tensor_tensor(rowg[:1, :], rowm[:1, :],
                                                   rowm[:1, :], ALU.mult))
        c_d = p.emit("vector", lambda v: v.tensor_tensor(rowv[:1, :], rowv[:1, :],
                                                         rowg[:1, :], ALU.subtract),
                     inc="dve")
        p.emit("scalar", lambda a: a.activation(out=rowr[:1, :], in_=rowv[:1, :],
                                                func=AF.Ln, bias=epsc[0:1, 0:1]),
               waits=[("dve", c_d)])
        c_a = p.emit("scalar", lambda a: a.activation(out=rowr[:1, :], in_=rowr[:1, :],
                                                      func=AF.Exp, scale=-0.5),
                     inc="act")
        prev = [None] * 32
        last_dve = None
        for th in range(2):
            p.emit("tensor", lambda t, th=th: t.matmul(
                ps[2][:, :], oner[:1, :], rowm[:1, th * 512:th * 512 + 512],
                start=True, stop=True), waits=[("act", c_a)] +
                ([("dve", last_dve)] if th == 1 else []), inc="pe")
            c_bc2 = p.emit("tensor", lambda t, th=th: t.matmul(
                ps[3][:, :], oner[:1, :], rowr[:1, th * 512:th * 512 + 512],
                start=True, stop=True), inc="pe")
            for ct in range(16):
                i = th * 16 + ct
                xsl = (i % 2) * 512
                w_rl = [sth[j] for j in range(28, 32)] if i == 0 else []
                if i >= 2:
                    w_rl += [("dve", prev[i - 2][0]), prev[i - 2][1]]
                rl = p.dma("scalar", lambda sq, ct=ct, th=th, xsl=xsl: sq.dma_start(
                    out=x0buf[:, xsl:xsl + 512],
                    in_=x0d[ct * 128:(ct + 1) * 128, th * 512:th * 512 + 512]),
                    "rl", 2, waits=w_rl)
                p.emit("vector", lambda v, xsl=xsl: v.tensor_tensor(
                    t0[:, :], x0buf[:, xsl:xsl + 512], ps[2][:, :], ALU.subtract),
                    waits=[rl] + ([("pe", c_bc2)] if ct == 0 else []))
                c_n = p.emit("vector", lambda v, xsl=xsl: v.tensor_tensor(
                    x0buf[:, xsl:xsl + 512], t0[:, :], ps[3][:, :], ALU.mult),
                    inc="dve")
                p.emit("vector", lambda v, xsl=xsl: v.tensor_reduce(
                    mxa[:, 1:2], x0buf[:, xsl:xsl + 512], AX.X, ALU.max,
                    apply_absolute_value=True))
                p.emit("vector", lambda v: v.tensor_tensor(
                    mxa[:, 0:1], mxa[:, 0:1], mxa[:, 1:2], ALU.max))
                c_o = p.dma("sync", lambda sy, ct=ct, th=th, xsl=xsl: sy.dma_start(
                    out=x3T_o[ct * 128:(ct + 1) * 128, th * 512:th * 512 + 512],
                    in_=x0buf[:, xsl:xsl + 512]), "xo", 2, waits=[("dve", c_n)])
                prev[i] = (c_n, c_o)
            last_dve = p.v("dve")
        c_amx = p.v("dve")
        c_tr = p.emit("tensor", lambda t: t.transpose(psr[0][:1, 0:128], mxa[:, 0:1],
                                                      idn[:, :]),
                      waits=[("dve", c_amx), ("dve", c_idn)], inc="pe")
        c_am = p.emit("vector", lambda v: v.tensor_reduce(
            srow[:1, 0:1], psr[0][:1, 0:128], AX.X, ALU.max), waits=[("pe", c_tr)],
            inc="dve")
        p.dma("sync", lambda sy: sy.dma_start(out=amx_o[:, :], in_=srow[:1, 0:1]),
              "am", 1, waits=[("dve", c_am)])

        allw = [(n, p.c[n]) for n in list(p.h.keys())]
        p.emit("sync", lambda sy: None, waits=allw)
        with nc.Block() as block:
            p.replay(block)
    return nc



F32 = mybir.dt.float32
BF16 = mybir.dt.bfloat16
AF = mybir.ActivationFunctionType
ALU = mybir.AluOpType
AX = mybir.AxisListType

DM = 2048
TO = 1024
C1 = float(2 ** 20 + 1)
C2 = float(2 ** 14)


def build_launch4(Prog, dbg=False):
    nc = bass.Bass("TRN2", target_bir_lowering=False, debug=False)
    x3T_h = nc.dram_tensor("x3T", [DM, TO], F32, kind="ExternalInput")
    wT_h = nc.dram_tensor("wT", [DM, DM], BF16, kind="ExternalInput")
    am8_h = nc.dram_tensor("am8", [1, 8], F32, kind="ExternalInput")
    yT_o = nc.dram_tensor("yT", [DM, TO], F32, kind="ExternalOutput").ap()
    x0d = nc.dram_tensor("x0d", [DM, TO], F32,
                         kind="ExternalOutput" if dbg else "Internal").ap()
    x3T = x3T_h.ap()

    with ExitStack() as ctx:
        e = ctx.enter_context
        p = Prog(nc, ctx, [])
        x3 = e(nc.sbuf_tensor("x3", [128, 16 * TO], F32))
        wsb = e(nc.sbuf_tensor("wsb", [128, 16 * DM], BF16))
        xq = e(nc.sbuf_tensor("xq", [128, 16 * TO], BF16))
        t0 = e(nc.sbuf_tensor("t0", [128, 512], F32))
        t1 = e(nc.sbuf_tensor("t1", [128, 512], F32))
        t2 = e(nc.sbuf_tensor("t2", [128, 512], F32))
        t3 = e(nc.sbuf_tensor("t3", [128, 512], F32))
        rowm = e(nc.sbuf_tensor("rowm", [1, TO], F32))
        rowv = e(nc.sbuf_tensor("rowv", [1, TO], F32))
        rowr = e(nc.sbuf_tensor("rowr", [1, TO], F32))
        rowg = e(nc.sbuf_tensor("rowg", [1, TO], F32))
        onec = e(nc.sbuf_tensor("onec", [128, 1], F32))
        onecb = e(nc.sbuf_tensor("onecb", [128, 1], BF16))
        oner = e(nc.sbuf_tensor("oner", [1, 128], F32))
        epsc = e(nc.sbuf_tensor("epsc", [1, 1], F32))
        idn = e(nc.sbuf_tensor("idn", [128, 128], F32))
        mxc = e(nc.sbuf_tensor("mxc", [128, 4], F32))
        srow = e(nc.sbuf_tensor("srow", [1, 16], F32))
        sqs = e(nc.sbuf_tensor("sqs", [128, 2 * 512], BF16))
        x0buf = e(nc.sbuf_tensor("x0buf", [128, 4 * 512], F32))
        ps = [e(nc.psum_tensor(f"ps{i}", [128, 512], F32)) for i in range(4)]
        psr = [e(nc.psum_tensor(f"psr{i}", [1, 512], F32)) for i in range(4)]


        p.emit("gpsimd", lambda g: g.memset(onec[:, :], 1.0), gpser=True)
        p.emit("gpsimd", lambda g: g.memset(onecb[:, :], 1.0), gpser=True)
        p.emit("gpsimd", lambda g: g.memset(oner[:1, :], 1.0), gpser=True)
        p.emit("gpsimd", lambda g: g.memset(epsc[:1, :], 1e-5), gpser=True)
        p.emit("gpsimd", lambda g: g.iota(idn[:, :], [[-1, 128]], base=0,
                                          channel_multiplier=1,
                                          allow_small_or_imprecise_dtypes=True),
               gpser=True)
        init_gp = p.v("gp")
        c_idn = p.emit("vector", lambda v: v.tensor_scalar(
            idn[:, :], idn[:, :], 0.0, None, ALU.is_equal), waits=[("gp", init_gp)],
            inc="dve")

        x3_ld = []
        for kt in range(16):
            x3_ld.append(p.dma("sync", lambda sy, kt=kt: sy.dma_start(
                out=x3[:, kt * TO:(kt + 1) * TO], in_=x3T[kt * 128:(kt + 1) * 128, :]),
                "lx", 4))
        w_ld = []
        for kt in range(16):
            w_ld.append(p.dma("scalar", lambda sq, kt=kt: sq.dma_start(
                out=wsb[:, kt * DM:(kt + 1) * DM],
                in_=wT_h.ap()[kt * 128:(kt + 1) * 128, :]), "lw", 4))
        am_ld = p.dma("sync", lambda sy: sy.dma_start(out=srow[:1, 0:8],
                                                      in_=am8_h.ap()[:, :]), "la", 1)

        for kt in range(16):
            p.emit("vector", lambda v, kt=kt: v.tensor_reduce(
                mxc[:, 1:2], wsb[:, kt * DM:(kt + 1) * DM], AX.X, ALU.max,
                apply_absolute_value=True), waits=[w_ld[kt]])
            if kt == 0:
                p.emit("vector", lambda v: v.tensor_copy(mxc[:, 0:1], mxc[:, 1:2]))
            else:
                p.emit("vector", lambda v: v.tensor_tensor(
                    mxc[:, 0:1], mxc[:, 0:1], mxc[:, 1:2], ALU.max))
        c_wmax = p.emit("vector", lambda v: None, inc="dve")
        c_tr = p.emit("tensor", lambda t: t.transpose(psr[0][:1, 0:128], mxc[:, 0:1],
                                                      idn[:, :]),
                      waits=[("dve", c_wmax), ("dve", c_idn)], inc="pe")
        p.emit("vector", lambda v: v.tensor_reduce(
            srow[:1, 8:9], psr[0][:1, 0:128], AX.X, ALU.max), waits=[("pe", c_tr)])
        p.emit("vector", lambda v: v.tensor_reduce(
            srow[:1, 9:10], srow[:1, 0:8], AX.X, ALU.max), waits=[am_ld])
        p.emit("vector", lambda v: v.tensor_scalar(
            srow[:1, 10:12], srow[:1, 8:10], 1.0 / 448.0, 1e-12, ALU.mult, ALU.max))
        p.emit("vector", lambda v: v.reciprocal(srow[:1, 12:14], srow[:1, 10:12]))
        c_sc = p.emit("vector", lambda v: v.tensor_tensor(
            srow[:1, 14:15], srow[:1, 10:11], srow[:1, 11:12], ALU.mult), inc="dve")
        c_bc = p.emit("tensor", lambda t: t.matmul(ps[0][:, 0:3], oner[:1, :],
                                                   srow[:1, 12:15], start=True,
                                                   stop=True),
                      waits=[("dve", c_sc)], inc="pe")
        c_sccol = p.emit("vector", lambda v: v.tensor_copy(mxc[:, 1:4], ps[0][:, 0:3]),
                         waits=[("pe", c_bc)], inc="dve")

        def e4m3(dst, src, scol, extra_waits, eng="vector"):
            g = (lambda fn, w=(): p.emit(eng, fn, waits=w,
                                         gpser=(eng == "gpsimd")))
            g(lambda v: v.tensor_scalar(t0[:, :], src, scol, None, ALU.mult),
              extra_waits)
            g(lambda v: v.tensor_scalar(t1[:, :], t0[:, :], C1, None, ALU.mult))
            g(lambda v: v.tensor_tensor(t2[:, :], t1[:, :], t0[:, :], ALU.subtract))
            g(lambda v: v.tensor_tensor(t1[:, :], t1[:, :], t2[:, :], ALU.subtract))
            g(lambda v: v.tensor_scalar(t2[:, :], t0[:, :], C2, -C2, ALU.add, ALU.add))
            g(lambda v: v.tensor_tensor(t3[:, :], t0[:, :], t0[:, :], ALU.mult))
            g(lambda v: v.tensor_scalar(t3[:, :], t3[:, :], float(2 ** -12), None,
                                        ALU.is_lt))
            g(lambda v: v.tensor_tensor(t2[:, :], t2[:, :], t1[:, :], ALU.subtract))
            g(lambda v: v.tensor_tensor(t3[:, :], t3[:, :], t2[:, :], ALU.mult))
            return p.emit(eng, lambda v: v.tensor_tensor(dst, t1[:, :], t3[:, :],
                                                         ALU.add),
                          gpser=(eng == "gpsimd"))

        g0 = e(nc.sbuf_tensor("g0", [128, 512], F32))
        g1 = e(nc.sbuf_tensor("g1", [128, 512], F32))
        g2 = e(nc.sbuf_tensor("g2", [128, 512], F32))
        g3 = e(nc.sbuf_tensor("g3", [128, 512], F32))

        def e4m3_gp(dst, src, scol, extra_waits):
            g = (lambda fn, w=(): p.emit("gpsimd", fn, waits=w, gpser=True))
            g(lambda v: v.tensor_scalar(g0[:, :], src, scol, None, ALU.mult),
              extra_waits)
            g(lambda v: v.tensor_scalar(g1[:, :], g0[:, :], C1, None, ALU.mult))
            g(lambda v: v.tensor_tensor(g2[:, :], g1[:, :], g0[:, :], ALU.subtract))
            g(lambda v: v.tensor_tensor(g1[:, :], g1[:, :], g2[:, :], ALU.subtract))
            g(lambda v: v.tensor_scalar(g2[:, :], g0[:, :], C2, -C2, ALU.add, ALU.add))
            g(lambda v: v.tensor_tensor(g3[:, :], g0[:, :], g0[:, :], ALU.mult))
            g(lambda v: v.tensor_scalar(g3[:, :], g3[:, :], float(2 ** -12), None,
                                        ALU.is_lt))
            g(lambda v: v.tensor_tensor(g2[:, :], g2[:, :], g1[:, :], ALU.subtract))
            g(lambda v: v.tensor_tensor(g3[:, :], g3[:, :], g2[:, :], ALU.mult))
            return g(lambda v: v.tensor_tensor(dst, g1[:, :], g3[:, :], ALU.add))

        xq_done = [None] * 32
        for i in range(32):
            kt, th = i // 2, i % 2
            w0 = [("dve", c_sccol), x3_ld[kt]] if i <= 3 else [x3_ld[kt]]
            xq_done[i] = e4m3(xq[:, kt * TO + th * 512:kt * TO + th * 512 + 512],
                              x3[:, kt * TO + th * 512:kt * TO + th * 512 + 512],
                              mxc[:, 2:3], w0)
        wq_done = [None] * 64
        for i in range(64):
            kt, c4 = i // 4, i % 4
            w0 = [("dve", c_sccol), w_ld[kt]] if True else []
            wq_done[i] = e4m3_gp(wsb[:, kt * DM + c4 * 512:kt * DM + c4 * 512 + 512],
                                 wsb[:, kt * DM + c4 * 512:kt * DM + c4 * 512 + 512],
                                 mxc[:, 1:2], w0)
        xq_all = p.v("dve")
        wq_all = p.v("gp")

        stats_hist = []
        sth = [None] * 32
        evd = [None] * 32
        sqc_hist = [None] * 32
        for ct in range(16):
            def mm_th(th):
                def f(t, th=th, ct=ct):
                    r = None
                    for kt in range(16):
                        r = t.matmul(ps[th][:, :],
                                     wsb[:, kt * DM + ct * 128:kt * DM + ct * 128 + 128],
                                     xq[:, kt * TO + th * 512:kt * TO + th * 512 + 512],
                                     start=(kt == 0), stop=(kt == 15))
                    return r
                return f
            w_mm = [("dve", xq_all), ("gp", wq_all)] if ct == 0 else \
                [("dve", evd[2 * ct - 1])]
            mmc = [p.emit("tensor", mm_th(0), waits=w_mm, inc="pe"),
                   p.emit("tensor", mm_th(1), inc="pe")]
            for th in range(2):
                i = ct * 2 + th
                xsl = (i % 4) * 512
                w_ev = [("pe", mmc[th])]
                if i >= 4:
                    w_ev.append(sth[i - 4])
                if i >= 2:
                    w_ev.append(("act", sqc_hist[i - 2]))
                p.emit("vector", lambda v, th=th, xsl=xsl: v.tensor_scalar(
                    sqs[:, (th % 2) * 512:(th % 2) * 512 + 512], ps[th][:, :],
                    mxc[:, 3:4], None, ALU.mult), waits=w_ev)
                c_ev = p.emit("vector", lambda v, ct=ct, th=th, xsl=xsl: v.tensor_tensor(
                    x0buf[:, xsl:xsl + 512],
                    x3[:, ct * TO + th * 512:ct * TO + th * 512 + 512],
                    sqs[:, (th % 2) * 512:(th % 2) * 512 + 512], ALU.add), inc="dve")
                evd[i] = c_ev
                sq_c = p.emit("scalar", lambda a, xsl=xsl, i=i: a.activation(
                    out=sqs[:, (i % 2) * 512:(i % 2) * 512 + 512],
                    in_=x0buf[:, xsl:xsl + 512], func=AF.Square),
                    waits=[("dve", c_ev)] + ([("pe", stats_hist[i - 2])] if i >= 2 else []),
                    inc="act")
                sqc_hist[i] = sq_c
                p.emit("tensor", lambda t, th=th, ct=ct, xsl=xsl: t.matmul(
                    psr[th][:1, :], onec[:, :], x0buf[:, xsl:xsl + 512],
                    start=(ct == 0), stop=(ct == 15)), waits=[("act", sq_c)], inc="pe")
                stats_hist.append(p.emit("tensor", lambda t, th=th, ct=ct, i=i: t.matmul(
                    psr[2 + th][:1, :], onecb[:, :],
                    sqs[:, (i % 2) * 512:(i % 2) * 512 + 512],
                    start=(ct == 0), stop=(ct == 15)), inc="pe"))
                sth[i] = p.dma("sync", lambda sy, ct=ct, th=th, xsl=xsl: sy.dma_start(
                    out=x0d[ct * 128:(ct + 1) * 128, th * 512:th * 512 + 512],
                    in_=x0buf[:, xsl:xsl + 512]), "so", 4, waits=[("dve", c_ev)])
        stats_pe = p.v("pe")


        p.emit("vector", lambda v: v.tensor_scalar(rowm[:1, 0:512], psr[0][:1, :],
                                                   1.0 / DM, None, ALU.mult),
               waits=[("pe", stats_pe)])
        p.emit("vector", lambda v: v.tensor_scalar(rowm[:1, 512:1024], psr[1][:1, :],
                                                   1.0 / DM, None, ALU.mult))
        p.emit("vector", lambda v: v.tensor_scalar(rowv[:1, 0:512], psr[2][:1, :],
                                                   1.0 / DM, None, ALU.mult))
        p.emit("vector", lambda v: v.tensor_scalar(rowv[:1, 512:1024], psr[3][:1, :],
                                                   1.0 / DM, None, ALU.mult))
        p.emit("vector", lambda v: v.tensor_tensor(rowg[:1, :], rowm[:1, :],
                                                   rowm[:1, :], ALU.mult))
        c_d = p.emit("vector", lambda v: v.tensor_tensor(rowv[:1, :], rowv[:1, :],
                                                         rowg[:1, :], ALU.subtract),
                     inc="dve")
        p.emit("scalar", lambda a: a.activation(out=rowr[:1, :], in_=rowv[:1, :],
                                                func=AF.Ln, bias=epsc[0:1, 0:1]),
               waits=[("dve", c_d)])
        c_a = p.emit("scalar", lambda a: a.activation(out=rowr[:1, :], in_=rowr[:1, :],
                                                      func=AF.Exp, scale=-0.5),
                     inc="act")
        prev = [None] * 32
        last_dve = None
        for th in range(2):
            p.emit("tensor", lambda t, th=th: t.matmul(
                ps[2][:, :], oner[:1, :], rowm[:1, th * 512:th * 512 + 512],
                start=True, stop=True), waits=[("act", c_a)] +
                ([("dve", last_dve)] if th == 1 else []), inc="pe")
            c_bc2 = p.emit("tensor", lambda t, th=th: t.matmul(
                ps[3][:, :], oner[:1, :], rowr[:1, th * 512:th * 512 + 512],
                start=True, stop=True), inc="pe")
            for ct in range(16):
                i = th * 16 + ct
                xsl = (i % 2) * 512
                w_rl = [sth[j] for j in range(28, 32)] if i == 0 else []
                if i >= 2:
                    w_rl += [("dve", prev[i - 2][0]), prev[i - 2][1]]
                rl = p.dma("scalar", lambda sq, ct=ct, th=th, xsl=xsl: sq.dma_start(
                    out=x0buf[:, xsl:xsl + 512],
                    in_=x0d[ct * 128:(ct + 1) * 128, th * 512:th * 512 + 512]),
                    "rl", 2, waits=w_rl)
                p.emit("vector", lambda v, xsl=xsl: v.tensor_tensor(
                    t0[:, :], x0buf[:, xsl:xsl + 512], ps[2][:, :], ALU.subtract),
                    waits=[rl] + ([("pe", c_bc2)] if ct == 0 else []))
                c_n = p.emit("vector", lambda v, xsl=xsl: v.tensor_tensor(
                    x0buf[:, xsl:xsl + 512], t0[:, :], ps[3][:, :], ALU.mult),
                    inc="dve")
                c_o = p.dma("sync", lambda sy, ct=ct, th=th, xsl=xsl: sy.dma_start(
                    out=yT_o[ct * 128:(ct + 1) * 128, th * 512:th * 512 + 512],
                    in_=x0buf[:, xsl:xsl + 512]), "xo", 2, waits=[("dve", c_n)])
                prev[i] = (c_n, c_o)
            last_dve = p.v("dve")

        allw = [(n, p.c[n]) for n in list(p.h.keys())]
        p.emit("sync", lambda sy: None, waits=allw)
        with nc.Block() as block:
            p.replay(block)
    return nc



def host_prep_l1(z, c):
    W_qkv, W_z = z["L0_attn_W_qkv"], z["L0_attn_W_z"]
    W_b, W_a = z["L0_attn_W_b"], z["L0_attn_W_a"]
    conv_w = z["L0_attn_conv_w"]
    dt_bias, A_log = z["L0_attn_dt_bias"], z["L0_attn_A_log"]
    norm_w = z["L0_attn_norm_w"]
    qcols = W_qkv[:, 256 * c:256 * c + 256]
    kcols = W_qkv[:, 2048 + 256 * c:2048 + 256 * c + 256]
    vcols = W_qkv[:, 4096 + 512 * c:4096 + 512 * c + 512]
    zcols = W_z[:, 512 * c:512 * c + 512]
    bcols = W_b[:, 4 * c:4 * c + 4]
    acols = W_a[:, 4 * c:4 * c + 4]
    Wcat = np.ascontiguousarray(np.concatenate(
        [qcols, kcols, vcols, zcols, bcols, acols], axis=1))
    cw = np.zeros((128, 32), np.float32)
    for ft in range(8):
        for pp in range(128):
            if ft < 2:
                ch = 256 * c + ft * 128 + pp
            elif ft < 4:
                ch = 2048 + 256 * c + (ft - 2) * 128 + pp
            else:
                ch = 4096 + 512 * c + (ft - 4) * 128 + pp
            cw[pp, ft * 4:ft * 4 + 4] = conv_w[ch, :]
    dtbv = dt_bias[4 * c:4 * c + 4].reshape(4, 1).astype(np.float32)
    alogv = A_log[4 * c:4 * c + 4].reshape(4, 1).astype(np.float32)
    nwbcv = np.ascontiguousarray(np.broadcast_to(norm_w, (128, 128))).astype(np.float32)
    idnbv = np.eye(128).astype(BF)
    ii = np.arange(128)[:, None]
    jj = np.arange(128)[None, :]
    blk = (ii // 64) == (jj // 64)
    mslv = (-1.0 * (blk & (ii > jj))).astype(np.float32)
    muiv = (1.0 * (blk & (jj >= ii))).astype(np.float32)
    masks = np.concatenate([mslv, muiv], axis=1).astype(np.float32)
    return {"Wcat": Wcat, "cw": cw, "dtb": dtbv, "alog": alogv, "nwbc": nwbcv,
            "idnb": idnbv, "masks": masks}


def host_prep_l3(c):
    SJ = np.zeros((128, 128), np.float32)
    SG = np.ones((1, 2 * SEQ), np.float32)
    for qc in (0, 1):
        i_lo = 1024 * c + 512 * qc
        i_hi = i_lo + 511
        for jt in range(64):
            strad = jt in STRADDLE[qc]
            for pp in range(128):
                jp = 128 * jt + pp
                ja = (jp + 1024 * c) % SEQ
                if strad:
                    continue
                if ja < 64:
                    SJ[pp, qc * 64 + jt] = -1e12
                    SG[0, qc * SEQ + jp] = 1.0
                else:
                    sgn = 1.0 if ja > i_hi else -1.0
                    SJ[pp, qc * 64 + jt] = sgn * (i_lo - ja)
                    SG[0, qc * SEQ + jp] = sgn
    PEN = np.zeros((128, NSL), np.float32)
    for (qc, jt), sl in SSLOT.items():
        for pp in range(128):
            jp = 128 * jt + pp
            ja = (jp + 1024 * c) % SEQ
            wrapped = jp >= (SEQ - 1024 * c) if c > 0 else False
            kill = ja < 64
            if (qc, jt) in WRAP:
                if not wrapped:
                    kill = True
            else:
                if wrapped:
                    kill = True
            if kill:
                PEN[pp, sl] = 1e12
    u0 = np.full((128, 1), 1.0 if c == 0 else 0.0, np.float32)
    irow = (1024 * c + np.arange(1024)).astype(np.float32).reshape(1, 1024)
    hrow = (-(np.arange(16) + 1) / 2.0).astype(np.float32).reshape(1, 16)
    cstc = np.array([[2.0, 0, 0, 0]], np.float32)
    return SJ, SG, PEN, u0, irow, hrow, cstc


def _bf16(a):
    a = np.asarray(a)
    if a.dtype == np.uint16:
        return a.view(BF)
    return a.astype(BF) if a.dtype != BF else a


U32 = mybir.dt.uint32
from concourse.aot_env import allow_ici_all_to_all
_base = 2.0 ** (-2.0 ** (-(np.log2(np.float64(16.0)) - 3.0)))
SLOPES = [float(_base ** (h + 1)) for h in range(16)]
def _dmin(qc, jt):
    lo = 128 * jt - (512 * qc + 511)
    hi = 128 * jt + 127 - 512 * qc
    dm = 10 ** 9
    for dd in range(lo, hi + 1):
        a = abs(dd)
        dm = min(dm, min(a, 8192 - a))
    return dm
KEEP = {}
for _qc in (0, 1):
    for _jt in range(64):
        _dm = _dmin(_qc, _jt)
        for _qh in range(16):
            _k = (_jt in STRADDLE[_qc]) or (SLOPES[_qh] * (_dm - 257) < 128.0)
            KEEP[(_qc, _jt, _qh)] = _k



def build_fused(Prog, dbg=False, nph=4):
    nc = bass.Bass("TRN2", target_bir_lowering=False, debug=False)
    TO = 1024
    rg = [list(range(8))]
    a2a_in_h = nc.dram_tensor("a2ain", [4096, TO], BF16)
    a2a_out_h = nc.dram_tensor("a2aout", [4096, TO], BF16)
    kag_in_h = nc.dram_tensor("kagin", [1024, TO], BF16)
    vag_in_h = nc.dram_tensor("vagin", [1024, TO], BF16)
    kag_h = nc.dram_tensor("kag", [SEQ, TO], BF16)
    vag_h = nc.dram_tensor("vag", [SEQ, TO], BF16)
    x2T_i = nc.dram_tensor("x2Ti", [DM, TO], F32)
    qroT_i = nc.dram_tensor("qroTi", [DM, TO], F32)
    x3T_i = nc.dram_tensor("x3Ti", [DM, TO], F32)
    arin_h = nc.dram_tensor("arin", [1, 1], F32)
    arout_h = nc.dram_tensor("arout", [1, 1], F32)
    idxk_h = nc.dram_tensor("idxk", [128, 64], U32, kind="ExternalInput")
    idn_h = nc.dram_tensor("idnb", [128, 128], BF16, kind="ExternalInput")
    yT_glob = [None]
    with ExitStack() as pctx:
        p = Prog(nc, pctx, ["din", "dw", "st", "m1", "m2", "m3"])
        def _phase1():
            hT_h = nc.dram_tensor("hidT", [2048, SEQ], F32, kind="ExternalInput")
            wc_h = nc.dram_tensor("Wcat", [2048, 1544], F32, kind="ExternalInput")
            cw_h = nc.dram_tensor("cw", [128, 32], F32, kind="ExternalInput")
            dtb_h = nc.dram_tensor("dtb", [4, 1], F32, kind="ExternalInput")
            alog_h = nc.dram_tensor("alog", [4, 1], F32, kind="ExternalInput")
            nw_h = nc.dram_tensor("nwbc", [128, 128], F32, kind="ExternalInput")
            msk_h = nc.dram_tensor("masks", [128, 256], F32, kind="ExternalInput")
            A2A_MODE = True
            if not A2A_MODE:
                nc_gout_handle = nc.dram_tensor("gout", [SEQ, 512], BF16,
                                                kind="ExternalOutput")
            pkgd = nc.dram_tensor("pkgd", [28, TS], F32).ap()
            rowd = nc.dram_tensor("rowd", [3, 4 * TS], F32).ap()

            with ExitStack() as ctx:
                e = ctx.enter_context
                wsb = e(nc.sbuf_tensor("f1wsb", [128, 16 * 1544], BF16))
                hb = e(nc.sbuf_tensor("f1hb", [128, 16 * 512], BF16))
                mr = e(nc.sbuf_tensor("f1mr", [128, 8 * 516], F32))
                cs = e(nc.sbuf_tensor("f1cs", [128, 512], F32))
                qs = e(nc.sbuf_tensor("f1qs", [128, 2 * TS], BF16))
                ks = e(nc.sbuf_tensor("f1ks", [128, 2 * TS], BF16))
                vs = e(nc.sbuf_tensor("f1vs", [128, 4 * TS], BF16))
                zs = e(nc.sbuf_tensor("f1zs", [128, 4 * TS], BF16))
                sq = e(nc.sbuf_tensor("f1sq", [128, 512], BF16))
                ba = e(nc.sbuf_tensor("f1ba", [8, 2 * 512], F32))
                gr = e(nc.sbuf_tensor("f1gr", [4, TS], F32))
                gc = e(nc.sbuf_tensor("f1gc", [4, TS], F32))
                br = e(nc.sbuf_tensor("f1br", [4, TS], F32))
                lnk = e(nc.sbuf_tensor("f1lnk", [4, TS], F32))
                lnq = e(nc.sbuf_tensor("f1lnq", [4, TS], F32))
                r1 = e(nc.sbuf_tensor("f1r1", [4, TS], F32))
                r2 = e(nc.sbuf_tensor("f1r2", [4, TS], F32))
                r3 = e(nc.sbuf_tensor("f1r3", [4, TS], F32))
                r4 = e(nc.sbuf_tensor("f1r4", [4, TS], F32))
                b1r = e(nc.sbuf_tensor("f1b1r", [1, 4 * TS], F32))
                egls = e(nc.sbuf_tensor("f1egls", [128, 64], F32))
                pkg28 = e(nc.sbuf_tensor("f1pkg28", [28, TS], F32))
                colsb = e(nc.sbuf_tensor("f1colsb", [128, 28], F32))
                s2q4 = e(nc.sbuf_tensor("f1s2q4", [4, TS], F32))
                s2k4 = e(nc.sbuf_tensor("f1s2k4", [4, TS], F32))
                ktm = e(nc.sbuf_tensor("f1ktm", [128, 2 * 128], BF16))
                vtm = e(nc.sbuf_tensor("f1vtm", [128, 4 * 128], BF16))
                ztm = e(nc.sbuf_tensor("f1ztm", [128, 4 * 128], BF16))
                asb = e(nc.sbuf_tensor("f1asb", [128, 4 * 128], BF16))
                atsb = e(nc.sbuf_tensor("f1atsb", [128, 4 * 128], BF16))
                antb = e(nc.sbuf_tensor("f1antb", [128, 4 * 128], BF16))
                ttsb = e(nc.sbuf_tensor("f1ttsb", [128, 4 * 128], BF16))
                ttf = e(nc.sbuf_tensor("f1ttf", [128, 4 * 128], F32))
                xsb = e(nc.sbuf_tensor("f1xsb", [128, 256], BF16))
                usb = e(nc.sbuf_tensor("f1usb", [128, 4 * 128], F32))
                wfm = e(nc.sbuf_tensor("f1wfm", [128, 4 * 128], BF16))
                kdt4 = e(nc.sbuf_tensor("f1kdt", [128, 512], BF16))
                vnb4 = e(nc.sbuf_tensor("f1vnb", [64, 512], BF16))
                sscol = e(nc.sbuf_tensor("f1sscol", [128, 8], F32))
                l2c = e(nc.sbuf_tensor("f1l2c", [4, 2], F32))
                Sf = e(nc.sbuf_tensor("f1Sf", [128, 4 * 128], F32))
                Sb = e(nc.sbuf_tensor("f1Sb", [128, 4 * 128], BF16))
                goutb = e(nc.sbuf_tensor("f1goutb", [128, 512], BF16))
                goutT = e(nc.sbuf_tensor("f1goutT", [128, 512], BF16))
                cwsb = e(nc.sbuf_tensor("f1cwsb", [128, 32], F32))
                dtb = e(nc.sbuf_tensor("f1dtbS", [4, 1], F32))
                nalog = e(nc.sbuf_tensor("f1nalog", [4, 1], F32))
                nwbc = e(nc.sbuf_tensor("f1nwbcS", [128, 128], F32))
                idnb = e(nc.sbuf_tensor("f1idnbS", [128, 128], BF16))
                msl = e(nc.sbuf_tensor("f1mslS", [128, 128], F32))
                mui = e(nc.sbuf_tensor("f1muiS", [128, 128], F32))
                idnf = e(nc.sbuf_tensor("f1idnf", [128, 128], F32))
                onecb = e(nc.sbuf_tensor("f1onecb", [128, 1], BF16))
                oner = e(nc.sbuf_tensor("f1oner", [1, 128], F32))
                sel0 = e(nc.sbuf_tensor("f1sel0", [128, 4], BF16))
                sel1 = e(nc.sbuf_tensor("f1sel1", [128, 4], BF16))
                e6c = e(nc.sbuf_tensor("f1e6c", [128, 1], F32))
                one4 = e(nc.sbuf_tensor("f1one4", [4, 1], F32))
                ps0 = e(nc.psum_tensor("f1ps0", [128, 512], F32))
                ps1 = e(nc.psum_tensor("f1ps1", [128, 512], F32))
                ps2 = e(nc.psum_tensor("f1ps2", [128, 512], F32))
                ps3 = e(nc.psum_tensor("f1ps3", [128, 512], F32))
                ps4 = e(nc.psum_tensor("f1ps4", [128, 512], F32))
                ps5 = e(nc.psum_tensor("f1ps5", [128, 512], F32))
                ps6 = e(nc.psum_tensor("f1ps6", [128, 1024], BF16))
                ps7 = e(nc.psum_tensor("f1ps7", [128, 512], F32))
                ps7b = ps7.bitcast(BF16)

                p.emit("gpsimd", lambda g: g.memset(onecb[:, :], 1.0), gpser=True)
                p.emit("gpsimd", lambda g: g.memset(oner[:1, :], 1.0), gpser=True)
                p.emit("gpsimd", lambda g: g.memset(e6c[:, :], 1e-6), gpser=True)
                p.emit("gpsimd", lambda g: g.memset(one4[:4, :], 1.0), gpser=True)
                p.emit("gpsimd", lambda g: g.memset(sel0[:, :], 0.0), gpser=True)
                p.emit("gpsimd", lambda g: g.memset(sel1[:, :], 0.0), gpser=True)
                p.emit("gpsimd", lambda g: g.memset(sel0[:, 0:2], 1.0), gpser=True)
                p.emit("gpsimd", lambda g: g.memset(sel1[:, 2:4], 1.0), gpser=True)
                p.emit("gpsimd", lambda g: g.memset(mr[:, :], 0.0), gpser=True)
                p.emit("gpsimd", lambda g: g.memset(Sf[:, :], 0.0), gpser=True)
                p.emit("gpsimd", lambda g: g.memset(Sb[:, :], 0.0), gpser=True)
                init_gp = p.v("gp")
                for kt in range(16):
                    p.dma("gpsimd", lambda g, kt=kt: g.dma_start(
                        out=wsb[:, kt * 1544:(kt + 1) * 1544],
                        in_=wc_h.ap()[kt * 128:(kt + 1) * 128, :]), "lw", 4)
                wld = [(n, p.c[n]) for n in ("lw0", "lw1", "lw2", "lw3")]
                cw_ld = p.dma("sync", lambda sy: sy.dma_start(out=cwsb[:, :],
                                                              in_=cw_h.ap()[:, :]), "lc", 1)
                dtb_ld = p.dma("sync", lambda sy: sy.dma_start(out=dtb[:4, :],
                                                               in_=dtb_h.ap()[:, :]), "lc2", 1)
                al_ld = p.dma("sync", lambda sy: sy.dma_start(out=nalog[:4, :],
                                                              in_=alog_h.ap()[:, :]), "lc3", 1)
                nw_ld = p.dma("sync", lambda sy: sy.dma_start(out=nwbc[:, :],
                                                              in_=nw_h.ap()[:, :]), "lc4", 1)
                id_ld = p.dma("sync", lambda sy: sy.dma_start(out=idnb[:, :],
                                                              in_=idn_h.ap()[:, :]), "lc5", 1)
                mk_ld = p.dma("sync", lambda sy: sy.dma_start(out=msl[:, :],
                                                              in_=msk_h.ap()[:, 0:128]),
                              "lc6", 1)
                mk2_ld = p.dma("sync", lambda sy: sy.dma_start(out=mui[:, :],
                                                               in_=msk_h.ap()[:, 128:256]),
                               "lc7", 1)
                c_ea = p.emit("scalar", lambda a: a.activation(
                    out=nalog[:4, :], in_=nalog[:4, :], func=AF.Exp), waits=[al_ld],
                    inc="act")
                c_na = p.emit("vector", lambda v: v.tensor_scalar(
                    nalog[:4, :], nalog[:4, :], -1.0, None, ALU.mult),
                    waits=[("act", c_ea)], inc="dve")
                c_idf = p.emit("vector", lambda v: v.tensor_copy(idnf[:, :], idnb[:, :]),
                               waits=[id_ld], inc="dve")

                hb_free = None
                seg_scan_done = None
                prev_pkgT = None
                st_gout = {}
                prev_pe_gt = None
                grp_ps01_free = None
                grp_ps4_free = None
                grp_ps3_free = None
                prev_k1 = None
                prevs = {}
                tr_i = [0]
                trev = {}
                kdc4 = e(nc.sbuf_tensor("f1kdc", [64, 512], BF16))
                ss_ev = [None]
                antc4 = e(nc.sbuf_tensor("f1antc", [64, 256], BF16))
                qec4 = e(nc.sbuf_tensor("f1qec", [64, 4], F32))

                for s in range(NSEG):
                    ev_ft = [None] * 13
                    for hf in range(2):
                        base_t = s * TS + hf * 512
                        hld = [None] * 16
                        for kt in range(16):
                            w_h = [("pe", hb_free)] if hb_free is not None else []
                            hld[kt] = p.dma("gpsimd", lambda g2, kt=kt, base_t=base_t:
                                            g2.dma_start(
                                                out=hb[:, kt * 512:(kt + 1) * 512],
                                                in_=hT_h.ap()[kt * 128:(kt + 1) * 128,
                                                              base_t:base_t + 512]),
                                            "lh", 4, waits=w_h)
                        for ct in range(13):
                            pp = [ps0, ps1][ct % 2]
                            if ct < 12:
                                def mmp(t, ct=ct, pp=pp):
                                    r = None
                                    for kt in range(16):
                                        r = t.matmul(pp[:, :],
                                                     wsb[:, kt * 1544 + ct * 128:
                                                         kt * 1544 + ct * 128 + 128],
                                                     hb[:, kt * 512:(kt + 1) * 512],
                                                     start=(kt == 0), stop=(kt == 15))
                                    return r
                            else:
                                pp = ps2

                                def mmp(t, pp=pp):
                                    r = None
                                    for kt in range(16):
                                        r = t.matmul(pp[0:8, :],
                                                     wsb[:, kt * 1544 + 1536:
                                                         kt * 1544 + 1544],
                                                     hb[:, kt * 512:(kt + 1) * 512],
                                                     start=(kt == 0), stop=(kt == 15))
                                    return r
                            w_mm = [hld[12], hld[13], hld[14], hld[15]]
                            if (s, hf, ct) == (0, 0, 0):
                                w_mm += wld
                            if (hf, ct) == (0, 0) and s > 0:
                                w_mm.append(("dve", seg_scan_done))
                            if ct >= 2:
                                w_mm.append(("dve", ev_ft[ct - 2]))
                            elif ev_ft[10 + ct] is not None:
                                w_mm.append(("dve", ev_ft[10 + ct]))
                            mmc = p.emit("tensor", mmp, waits=w_mm, inc="pe")
                            if ct == 12:
                                hb_free = mmc
                            if ct < 8:
                                ft = ct
                                w_ev = [("pe", mmc)]
                                if (s, hf) == (0, 0):
                                    w_ev += [("gp", init_gp), cw_ld]
                                p.emit("vector", lambda v, ft=ft, pp=pp: v.tensor_copy(
                                    mr[:, ft * 516 + 3:ft * 516 + 515], pp[:, :]),
                                    waits=w_ev)
                                p.emit("vector", lambda v, ft=ft: v.tensor_scalar(
                                    cs[:, :], mr[:, ft * 516 + 3:ft * 516 + 515],
                                    cwsb[:, ft * 4 + 3:ft * 4 + 4], None, ALU.mult))
                                for j in range(3):
                                    p.emit("vector", lambda v, ft=ft, j=j:
                                           v.scalar_tensor_tensor(
                                               cs[:, :], mr[:, ft * 516 + j:ft * 516 + j + 512],
                                               cwsb[:, ft * 4 + j:ft * 4 + j + 1], cs[:, :],
                                               ALU.mult, ALU.add))
                                c_halo = p.emit("vector", lambda v, ft=ft: v.tensor_copy(
                                    mr[:, ft * 516:ft * 516 + 3],
                                    mr[:, ft * 516 + 512:ft * 516 + 515]), inc="dve")
                                ev_ft[ct] = c_halo
                                dst = [qs, qs, ks, ks, vs, vs, vs, vs][ct]
                                hd = [0, 1, 0, 1, 0, 1, 2, 3][ct]
                                c_si = p.emit("scalar", lambda a, dst=dst, hd=hd, hf=hf:
                                              a.activation(
                                                  out=dst[:, hd * TS + hf * 512:
                                                          hd * TS + hf * 512 + 512],
                                                  in_=cs[:, :], func=AF.Silu),
                                              waits=[("dve", c_halo)], inc="act")
                                if ct < 4:
                                    c_sq = p.emit("scalar", lambda a, dst=dst, hd=hd, hf=hf:
                                                  a.activation(
                                                      out=sq[:, 0:512],
                                                      in_=dst[:, hd * TS + hf * 512:
                                                              hd * TS + hf * 512 + 512],
                                                      func=AF.Square), inc="act")
                                    sel = [sel0, sel1][hd]
                                    psq = ps7
                                    w_cm = [("act", c_sq)]
                                    if ct in (0, 2) and ss_ev[0] is not None:
                                        w_cm.append(("dve", ss_ev[0]))
                                    c_m = p.emit("tensor", lambda t, sel=sel, ct=ct: t.matmul(
                                        psq[0:4, :], sel[:, :], sq[:, 0:512],
                                        start=(ct in (0, 2)), stop=(ct in (1, 3))),
                                        waits=w_cm, inc="pe")
                                    if ct in (1, 3):
                                        d4 = [s2q4, s2k4][ct // 2]
                                        ss_ev[0] = p.emit("vector",
                                                          lambda v, d4=d4, hf=hf: v.tensor_copy(
                                                              d4[:4, hf * 512:hf * 512 + 512],
                                                              psq[0:4, :]),
                                                          waits=[("pe", c_m)] +
                                                          ([("act", prevs["q1"])]
                                                           if "q1" in prevs else []),
                                                          inc="dve")
                                ev_ft[ct] = p.emit("vector", lambda v: None,
                                                   waits=[("act", c_si)], inc="dve")
                            elif ct < 12:
                                hd = ct - 8
                                c_si = p.emit("scalar", lambda a, hd=hd, hf=hf, pp=pp:
                                              a.activation(
                                                  out=zs[:, hd * TS + hf * 512:
                                                          hd * TS + hf * 512 + 512],
                                                  in_=pp[:, :], func=AF.Silu),
                                              waits=[("pe", mmc)], inc="act")
                                ev_ft[ct] = p.emit("vector", lambda v: None,
                                                   waits=[("act", c_si)], inc="dve")
                            else:
                                ev_ft[ct] = p.emit("vector", lambda v, hf=hf: v.tensor_copy(
                                    ba[0:8, hf * 512:hf * 512 + 512], ps2[0:8, :]),
                                    waits=[("pe", mmc)] +
                                    ([("act", prevs["be"]), prevs["b1"]]
                                     if "be" in prevs else []), inc="dve")
                    p1_done = p.v("dve")

                    if PH < 2:
                        continue
                    c_b1 = p.dma("sync", lambda sy: sy.dma_start(
                        out=rowd[0:1, :].rearrange("a (p t) -> (a p) t", p=4),
                        in_=ba[4:8, :]), "rb", 1,
                        waits=[("dve", p1_done)] + ([prevs["rl0"]] if "rl0" in prevs
                                                    else []))
                    c_b2 = p.dma("sync", lambda sy: sy.dma_start(
                        out=gr[:4, :], in_=rowd[0:1, :].rearrange(
                            "a (p t) -> (a p) t", p=4)), "rb2", 1,
                        waits=[c_b1] + ([("dve", prevs["rows_dve"])]
                                       if "rows_dve" in prevs else []))
                    c_be = p.emit("scalar", lambda a: a.activation(
                        out=br[:4, :], in_=ba[0:4, :], func=AF.Sigmoid),
                        waits=[("dve", p1_done)] +
                        ([prevs["st7"], ("dve", prevs["rows_dve"])]
                         if "st7" in prevs else []), inc="act")
                    c_lb = p.emit("scalar", lambda a: a.activation(
                        out=r1[:4, :], in_=br[:4, :], func=AF.Ln),
                        waits=[prevs["st6"]] if "st6" in prevs else [], inc="act")
                    c_ga = p.emit("vector", lambda v: v.tensor_scalar(
                        gr[:4, :], gr[:4, :], dtb[0:4, 0:1], None, ALU.add),
                        waits=[c_b2, dtb_ld], inc="dve")
                    c_ab = p.emit("scalar", lambda a: a.activation(
                        out=r2[:4, :], in_=gr[:4, :], func=AF.Abs),
                        waits=[("dve", c_ga)] + ([prevs["st4"]] if "st4" in prevs
                                                 else []), inc="act")
                    c_ex = p.emit("scalar", lambda a: a.activation(
                        out=r2[:4, :], in_=r2[:4, :], func=AF.Exp, scale=-1.0), inc="act")
                    c_ln = p.emit("scalar", lambda a: a.activation(
                        out=r2[:4, :], in_=r2[:4, :], func=AF.Ln, bias=one4[0:4, 0:1]),
                        inc="act")
                    p.emit("vector", lambda v: v.tensor_scalar(
                        r3[:4, :], gr[:4, :], 0.0, None, ALU.max),
                        waits=[("act", c_ln)] + ([prevs["sb1b"], prevs["st2"]]
                                                 if "sb1b" in prevs else []))
                    p.emit("vector", lambda v: v.tensor_tensor(
                        r3[:4, :], r3[:4, :], r2[:4, :], ALU.add))
                    c_g = p.emit("vector", lambda v: v.tensor_scalar(
                        gr[:4, :], r3[:4, :], nalog[0:4, 0:1], None, ALU.mult),
                        waits=[("dve", c_na)], inc="dve")
                    p.emit("vector", lambda v: v.tensor_scalar(
                        r2[:4, 0:64], gr[:4, 0:64], 0.0, None, ALU.mult),
                        waits=[("dve", c_g)])
                    for ch in range(16):
                        p.emit("vector", lambda v, ch=ch: v.tensor_tensor_scan(
                            gc[:4, ch * 64:(ch + 1) * 64], gr[:4, ch * 64:(ch + 1) * 64],
                            r2[:4, 0:64], 0.0, ALU.add, ALU.add),
                            waits=[prevs["sb0"]] if (ch == 0 and "sb0" in prevs) else [])
                    c_gc = p.emit("vector", lambda v: v.tensor_copy(gc[:4, 0:1], gc[:4, 0:1]),
                                  inc="dve")
                    c_k1 = p.emit("scalar", lambda a: a.activation(
                        out=lnk[:4, :], in_=s2k4[:4, :], func=AF.Ln, bias=e6c[0:4, 0:1]),
                        waits=[("dve", p1_done)], inc="act")
                    c_q1 = p.emit("scalar", lambda a: a.activation(
                        out=lnq[:4, :], in_=s2q4[:4, :], func=AF.Ln, bias=e6c[0:4, 0:1]),
                        inc="act")
                    p.emit("vector", lambda v: v.tensor_scalar(
                        lnk[:4, :], lnk[:4, :], -0.5, None, ALU.mult),
                        waits=[("act", c_q1)])
                    c_lk = p.emit("vector", lambda v: v.tensor_scalar(
                        lnq[:4, :], lnq[:4, :], -0.5, None, ALU.mult), inc="dve")
                    if s == 0:
                        p.emit("gpsimd", lambda g: g.memset(l2c[:4, 0:1], 2.0), gpser=True)
                        c_l2a = p.emit("scalar", lambda a: a.activation(
                            out=l2c[:4, 1:2], in_=l2c[:4, 0:1], func=AF.Ln),
                            waits=[("gp", p.v("gp"))], inc="act")
                        c_l2b = p.emit("vector", lambda v: v.tensor_scalar(
                            l2c[:4, 1:2], l2c[:4, 1:2], -3.5, None, ALU.mult),
                            waits=[("act", c_l2a)], inc="dve")
                    p.emit("vector", lambda v: v.tensor_tensor(
                        r2[:4, :], gc[:4, :], lnk[:4, :], ALU.add),
                        waits=[("dve", c_lk), ("dve", c_gc), ("act", c_lb)])
                    c_p1 = p.emit("vector", lambda v: v.tensor_tensor(
                        r2[:4, :], r2[:4, :], r1[:4, :], ALU.add), inc="dve")
                    c_p2 = p.emit("vector", lambda v: v.scalar_tensor_tensor(
                        r3[:4, :], gc[:4, :], -1.0, lnk[:4, :], ALU.mult, ALU.add),
                        inc="dve")
                    p.emit("vector", lambda v: v.tensor_tensor(
                        r4[:4, :], gc[:4, :], lnq[:4, :], ALU.add),
                        waits=[prevs["st5"], prevs["sb2"]] if "st5" in prevs else [])
                    c_p3 = p.emit("vector", lambda v: v.tensor_scalar(
                        r4[:4, :], r4[:4, :], l2c[0:4, 1:2], None, ALU.add),
                        inc="dve")
                    for ch in range(16):
                        p.emit("vector", lambda v, ch=ch: v.tensor_scalar(
                            r1[:4, ch * 64:(ch + 1) * 64], gc[:4, ch * 64:(ch + 1) * 64],
                            gc[0:4, ch * 64 + 63:ch * 64 + 64], -1.0,
                            ALU.subtract, ALU.mult),
                            waits=[("dve", c_p1)] if ch == 0 else [])
                    c_kd = p.emit("vector", lambda v: v.tensor_tensor(
                        r1[:4, :], r1[:4, :], lnk[:4, :], ALU.add), inc="dve")
                    sb2 = p.dma("scalar", lambda sq: sq.dma_start(
                        out=rowd[2:3, :].rearrange("a (p t) -> (a p) t", p=4),
                        in_=r4[:4, :]), "rb4", 1,
                        waits=[("dve", c_p3)] + ([prevs["rl_p3"]] if "rl_p3" in prevs
                                                 else []))
                    sb1b = p.dma("scalar", lambda sq: sq.dma_start(
                        out=rowd[1:2, :].rearrange("a (p t) -> (a p) t", p=4),
                        in_=r3[:4, :]), "rb5", 1,
                        waits=[("dve", c_p2)] + ([prevs["rl_p2"]] if "rl_p2" in prevs
                                                 else []))
                    sb0 = p.dma("scalar", lambda sq: sq.dma_start(
                        out=rowd[0:1, :].rearrange("a (p t) -> (a p) t", p=4),
                        in_=gc[:4, :]), "rb6", 1,
                        waits=[("dve", c_gc), c_b2])
                    rl0 = p.dma("scalar", lambda sq: sq.dma_start(
                        out=b1r[:1, :], in_=rowd[0:1, :]), "rb7", 1,
                        waits=[sb0] + ([("pe", prev_k1)] if prev_k1 is not None else []))
                    st1 = p.dma("sync", lambda sy: sy.dma_start(
                        out=pkgd[0:4, :], in_=r2[:4, :]), "pk", 8, waits=[("dve", c_p1)])
                    st2 = p.dma("sync", lambda sy: sy.dma_start(
                        out=pkgd[4:8, :], in_=r3[:4, :]), "pk", 8, waits=[("dve", c_p2)])
                    st3 = p.dma("sync", lambda sy: sy.dma_start(
                        out=pkgd[8:12, :], in_=r4[:4, :]), "pk", 8, waits=[("dve", c_p3)])
                    c_e1 = p.emit("scalar", lambda a: a.activation(
                        out=r2[:4, :], in_=r2[:4, :], func=AF.Exp), waits=[st1], inc="act")
                    st4 = p.dma("sync", lambda sy: sy.dma_start(
                        out=pkgd[12:16, :], in_=r2[:4, :]), "pk", 8, waits=[("act", c_e1)])
                    c_e2 = p.emit("scalar", lambda a: a.activation(
                        out=r4[:4, :], in_=r4[:4, :], func=AF.Exp), waits=[st3, sb2],
                        inc="act")
                    st5 = p.dma("sync", lambda sy: sy.dma_start(
                        out=pkgd[16:20, :], in_=r4[:4, :]), "pk", 8, waits=[("act", c_e2)])
                    c_e3 = p.emit("scalar", lambda a: a.activation(
                        out=r1[:4, :], in_=r1[:4, :], func=AF.Exp),
                        waits=[("dve", c_kd)], inc="act")
                    st6 = p.dma("sync", lambda sy: sy.dma_start(
                        out=pkgd[20:24, :], in_=r1[:4, :]), "pk", 8, waits=[("act", c_e3)])
                    st7 = p.dma("sync", lambda sy: sy.dma_start(
                        out=pkgd[24:28, :], in_=br[:4, :]), "pk", 8, waits=[("act", c_be)])
                    for v_i in range(4):
                        rhs = bass.AP(b1r, v_i * TS + 63, [[4 * TS, 1], [64, 16]])
                        c_eg = p.emit("tensor", lambda t, rhs=rhs, v_i=v_i: t.matmul(
                            ps7[:, 16 * v_i:16 * v_i + 16], oner[:1, :], rhs,
                            start=True, stop=True),
                            waits=([rl0, ("dve", p.v("dve"))] +
                                   ([("act", prevs["ege"])] if "ege" in prevs else []))
                            if v_i == 0 else [],
                            inc="pe")
                    c_ege = p.emit("scalar", lambda a: a.activation(
                        out=egls[:, :], in_=ps7[:, 0:64], func=AF.Exp),
                        waits=[("pe", c_eg)], inc="act")
                    prevs["ege"] = c_ege
                    pk_ld = p.dma("sync", lambda sy: sy.dma_start(
                        out=pkg28[:, :], in_=pkgd[:, :]), "pk2", 1,
                        waits=[st1, st2, st3, st4, st5, st6, st7] +
                        ([("dve", prev_pkgT)] if prev_pkgT is not None else []))
                    prevs.update(st2=st2, st4=st4, st5=st5, st6=st6, st7=st7,
                                 sb0=sb0, sb1b=sb1b, sb2=sb2, rl0=rl0,
                                 q1=c_q1, be=c_be, b1=c_b1, rows_dve=p.v("dve"))

                    for g in range(NG):
                        gb = g * 128
                        if PH < 31:
                            continue
                        w_t = [pk_ld, id_ld]
                        if prev_pkgT is not None:
                            w_t.append(("dve", prev_pkgT))
                        if grp_ps4_free is not None:
                            w_t.append(("dve", grp_ps4_free))
                        c_pt = p.emit("tensor", lambda t, gb=gb: t.transpose(
                            ps4[:, 0:28], pkg28[:, gb:gb + 128], idnf[0:28, 0:28]),
                            waits=w_t + [("dve", c_idf)], inc="pe")
                        c_ct = p.emit("vector", lambda v: v.tensor_copy(
                            colsb[:, :], ps4[:, 0:28]), waits=[("pe", c_pt)], inc="dve")
                        prev_pkgT = c_ct
                        if PH < 32:
                            continue

                        items = []
                        for (name, src, n_h, dstt) in [("k", ks, 2, ktm),
                                                       ("v", vs, 4, vtm), ("z", zs, 4, ztm)]:
                            for hd in range(n_h):
                                items.append((src, hd, dstt))
                        for b0 in (0, 8):
                            batch = items[b0:b0 + 8]
                            for bi, (src, hd, dstt) in enumerate(batch):
                                slot = bi
                                w_tr = []
                                if bi == 0:
                                    w_tr.append(("dve", max(
                                        [v for v in trev.values() if v is not None]
                                        + [p.v("dve") if b0 else 0])))
                                    if tr_i[0] == 0:
                                        w_tr.append(id_ld)
                                tr_i[0] += 1
                                c_tr = p.emit("tensor", lambda t, src=src, hd=hd, gb=gb,
                                              slot=slot: t.transpose(
                                                  ps6[:, slot * 128:slot * 128 + 128],
                                                  src[:, hd * TS + gb:hd * TS + gb + 128],
                                                  idnb[:, :]), waits=w_tr, inc="pe")
                            pe_b = p.v("pe")
                            for bi, (src, hd, dstt) in enumerate(batch):
                                slot = bi
                                c_te = p.emit("vector", lambda v, dstt=dstt, hd=hd,
                                              slot=slot: v.tensor_copy(
                                                  dstt[:, hd * 128:hd * 128 + 128],
                                                  ps6[:, slot * 128:slot * 128 + 128]),
                                              waits=[("pe", pe_b)] if bi == 0 else [],
                                              inc="dve")
                                trev[slot] = c_te
                        tr_done = p.v("dve")
                        if PH < 33:
                            continue

                        w_g = [("dve", grp_ps01_free)] if grp_ps01_free else []
                        for v_i in range(4):
                            kh = v_i // 2
                            p.emit("tensor", lambda t, v_i=v_i, kh=kh, gb=gb: t.matmul(
                                ps0[:, v_i * 128:v_i * 128 + 128],
                                ks[:, kh * TS + gb:kh * TS + gb + 128],
                                ks[:, kh * TS + gb:kh * TS + gb + 128],
                                start=True, stop=True), waits=w_g if v_i == 0 else [],
                                inc="pe")
                            p.emit("tensor", lambda t, v_i=v_i, kh=kh, gb=gb: t.matmul(
                                ps1[:, v_i * 128:v_i * 128 + 128],
                                ks[:, kh * TS + gb:kh * TS + gb + 128],
                                qs[:, kh * TS + gb:kh * TS + gb + 128],
                                start=True, stop=True), inc="pe")
                        gram_pe = p.v("pe")
                        if PH < 40:
                            continue

                        w_rlA = [sb1b, ("act", c_ege)]
                        if prev_k1 is not None:
                            w_rlA.append(("pe", prev_k1))
                        rl_p2 = p.dma("scalar", lambda sq: sq.dma_start(
                            out=b1r[:1, :], in_=rowd[1:2, :]), "rbp", 2, waits=w_rlA)
                        prevs["rl_p2"] = rl_p2
                        w_k1 = [rl_p2, ("dve", c_ct)]
                        if grp_ps4_free is not None:
                            w_k1.append(("dve", grp_ps4_free))
                        rhsA = bass.AP(b1r, gb, [[4 * TS, 1], [TS, 4], [1, 128]])
                        c_k1a = p.emit("tensor", lambda t, rhsA=rhsA: t.matmul(
                            ps4[:, :], oner[:1, :], rhsA, start=True, stop=True),
                            waits=w_k1, inc="pe")
                        a_done = [None] * 4
                        for v_i in range(4):
                            c_e = p.emit("vector", lambda v, v_i=v_i: v.tensor_scalar(
                                cs[:, 128:256], ps4[:, v_i * 128:v_i * 128 + 128],
                                colsb[:, v_i:v_i + 1], 20.0, ALU.add, ALU.min),
                                waits=[("pe", c_k1a)], inc="dve")
                            c_da = p.emit("scalar", lambda a: a.activation(
                                out=cs[:, 128:256], in_=cs[:, 128:256], func=AF.Exp),
                                waits=[("dve", c_e)], inc="act")
                            p.emit("vector", lambda v, v_i=v_i: v.tensor_tensor(
                                cs[:, 0:128], ps0[:, v_i * 128:v_i * 128 + 128],
                                cs[:, 128:256], ALU.mult),
                                waits=[("act", c_da), ("pe", gram_pe)] +
                                ([mk_ld] if (s, g, v_i) == (0, 0, 0) else []))
                            c_A = p.emit("vector", lambda v, v_i=v_i: v.tensor_tensor(
                                asb[:, v_i * 128:v_i * 128 + 128], cs[:, 0:128],
                                msl[:, :], ALU.mult), inc="dve")
                            c_at = p.emit("tensor", lambda t, v_i=v_i: t.transpose(
                                ps6[:, v_i * 128:v_i * 128 + 128],
                                asb[:, v_i * 128:v_i * 128 + 128], idnb[:, :]),
                                waits=[("dve", c_A), ("dve", trev_all(trev))], inc="pe")
                            a_done[v_i] = p.emit("vector", lambda v, v_i=v_i: v.tensor_copy(
                                atsb[:, v_i * 128:v_i * 128 + 128],
                                ps6[:, v_i * 128:v_i * 128 + 128]),
                                waits=[("pe", c_at)], inc="dve")
                            trev[v_i] = a_done[v_i]
                        rl_p3 = p.dma("scalar", lambda sq: sq.dma_start(
                            out=b1r[:1, :], in_=rowd[2:3, :]), "rbp", 2,
                            waits=[sb2, ("pe", c_k1a)])
                        prevs["rl_p3"] = rl_p3
                        rhsQ = bass.AP(b1r, gb, [[4 * TS, 1], [TS, 4], [1, 128]])
                        c_k1b = p.emit("tensor", lambda t, rhsQ=rhsQ: t.matmul(
                            ps4[:, :], oner[:1, :], rhsQ, start=True, stop=True),
                            waits=[rl_p3, ("dve", p.v("dve"))], inc="pe")
                        prev_k1 = c_k1b
                        ant_done = [None] * 4
                        for v_i in range(4):
                            c_e = p.emit("vector", lambda v, v_i=v_i: v.tensor_scalar(
                                cs[:, 128:256], ps4[:, v_i * 128:v_i * 128 + 128],
                                colsb[:, 4 + v_i:5 + v_i], 20.0, ALU.add, ALU.min),
                                waits=[("pe", c_k1b)], inc="dve")
                            c_da = p.emit("scalar", lambda a: a.activation(
                                out=cs[:, 128:256], in_=cs[:, 128:256], func=AF.Exp),
                                waits=[("dve", c_e)], inc="act")
                            p.emit("vector", lambda v, v_i=v_i: v.tensor_tensor(
                                cs[:, 0:128], ps1[:, v_i * 128:v_i * 128 + 128],
                                cs[:, 128:256], ALU.mult),
                                waits=[("act", c_da)] +
                                ([mk2_ld] if (s, g, v_i) == (0, 0, 0) else []))
                            ant_done[v_i] = p.emit("vector", lambda v, v_i=v_i:
                                                   v.tensor_tensor(
                                                       antb[:, v_i * 128:v_i * 128 + 128],
                                                       cs[:, 0:128], mui[:, :], ALU.mult),
                                                   inc="dve")
                        dec_done = p.v("dve")
                        if PH < 50:
                            continue

                        for v_i in range(4):
                            p.emit("vector", lambda v, v_i=v_i: v.tensor_copy(
                                ttf[:, v_i * 128:v_i * 128 + 128],
                                atsb[:, v_i * 128:v_i * 128 + 128]),
                                waits=[("dve", dec_done)] if v_i == 0 else [])
                            p.emit("vector", lambda v, v_i=v_i: v.tensor_tensor(
                                ttf[:, v_i * 128:v_i * 128 + 128],
                                ttf[:, v_i * 128:v_i * 128 + 128], idnf[:, :], ALU.add))
                        c_tb0 = p.emit("vector", lambda v: v.tensor_copy(
                            ttsb[:, :], ttf[:, :]), inc="dve")
                        for it in range(5):
                            for v_i in range(4):
                                w_sq = []
                                if v_i == 0:
                                    w_sq.append(("dve", p.v("dve")))
                                    if grp_ps3_free is not None and it == 0:
                                        w_sq.append(("dve", grp_ps3_free))
                                p.emit("tensor", lambda t, v_i=v_i: t.matmul(
                                    ps1[:, v_i * 128:v_i * 128 + 128],
                                    atsb[:, v_i * 128:v_i * 128 + 128],
                                    asb[:, v_i * 128:v_i * 128 + 128],
                                    start=True, stop=True), waits=w_sq, inc="pe")
                                p.emit("tensor", lambda t, v_i=v_i: t.matmul(
                                    ps2[:, v_i * 128:v_i * 128 + 128],
                                    asb[:, v_i * 128:v_i * 128 + 128],
                                    atsb[:, v_i * 128:v_i * 128 + 128],
                                    start=True, stop=True), inc="pe")
                                p.emit("tensor", lambda t, v_i=v_i: t.matmul(
                                    ps3[:, v_i * 128:v_i * 128 + 128],
                                    asb[:, v_i * 128:v_i * 128 + 128],
                                    ttsb[:, v_i * 128:v_i * 128 + 128],
                                    start=True, stop=True), inc="pe")
                            sq_pe = p.v("pe")
                            p.emit("vector", lambda v: v.tensor_tensor(
                                ttf[:, :], ttf[:, :], ps3[:, :], ALU.add),
                                waits=[("pe", sq_pe)])
                            c_b2e = p.emit("vector", lambda v: v.tensor_copy(
                                asb[:, :], ps1[:, :]), inc="dve")
                            c_bt2e = p.emit("vector", lambda v: v.tensor_copy(
                                atsb[:, :], ps2[:, :]), inc="dve")
                            p.emit("vector", lambda v: v.tensor_copy(
                                ttsb[:, :], ttf[:, :]), inc="dve")
                        c_ttf = p.v("dve")
                        grp_ps3_free = c_ttf
                        grp_ps01_free = c_ttf
                        if PH < 60:
                            continue

                        uw_done = [None] * 4
                        kdt_done = [None] * 4
                        for v_i in range(4):
                            kh = v_i // 2
                            p.emit("vector", lambda v, v_i=v_i: v.tensor_scalar(
                                xsb[:, 0:128], vtm[:, v_i * 128:v_i * 128 + 128],
                                colsb[:, 24 + v_i:25 + v_i], None, ALU.mult),
                                waits=[("dve", tr_done), ("dve", c_ttf)])
                            c_x = p.emit("vector", lambda v, v_i=v_i, kh=kh: v.tensor_scalar(
                                xsb[:, 128:256], ktm[:, kh * 128:kh * 128 + 128],
                                colsb[:, 12 + v_i:13 + v_i], None, ALU.mult), inc="dve")
                            uo = (v_i % 2) * 256
                            c_uw = p.emit("tensor", lambda t, v_i=v_i, uo=uo: t.matmul(
                                ps4[:, uo:uo + 256], ttsb[:, v_i * 128:v_i * 128 + 128],
                                xsb[:, :], start=True, stop=True),
                                waits=[("dve", c_x)], inc="pe")
                            c_u = p.emit("vector", lambda v, v_i=v_i, uo=uo: v.tensor_copy(
                                usb[:, v_i * 128:v_i * 128 + 128], ps4[:, uo:uo + 128]),
                                waits=[("pe", c_uw)], inc="dve")
                            c_w = p.emit("vector", lambda v, uo=uo: v.tensor_copy(
                                xsb[:, 0:128], ps4[:, uo + 128:uo + 256]), inc="dve")
                            c_wt = p.emit("tensor", lambda t, v_i=v_i: t.transpose(
                                ps6[:, v_i * 128:v_i * 128 + 128], xsb[:, 0:128], idnb[:, :]),
                                waits=[("dve", c_w), ("dve", trev_all(trev))], inc="pe")
                            c_wf = p.emit("vector", lambda v, v_i=v_i: v.tensor_copy(
                                wfm[:, v_i * 128:v_i * 128 + 128],
                                ps6[:, v_i * 128:v_i * 128 + 128]),
                                waits=[("pe", c_wt)], inc="dve")
                            trev[v_i] = c_wf
                            uw_done[v_i] = c_wf
                            kdt_done[v_i] = p.emit("vector", lambda v, v_i=v_i, kh=kh:
                                                   v.tensor_scalar(
                                kdt4[:, v_i * 128:v_i * 128 + 128],
                                ktm[:, kh * 128:kh * 128 + 128],
                                colsb[:, 20 + v_i:21 + v_i], None, ALU.mult), inc="dve")
                        grp_ps4_free = p.v("dve")
                        pb = [ps5, ps2, ps3, ps7]
                        for c2 in range(2):
                            chg = g * 2 + c2
                            t0c = c2 * 64
                            for v_i in range(4):
                                p.emit("vector", lambda v, v_i=v_i, t0c=t0c: v.tensor_copy(
                                    kdc4[0:64, v_i * 128:v_i * 128 + 128],
                                    kdt4[t0c:t0c + 64, v_i * 128:v_i * 128 + 128]),
                                    waits=[("dve", kdt_done[3])] if (v_i, c2) == (0, 0)
                                    else [])
                                p.emit("vector", lambda v, v_i=v_i, t0c=t0c: v.tensor_copy(
                                    antc4[0:64, v_i * 64:v_i * 64 + 64],
                                    antb[t0c:t0c + 64,
                                         v_i * 128 + t0c:v_i * 128 + t0c + 64]),
                                    waits=[("dve", ant_done[v_i])] if c2 == 0 else [])
                                p.emit("vector", lambda v, v_i=v_i, t0c=t0c: v.tensor_copy(
                                    qec4[0:64, v_i:v_i + 1],
                                    colsb[t0c:t0c + 64, 16 + v_i:17 + v_i]), inc="dve")
                            pre_dve = p.v("dve")
                            c_m1 = [None] * 4
                            for v_i in range(4):
                                c_m1[v_i] = p.emit("tensor", lambda t, v_i=v_i, t0c=t0c:
                                                   t.matmul(
                                    pb[v_i][0:64, 0:128],
                                    wfm[:, v_i * 128 + t0c:v_i * 128 + t0c + 64],
                                    Sb[:, v_i * 128:v_i * 128 + 128],
                                    start=True, stop=True),
                                    waits=[("dve", pre_dve)] if v_i == 0 else [],
                                    inc="pe")
                            c_vb = [None] * 4
                            for v_i in range(4):
                                vnf = (4 + v_i) * 516 + 3
                                p.emit("vector", lambda v, v_i=v_i, t0c=t0c, vnf=vnf:
                                       v.tensor_tensor(
                                    mr[0:64, vnf:vnf + 128],
                                    usb[t0c:t0c + 64, v_i * 128:v_i * 128 + 128],
                                    pb[v_i][0:64, 0:128], ALU.subtract),
                                    waits=[("pe", c_m1[v_i])])
                                c_vb[v_i] = p.emit("vector", lambda v, v_i=v_i, vnf=vnf:
                                                   v.tensor_copy(
                                    vnb4[0:64, v_i * 128:v_i * 128 + 128],
                                    mr[0:64, vnf:vnf + 128]), inc="dve")
                            c_m2 = [None] * 4
                            for v_i in range(4):
                                c_m2[v_i] = p.emit("tensor", lambda t, v_i=v_i: t.matmul(
                                    pb[v_i][0:64, 128:256],
                                    antc4[0:64, v_i * 64:v_i * 64 + 64],
                                    vnb4[0:64, v_i * 128:v_i * 128 + 128],
                                    start=True, stop=True),
                                    waits=[("dve", c_vb[v_i])], inc="pe")
                            c_av = [None] * 4
                            for v_i in range(4):
                                avn = (4 + v_i) * 516 + 3 + 128
                                c_av[v_i] = p.emit("vector", lambda v, v_i=v_i, avn=avn:
                                                   v.tensor_copy(
                                    mr[0:64, avn:avn + 128], pb[v_i][0:64, 128:256]),
                                    waits=[("pe", c_m2[v_i])], inc="dve")
                            c_m3 = [None] * 4
                            for v_i in range(4):
                                kh = v_i // 2
                                c_m3[v_i] = p.emit("tensor", lambda t, v_i=v_i, kh=kh,
                                                   gb=gb, t0c=t0c: t.matmul(
                                    pb[v_i][0:64, 256:384],
                                    qs[:, kh * TS + gb + t0c:kh * TS + gb + t0c + 64],
                                    Sb[:, v_i * 128:v_i * 128 + 128],
                                    start=True, stop=True),
                                    waits=[("dve", c_av[v_i])], inc="pe")
                            c_oc = [None] * 4
                            for v_i in range(4):
                                vnf = (4 + v_i) * 516 + 3
                                c_oc[v_i] = p.emit("vector", lambda v, v_i=v_i, vnf=vnf:
                                                   v.scalar_tensor_tensor(
                                    mr[0:64, vnf + 256:vnf + 384],
                                    pb[v_i][0:64, 256:384],
                                    qec4[0:64, v_i:v_i + 1],
                                    mr[0:64, vnf + 128:vnf + 256], ALU.mult, ALU.add),
                                    waits=[("pe", c_m3[v_i])], inc="dve")
                            c_m4 = [None] * 4
                            for v_i in range(4):
                                c_m4[v_i] = p.emit("tensor", lambda t, v_i=v_i: t.matmul(
                                    pb[v_i][:, 384:512],
                                    kdc4[0:64, v_i * 128:v_i * 128 + 128],
                                    vnb4[0:64, v_i * 128:v_i * 128 + 128],
                                    start=True, stop=True),
                                    waits=[("dve", c_oc[v_i])], inc="pe")
                            for v_i in range(4):
                                p.emit("vector", lambda v, v_i=v_i, chg=chg:
                                       v.scalar_tensor_tensor(
                                    Sf[:, v_i * 128:v_i * 128 + 128],
                                    Sf[:, v_i * 128:v_i * 128 + 128],
                                    egls[:, v_i * 16 + chg:v_i * 16 + chg + 1],
                                    pb[v_i][:, 384:512], ALU.mult, ALU.add),
                                    waits=[("pe", c_m4[v_i])], inc="dve")
                                p.emit("vector", lambda v, v_i=v_i: v.tensor_copy(
                                    Sb[:, v_i * 128:v_i * 128 + 128],
                                    Sf[:, v_i * 128:v_i * 128 + 128]), inc="dve")
                            if PH < 70:
                                continue
                            c_r2 = [None] * 4
                            for v_i in range(4):
                                vnf = (4 + v_i) * 516 + 3
                                p.emit("scalar", lambda a, v_i=v_i, vnf=vnf: a.activation(
                                    out=cs[0:64, 0:128],
                                    in_=mr[0:64, vnf + 256:vnf + 384], func=AF.Square,
                                    accum_out=sscol[0:64, 2 * v_i:2 * v_i + 1]),
                                    waits=[("dve", c_oc[v_i])], inc="act")
                                p.emit("scalar", lambda a, v_i=v_i: a.activation(
                                    out=sscol[0:64, 2 * v_i + 1:2 * v_i + 2],
                                    in_=sscol[0:64, 2 * v_i:2 * v_i + 1], func=AF.Ln,
                                    scale=1.0 / 128.0, bias=e6c[0:64, 0:1]), inc="act")
                                c_r2[v_i] = p.emit("scalar", lambda a, v_i=v_i:
                                                   a.activation(
                                    out=sscol[0:64, 2 * v_i + 1:2 * v_i + 2],
                                    in_=sscol[0:64, 2 * v_i + 1:2 * v_i + 2], func=AF.Exp,
                                    scale=-0.5), inc="act")
                            for v_i in range(4):
                                vnf = (4 + v_i) * 516 + 3
                                occ = vnf + 256
                                p.emit("vector", lambda v, v_i=v_i, occ=occ:
                                       v.tensor_scalar(
                                    mr[0:64, occ:occ + 128], mr[0:64, occ:occ + 128],
                                    sscol[0:64, 2 * v_i + 1:2 * v_i + 2], None,
                                    ALU.mult), waits=[("act", c_r2[v_i])])
                                p.emit("vector", lambda v, occ=occ: v.tensor_tensor(
                                    mr[0:64, occ:occ + 128], mr[0:64, occ:occ + 128],
                                    nwbc[0:64, :], ALU.mult),
                                    waits=[nw_ld] if (s, g, v_i, c2) == (0, 0, 0, 0)
                                    else [])
                                w_gt = [("pe", c_m4[v_i])]
                                if A2A_MODE and c2 == 0 and prev_pe_gt is not None:
                                    w_gt.append(("pe", prev_pe_gt))
                                if (not A2A_MODE) and (v_i, c2) == (0, 0) and \
                                        (s * 8 + g - 1) >= 0:
                                    w_gt.append(st_gout[(s * 8 + g - 1) % 16])
                                p.emit("vector", lambda v, v_i=v_i, t0c=t0c: v.tensor_copy(
                                    vnb4[0:64, v_i * 128:v_i * 128 + 128],
                                    ztm[t0c:t0c + 64, v_i * 128:v_i * 128 + 128]),
                                    waits=w_gt)
                                p.emit("vector", lambda v, v_i=v_i, t0c=t0c, occ=occ:
                                       v.tensor_tensor(
                                    goutb[t0c:t0c + 64, v_i * 128:v_i * 128 + 128],
                                    mr[0:64, occ:occ + 128],
                                    vnb4[0:64, v_i * 128:v_i * 128 + 128],
                                    ALU.mult), inc="dve")
                        if A2A_MODE:
                            for ft in range(4):
                                p.emit("tensor", lambda t, ft=ft: t.transpose(
                                    ps7b[:, ft * 128:(ft + 1) * 128],
                                    goutb[:, ft * 128:(ft + 1) * 128], idnb[:, :]),
                                    waits=[("dve", p.v("dve"))] if ft == 0 else [],
                                    inc="pe")
                            prev_pe_gt = p.v("pe")
                            for ft in range(4):
                                w_ge = [("pe", prev_pe_gt)] if ft == 0 else []
                                if ft == 0 and (s * 8 + g - 1) >= 0:
                                    w_ge.append(st_gout[(s * 8 + g - 1) % 16])
                                c_gte = p.emit("vector", lambda v, ft=ft: v.tensor_copy(
                                    goutT[:, ft * 128:(ft + 1) * 128],
                                    ps7b[:, ft * 128:(ft + 1) * 128]),
                                    waits=w_ge, inc="dve")
                            oap = bass.AP(a2a_in_h, (512 * s) * 1024 + g * 128,
                                          [[1024, 128], [128 * 1024, 4], [1, 128]])
                            st_gout[(s * 8 + g) % 16] = p.dma(
                                "sync", lambda sy, oap=oap: sy.dma_start(
                                    out=oap, in_=goutT[:, 0:512]),
                                "go", 1, waits=[("dve", p.v("dve"))])
                        else:
                            oap = bass.AP(nc_gout_handle, (s * TS + g * 128) * 512,
                                          [[512, 128], [1, 512]])
                            st_gout[(s * 8 + g) % 16] = p.dma(
                                "sync", lambda sy, oap=oap: sy.dma_start(
                                    out=oap, in_=goutb[:, 0:512]),
                                "go", 1, waits=[("dve", p.v("dve"))])
                    seg_scan_done = p.v("dve")
        _phase1()

        allw = [(n, p.c[n]) for n in list(p.h.keys())
                if n not in ("m1", "m2", "m3")]
        p.emit("sync", lambda sy: None, waits=allw, inc="m1", inck=1)

        def a2afn(g):
            with allow_ici_all_to_all():
                return g.collective_compute(
                    "AllToAll", mybir.AluOpType.bypass, replica_groups=rg,
                    ins=[a2a_in_h.ap().opt()], outs=[a2a_out_h.ap().opt()])
        c_a2a = p.emit("gpsimd", a2afn, waits=[("go0", p.c["go0"])])
        p.arm_barrier([("m1", 1)])
        if nph < 2:
            dbgo = nc.dram_tensor("dbgo", [1, 1], F32, kind="ExternalOutput")
            sdb = p.dma("gpsimd", lambda g: g.dma_start(
                out=dbgo.ap()[:, :], in_=a2a_out_h.ap()[0:1, 0:1]), "db", 1,
                waits=[("gp", p.c["gp"])])
            allw = [(n, p.c[n]) for n in list(p.h.keys())]
            p.emit("sync", lambda sy: None, waits=allw)
            with nc.Block() as block:
                p.replay(block)
            return nc
        def _phase2():
            gT_h = a2a_out_h
            hT_h = nc.dram_tensor("hT", [DM, TO], F32, kind="ExternalInput")
            Wout_h = nc.dram_tensor("Wout", [4096, DM], F32, kind="ExternalInput")
            w1T_h = nc.dram_tensor("w1T", [DM, 5632], F32, kind="ExternalInput")
            w2T_h = nc.dram_tensor("w2T", [DM, 5632], F32, kind="ExternalInput")
            w3T_h = nc.dram_tensor("w3T", [5632, DM], F32, kind="ExternalInput")
            Wkv_h = nc.dram_tensor("Wkv", [DM, DM], F32, kind="ExternalInput")
            Wq_h = nc.dram_tensor("Wq", [DM, DM], F32, kind="ExternalInput")
            sc_h = nc.dram_tensor("sc", [128, TO], F32, kind="ExternalInput")
            x2T_o = x2T_i.ap()
            kroT_o = kag_in_h.ap()
            vT_o = vag_in_h.ap()
            qroT_o = qroT_i.ap()
            kind = "ExternalOutput" if dbg else "Internal"
            if dbg:
                hdbg = nc.dram_tensor("hdbg", [44 * 128, TO], BF16, kind="ExternalOutput").ap()
                thrd = nc.dram_tensor("thrd", [128, 8], F32, kind="ExternalOutput").ap()
            x1d = nc.dram_tensor("x1d", [DM, TO], F32, kind=kind).ap()
            x0d = nc.dram_tensor("x0d2", [DM, TO], F32, kind=kind).ap()
            WoutB_t = nc.dram_tensor("WoutB", [4096, DM], BF16)
            WkvB_t = nc.dram_tensor("WkvB", [DM, DM], BF16)
            WqB_t = nc.dram_tensor("WqB", [DM, DM], BF16)
            gT, hT = gT_h.ap(), hT_h.ap()
            w1T, w2T, w3T = w1T_h.ap(), w2T_h.ap(), w3T_h.ap()

            with ExitStack() as ctx:
                e = ctx.enter_context
                for (src_h, dst_t, nrow) in ((Wout_h, WoutB_t, 4096), (Wkv_h, WkvB_t, DM),
                                             (Wq_h, WqB_t, DM)):
                    for rt in range(nrow // 128):
                        p.dma("gpsimd", lambda g, src_h=src_h, dst_t=dst_t, rt=rt:
                              g.dma_start(out=dst_t.ap()[rt * 128:(rt + 1) * 128, :],
                                          in_=src_h.ap()[rt * 128:(rt + 1) * 128, :]),
                              "lw", 4)
                wcast_done = [(n, p.c[n]) for n in ("lw0", "lw1", "lw2", "lw3")]

                x1b = e(nc.sbuf_tensor("f2x1b", [128, 16 * TO], BF16))
                wstr = e(nc.sbuf_tensor("f2wstr", [128, 4 * 512], F32))
                tpos = e(nc.sbuf_tensor("f2tpos", [128, 512], F32))
                tneg = e(nc.sbuf_tensor("f2tneg", [128, 512], F32))
                rowm = e(nc.sbuf_tensor("f2rowm", [1, TO], F32))
                rowv = e(nc.sbuf_tensor("f2rowv", [1, TO], F32))
                rowr = e(nc.sbuf_tensor("f2rowr", [1, TO], F32))
                gcol = e(nc.sbuf_tensor("f2gcol", [128, 8], F32))
                thr = e(nc.sbuf_tensor("f2thr", [128, 8], F32))
                onec = e(nc.sbuf_tensor("f2onec", [128, 1], F32))
                onecb = e(nc.sbuf_tensor("f2onecb", [128, 1], BF16))
                epsc = e(nc.sbuf_tensor("f2epsc", [1, 1], F32))
                grow = e(nc.sbuf_tensor("f2grow", [1, 4 * 512], F32))
                oner = e(nc.sbuf_tensor("f2oner", [1, 128], F32))
                sqs = e(nc.sbuf_tensor("f2sqs", [128, 2 * 512], BF16))
                sig = e(nc.sbuf_tensor("f2sig", [128, 2 * 512], BF16))
                hidb = e(nc.sbuf_tensor("f2hidb", [128, 4 * 512], F32))
                x0buf = e(nc.sbuf_tensor("f2x0buf", [128, 4 * 512], F32))
                ps = [e(nc.psum_tensor(f"f2ps{i}", [128, 512], F32)) for i in range(4)]
                psr = [e(nc.psum_tensor(f"f2psr{i}", [1, 512], F32)) for i in range(4)]

                p.emit("gpsimd", lambda g: g.memset(onec[:, :], 1.0), gpser=True)
                p.emit("gpsimd", lambda g: g.memset(onecb[:, :], 1.0), gpser=True)
                p.emit("gpsimd", lambda g: g.memset(epsc[:1, :], 1e-5), gpser=True)
                p.emit("gpsimd", lambda g: g.memset(gcol[:, :], 0.0), gpser=True)
                p.emit("gpsimd", lambda g: g.memset(grow[:1, :], 0.0), gpser=True)
                p.emit("gpsimd", lambda g: g.memset(oner[:1, :], 1.0), gpser=True)
                p.emit("gpsimd", lambda g: g.memset(rowm[:1, 0:8], 0.0), gpser=True)
                init_gp = p.v("gp")

                plan_g = []
                for w_i, (dram, n_pt, n_ft) in enumerate([(w1T, 16, 11), (w2T, 16, 11),
                                                          (w3T, 44, 4)]):
                    for pt in range(n_pt):
                        for ft in range(n_ft):
                            plan_g.append((dram, pt, ft, w_i))
                red_done = [None] * len(plan_g)
                seen = set()
                for i, (dram, pt, ft, w_i) in enumerate(plan_g):
                    sl = i % 4
                    waits = [red_done[i - 4]] if i >= 4 else []
                    dc = p.dma("scalar", lambda sq, dram=dram, pt=pt, ft=ft, sl=sl: sq.dma_start(
                        out=wstr[:, sl * 512:(sl + 1) * 512],
                        in_=dram[pt * 128:(pt + 1) * 128, ft * 512:(ft + 1) * 512]),
                        "gw", 4, waits=waits)
                    eng = "vector"
                    semn = "dve" if eng == "vector" else "gp"
                    col = w_i if eng == "vector" else 3 + w_i
                    tcol = 6 if eng == "vector" else 7
                    first = (eng, w_i) not in seen
                    seen.add((eng, w_i))
                    if eng == "vector":
                        if first:
                            c = p.emit(eng, lambda ee, sl=sl, col=col: ee.tensor_reduce(
                                gcol[:, col:col + 1], wstr[:, sl * 512:(sl + 1) * 512],
                                AX.X, ALU.add, apply_absolute_value=True),
                                waits=[dc, ("gp", init_gp)], inc=semn)
                        else:
                            p.emit(eng, lambda ee, sl=sl, tcol=tcol: ee.tensor_reduce(
                                gcol[:, tcol:tcol + 1], wstr[:, sl * 512:(sl + 1) * 512],
                                AX.X, ALU.add, apply_absolute_value=True), waits=[dc])
                            c = p.emit(eng, lambda ee, col=col, tcol=tcol: ee.tensor_tensor(
                                gcol[:, col:col + 1], gcol[:, col:col + 1],
                                gcol[:, tcol:tcol + 1], ALU.add), inc=semn)
                    else:
                        p.emit(eng, lambda ee, sl=sl: ee.tensor_reduce(
                            grow[0:1, 3 * 512:3 * 512 + 512], wstr[:, sl * 512:(sl + 1) * 512],
                            AX.C, ALU.add, apply_absolute_value=True), waits=[dc],
                            gpser=True)
                        c = p.emit(eng, lambda ee, w_i=w_i: ee.tensor_tensor(
                            grow[0:1, w_i * 512:w_i * 512 + 512],
                            grow[0:1, w_i * 512:w_i * 512 + 512],
                            grow[0:1, 3 * 512:3 * 512 + 512], ALU.add), gpser=True)
                    red_done[i] = (semn, c)
                gam_gp0 = p.v("gp")
                for w_i in range(3):
                    p.emit("vector", lambda v, w_i=w_i: v.tensor_reduce(
                        gcol[0:1, 3 + w_i:4 + w_i], grow[0:1, w_i * 512:w_i * 512 + 512],
                        AX.X, ALU.add), waits=[("gp", gam_gp0)] if w_i == 0 else [],
                        inc="dve")
                gam_dve, gam_gp = p.v("dve"), p.v("gp")
                c_pe = p.emit("tensor", lambda t: t.matmul(psr[0][:1, 0:8], onec[:, :],
                                                           gcol[:, 0:8], start=True, stop=True),
                              waits=[("dve", gam_dve), ("gp", gam_gp)], inc="pe")
                p.emit("vector", lambda v: v.tensor_copy(rowv[:1, 0:8], psr[0][:1, 0:8]),
                       waits=[("pe", c_pe)])
                p.emit("vector", lambda v: v.tensor_tensor(rowm[:1, 0:3], rowv[:1, 0:3],
                                                           rowv[:1, 3:6], ALU.add))
                p.emit("vector", lambda v: v.tensor_scalar(rowm[:1, 0:3], rowm[:1, 0:3],
                                                           0.5 / (5632.0 * 2048.0), 0.5e-8,
                                                           ALU.mult, ALU.add))
                c_dve = p.emit("vector", lambda v: v.tensor_scalar(
                    rowm[:1, 4:7], rowm[:1, 0:3], -1.0, None, ALU.mult), inc="dve")
                c_pe = p.emit("tensor", lambda t: t.matmul(ps[0][:, 0:8], oner[:1, :],
                                                           rowm[:1, 0:8], start=True, stop=True),
                              waits=[("dve", c_dve)], inc="pe")
                thr_done = p.emit("vector", lambda v: v.tensor_copy(thr[:, 0:8], ps[0][:, 0:8]),
                                  waits=[("pe", c_pe)], inc="dve")

                stats_hist = []

                def stats_tile(i, ct, th, xsl, wait_ev, first_wait=None):
                    if ct == 0 and th == 0:
                        stats_hist.clear()
                    w_sq = [("dve", wait_ev)]
                    if i >= 2:
                        w_sq.append(("pe", stats_hist[i - 2]))
                    sq_c = p.emit("scalar", lambda a, xsl=xsl, i=i: a.activation(
                        out=sqs[:, (i % 2) * 512:(i % 2) * 512 + 512],
                        in_=x0buf[:, xsl:xsl + 512], func=AF.Square),
                        waits=w_sq, inc="act")

                    w_mm = [("act", sq_c)]
                    if i <= 1 and first_wait is not None:
                        w_mm.append(("dve", first_wait))
                    p.emit("tensor", lambda t, th=th, ct=ct, xsl=xsl: t.matmul(
                        psr[th][:1, :], onec[:, :], x0buf[:, xsl:xsl + 512],
                        start=(ct == 0), stop=(ct == 15)), waits=w_mm, inc="pe")
                    stats_hist.append(p.emit("tensor", lambda t, th=th, ct=ct, i=i: t.matmul(
                        psr[2 + th][:1, :], onecb[:, :],
                        sqs[:, (i % 2) * 512:(i % 2) * 512 + 512],
                        start=(ct == 0), stop=(ct == 15)), inc="pe"))
                    return p.dma("sync", lambda sy, ct=ct, th=th, xsl=xsl: sy.dma_start(
                        out=x0d[ct * 128:(ct + 1) * 128, th * 512:th * 512 + 512],
                        in_=x0buf[:, xsl:xsl + 512]),
                        "so", 4, waits=[("dve", wait_ev)])

                def ln_rows(eps):
                    stats_pe = p.v("pe")
                    p.emit("vector", lambda v: v.tensor_scalar(rowm[:1, 0:512], psr[0][:1, :],
                                                               1.0 / DM, None, ALU.mult),
                           waits=[("pe", stats_pe)])
                    p.emit("vector", lambda v: v.tensor_scalar(rowm[:1, 512:1024], psr[1][:1, :],
                                                               1.0 / DM, None, ALU.mult))
                    p.emit("vector", lambda v: v.tensor_scalar(rowv[:1, 0:512], psr[2][:1, :],
                                                               1.0 / DM, None, ALU.mult))
                    p.emit("vector", lambda v: v.tensor_scalar(rowv[:1, 512:1024], psr[3][:1, :],
                                                               1.0 / DM, None, ALU.mult))
                    p.emit("vector", lambda v: v.tensor_tensor(grow[0:1, 0:TO], rowm[:1, :],
                                                               rowm[:1, :], ALU.mult))
                    c_d = p.emit("vector", lambda v: v.tensor_tensor(rowv[:1, :], rowv[:1, :],
                                                                     grow[0:1, 0:TO], ALU.subtract),
                                 inc="dve")
                    p.emit("scalar", lambda a: a.activation(
                        out=rowr[:1, :], in_=rowv[:1, :], func=AF.Ln, bias=epsc[0:1, 0:1]),
                        waits=[("dve", c_d)])
                    return p.emit("scalar", lambda a: a.activation(
                        out=rowr[:1, :], in_=rowr[:1, :], func=AF.Exp, scale=-0.5),
                        inc="act"), c_d

                def ln_normalize(c_a, wait_st, dst_b16, out_dram):
                    prev = [None] * 32
                    last_dve = None
                    for th in range(2):
                        p.emit("tensor", lambda t, th=th: t.matmul(
                            ps[2][:, :], oner[:1, :], rowm[:1, th * 512:th * 512 + 512],
                            start=True, stop=True), waits=[("act", c_a)] +
                            ([("dve", last_dve)] if th == 1 else []), inc="pe")
                        c_bc = p.emit("tensor", lambda t, th=th: t.matmul(
                            ps[3][:, :], oner[:1, :], rowr[:1, th * 512:th * 512 + 512],
                            start=True, stop=True), inc="pe")
                        for ct in range(16):
                            i = th * 16 + ct
                            xsl = (i % 2) * 512
                            w_rl = list(wait_st) if i == 0 else []
                            if i >= 2:
                                w_rl += [("dve", prev[i - 2][0]), prev[i - 2][1]]
                            rl = p.dma("scalar", lambda sq, ct=ct, th=th, xsl=xsl: sq.dma_start(
                                out=x0buf[:, xsl:xsl + 512],
                                in_=x0d[ct * 128:(ct + 1) * 128, th * 512:th * 512 + 512]),
                                "rl", 2, waits=w_rl)
                            p.emit("vector", lambda v, xsl=xsl: v.tensor_tensor(
                                tpos[:, :], x0buf[:, xsl:xsl + 512], ps[2][:, :], ALU.subtract),
                                waits=[rl] + ([("pe", c_bc)] if ct == 0 else []))
                            p.emit("vector", lambda v, xsl=xsl: v.tensor_tensor(
                                x0buf[:, xsl:xsl + 512], tpos[:, :], ps[3][:, :], ALU.mult))
                            c_n = p.emit("vector", lambda v, ct=ct, th=th, xsl=xsl: v.tensor_copy(
                                dst_b16[:, ct * TO + th * 512:ct * TO + th * 512 + 512],
                                x0buf[:, xsl:xsl + 512]), inc="dve")
                            c_o = p.dma("sync", lambda sy, ct=ct, th=th, xsl=xsl: sy.dma_start(
                                out=out_dram[ct * 128:(ct + 1) * 128, th * 512:th * 512 + 512],
                                in_=x0buf[:, xsl:xsl + 512]),
                                "xo", 2, waits=[("dve", c_n)])
                            prev[i] = (c_n, c_o)
                        last_dve = p.v("dve")
                    return p.v("dve"), [prev[30][1], prev[31][1]]

                stA = ExitStack()
                gsb = stA.enter_context(nc.sbuf_tensor("f2gsb", [128, 32 * TO], BF16))
                wpan = stA.enter_context(nc.sbuf_tensor("f2wpan", [128, 2 * 4096], BF16))
                w_gsb = [("gp", p.v("gp"))]
                for kt in range(32):
                    p.emit("gpsimd", lambda g, kt=kt: g.dma_start(
                        out=gsb[:, kt * TO:(kt + 1) * TO], in_=gT[kt * 128:(kt + 1) * 128, :]),
                        waits=w_gsb if kt == 0 else [], inc="din", inck=16)
                gsb_ld = p.v("din")

                evac_done = [None] * 16
                stA_st = [None] * 32
                for ct in range(16):
                    po = (ct % 2) * 4096
                    ap = bass.AP(WoutB_t, ct * 128, [[DM, 128], [128 * DM, 32], [1, 128]])
                    w_pan = [("pe", mmc)] if ct >= 2 else list(wcast_done)
                    pan_c = p.dma("scalar", lambda sq, ap=ap, po=po: sq.dma_start(
                        out=wpan[:, po:po + 4096], in_=ap), "pa", 2, waits=w_pan)
                    hid_c = [None, None]
                    for th in range(2):
                        hsl = ((ct * 2 + th) % 4) * 512
                        w_h = [("dve", evac_done[ct - 2])] if ct >= 2 else []
                        hid_c[th] = p.dma("sync", lambda sy, ct=ct, th=th, hsl=hsl: sy.dma_start(
                            out=hidb[:, hsl:hsl + 512],
                            in_=hT[ct * 128:(ct + 1) * 128, th * 512:th * 512 + 512]),
                            "ph", 4, waits=w_h)
                    mm_waits = [pan_c, ("din", gsb_ld if ct == 0 else None),
                                ("dve", evac_done[ct - 1] if ct >= 1 else thr_done)]

                    def mm_th(th):
                        def f(t, th=th, po=po):
                            r = None
                            for kt in range(32):
                                r = t.matmul(ps[th][:, :],
                                             wpan[:, po + kt * 128:po + kt * 128 + 128],
                                             gsb[:, kt * TO + th * 512:kt * TO + th * 512 + 512],
                                             start=(kt == 0), stop=(kt == 31))
                            return r
                        return f
                    mmc_th = [p.emit("tensor", mm_th(0), waits=mm_waits, inc="pe"),
                              p.emit("tensor", mm_th(1), inc="pe")]
                    mmc = mmc_th[1]
                    for th in range(2):
                        i = ct * 2 + th
                        xsl = (i % 4) * 512
                        w_ev = [("pe", mmc_th[th]), hid_c[th]]
                        if i >= 4:
                            w_ev.append(stA_st[i - 4])
                        c_ev = p.emit("vector", lambda v, th=th, xsl=xsl, i=i: v.tensor_tensor(
                            x0buf[:, xsl:xsl + 512], ps[th][:, :],
                            hidb[:, (i % 4) * 512:(i % 4) * 512 + 512], ALU.add),
                            waits=w_ev, inc="dve")
                        stA_st[i] = stats_tile(i, ct, th, xsl, c_ev, first_wait=thr_done)
                    evac_done[ct] = p.v("dve")
                stA.close()
                c_a1, ln1_cd = ln_rows(1e-5)
                ln1_dve, ln1_st = ln_normalize(c_a1, stA_st[28:32], x1b, x1d)

                stBH = ExitStack()
                hsb = stBH.enter_context(nc.sbuf_tensor("f2hsb", [128, 44 * TO], BF16))
                stB1 = ExitStack()
                tern = stB1.enter_context(nc.sbuf_tensor("f2tern", [128, 32 * 512], BF16))

                tern_c = {}
                jt_last_mm = None
                for jt in range(11):
                    for u in range(32):
                        kt, w_i = u // 2, u % 2
                        gi = jt * 32 + u
                        sl = gi % 4
                        dram = w1T if w_i == 0 else w2T
                        waits = []
                        if gi == 0:
                            waits = [("dve", gam_dve), ("gp", gam_gp)] + ln1_st
                        if gi >= 4:
                            waits.append(("dve", tern_c[gi - 4]))
                        dc = p.dma("sync", lambda sy, dram=dram, kt=kt, jt=jt, sl=sl:
                                   sy.dma_start(out=wstr[:, sl * 512:(sl + 1) * 512],
                                                in_=dram[kt * 128:(kt + 1) * 128,
                                                         jt * 512:(jt + 1) * 512]),
                                   "gw", 4, waits=waits)
                        gw = [dc]
                        if gi == 0:
                            gw += [("dve", thr_done)] + ln1_st
                        if jt > 0 and u == 0:
                            gw.append(("pe", jt_last_mm))
                        ts_slot = u
                        p.emit("vector", lambda g, sl=sl, w_i=w_i: g.tensor_scalar(
                            tpos[:, :], wstr[:, sl * 512:(sl + 1) * 512], thr[:, w_i:w_i + 1],
                            None, ALU.is_gt), waits=gw)
                        p.emit("vector", lambda g, sl=sl, w_i=w_i: g.tensor_scalar(
                            tneg[:, :], wstr[:, sl * 512:(sl + 1) * 512],
                            thr[:, 4 + w_i:5 + w_i], None, ALU.is_lt))
                        tern_c[gi] = p.emit("vector", lambda g, ts_slot=ts_slot:
                                            g.tensor_tensor(
                                                tern[:, ts_slot * 512:(ts_slot + 1) * 512],
                                                tneg[:, :], tpos[:, :], ALU.subtract), inc="dve")
                    tern_all = p.v("dve")
                    for jj in range(4):
                        w_mm = [("dve", tern_all), ("dve", p.v("dve") if (jt + jj) > 0 else None)]
                        mmh_c = {}
                        for w_i in range(2):
                            for th in range(2):
                                def mmh(t, w_i=w_i, th=th, jj=jj):
                                    pp = ps[w_i * 2 + th]
                                    r = None
                                    for kt in range(16):
                                        ts_slot = kt * 2 + w_i
                                        r = t.matmul(pp[:, :],
                                                     tern[:, ts_slot * 512 + jj * 128:
                                                          ts_slot * 512 + jj * 128 + 128],
                                                     x1b[:, kt * TO + th * 512:
                                                         kt * TO + th * 512 + 512],
                                                     start=(kt == 0), stop=(kt == 15))
                                    return r
                                mmh_c[(w_i, th)] = p.emit(
                                    "tensor", mmh,
                                    waits=w_mm if (w_i, th) == (0, 0) else [], inc="pe")
                        jt_last_mm = mmh_c[(1, 1)]
                        sig_c = {}
                        for th in range(2):
                            sig_c[th] = p.emit("scalar", lambda a, th=th: a.activation(
                                out=sig[:, th * 512:th * 512 + 512], in_=ps[th][:, :],
                                func=AF.Sigmoid, scale=-1.0),
                                waits=[("pe", mmh_c[(0, th)])], inc="act")
                        jcol = jt * 4 + jj
                        for th in range(2):
                            p.emit("vector", lambda v, jcol=jcol, th=th: v.scalar_tensor_tensor(
                                hsb[:, jcol * TO + th * 512:jcol * TO + th * 512 + 512],
                                ps[2 + th][:, :], -1.0, sig[:, th * 512:th * 512 + 512],
                                ALU.mult, ALU.mult),
                                waits=[("act", sig_c[th]), ("pe", mmh_c[(1, th)])],
                                inc="dve" if th == 1 else None)
                h_all_dve = p.v("dve")
                if dbg:
                    for j in range(44):
                        p.dma("sync", lambda sy, j=j: sy.dma_start(
                            out=hdbg[j * 128:(j + 1) * 128, :], in_=hsb[:, j * TO:(j + 1) * TO]),
                            "hdb", 2, waits=[("dve", h_all_dve)] if j == 0 else [])
                    p.dma("sync", lambda sy: sy.dma_start(out=thrd[:, :], in_=thr[:, 0:8]),
                          "hdb", 2)
                stB1.close()

                stB2 = ExitStack()
                w3pan = stB2.enter_context(nc.sbuf_tensor("f2w3pan", [128, 44 * 128], F32))
                t3pan = stB2.enter_context(nc.sbuf_tensor("f2t3pan", [128, 44 * 128], BF16))
                ln2_mm = [None] * 16
                stB_st = [None] * 32
                evh = [None] * 32
                for ct in range(16):
                    ap = bass.AP(w3T_h, ct * 128, [[DM, 128], [128 * DM, 44], [1, 128]])
                    w_pan = [("dve", h_all_dve)] if ct == 0 else [("pe", ln2_mm[ct - 1])]
                    pan_c = p.dma("scalar", lambda sq, ap=ap: sq.dma_start(
                        out=w3pan[:, :], in_=ap), "w3", 1, waits=w_pan)
                    for b in range(11):
                        p.emit("vector", lambda g, b=b: g.tensor_scalar(
                            tpos[:, :], w3pan[:, b * 512:b * 512 + 512],
                            thr[:, 2:3], None, ALU.is_gt),
                            waits=[pan_c] if b == 0 else [])
                        p.emit("vector", lambda g, b=b: g.tensor_scalar(
                            tneg[:, :], w3pan[:, b * 512:b * 512 + 512],
                            thr[:, 6:7], None, ALU.is_lt))
                        p.emit("vector", lambda g, b=b: g.tensor_tensor(
                            t3pan[:, b * 512:b * 512 + 512], tneg[:, :], tpos[:, :],
                            ALU.subtract))
                    tern3 = p.v("dve")
                    x1l = [None, None]
                    for th in range(2):
                        i = ct * 2 + th
                        hsl = (i % 4) * 512
                        w_h = [("dve", evh[i - 4])] if ct >= 2 else [("dve", h_all_dve)]
                        x1l[th] = p.dma("sync", lambda sy, ct=ct, th=th, hsl=hsl: sy.dma_start(
                            out=hidb[:, hsl:hsl + 512],
                            in_=x1d[ct * 128:(ct + 1) * 128, th * 512:th * 512 + 512]),
                            "px", 4, waits=w_h)

                    def mmy_th(th):
                        def f(t, th=th):
                            r = None
                            for j in range(44):
                                r = t.matmul(ps[th][:, :], t3pan[:, j * 128:j * 128 + 128],
                                             hsb[:, j * TO + th * 512:j * TO + th * 512 + 512],
                                             start=(j == 0), stop=(j == 43))
                            return r
                        return f
                    mmy_c = [p.emit("tensor", mmy_th(0),
                                    waits=[("dve", tern3),
                                           ("dve", evh[2 * ct - 1] if ct >= 1 else None)],
                                    inc="pe"),
                             p.emit("tensor", mmy_th(1), inc="pe")]
                    ln2_mm[ct] = mmy_c[1]
                    for th in range(2):
                        i = ct * 2 + th
                        xsl = (i % 4) * 512
                        w_ev = [("pe", mmy_c[th]), x1l[th]]
                        if i >= 4:
                            w_ev.append(stB_st[i - 4])
                        c_ev = p.emit("vector", lambda v, th=th, xsl=xsl, i=i: v.tensor_tensor(
                            x0buf[:, xsl:xsl + 512],
                            hidb[:, (i % 4) * 512:(i % 4) * 512 + 512],
                            ps[th][:, :], ALU.subtract), waits=w_ev, inc="dve")
                        evh[i] = c_ev
                        stB_st[i] = stats_tile(i, ct, th, xsl, c_ev, first_wait=ln1_cd)
                stB2.close()
                stBH.close()
                c_a2, ln2_cd = ln_rows(1e-5)
                x2b = e(nc.sbuf_tensor("f2x2b", [128, 16 * TO], BF16))
                x2_dve, x2_st = ln_normalize(c_a2, stB_st[28:32], x2b, x2T_o)

                stC = ExitStack()
                kvpan = stC.enter_context(nc.sbuf_tensor("f2kvpan", [128, 2 * 2048], BF16))
                kvout = stC.enter_context(nc.sbuf_tensor("f2kvout", [128, 2 * TO], F32))
                xqb = stC.enter_context(nc.sbuf_tensor("f2xqb", [128, 16 * TO], BF16))
                scs = stC.enter_context(nc.sbuf_tensor("f2scs", [128, TO], F32))
                scs_ld = p.dma("sync", lambda sy: sy.dma_start(out=scs[:, :],
                                                                in_=sc_h.ap()[:, :]), "scl", 1)

                cx_hist = []
                for i in range(32):
                    ct, th = i // 2, i % 2
                    sl = ct * TO + th * 512
                    w0 = [("dve", x2_dve)] if i == 0 else []
                    if i >= 2:
                        w0.append(("pe", cx_hist[i - 2]))
                    c_ab = p.emit("scalar", lambda a, sl=sl, i=i: a.activation(
                        out=sqs[:, (i % 2) * 512:(i % 2) * 512 + 512], in_=x2b[:, sl:sl + 512],
                        func=AF.Abs), waits=w0, inc="act")
                    cx_hist.append(p.emit("tensor", lambda t, th=th, ct=ct, i=i: t.matmul(
                        psr[th][:1, :], onecb[:, :], sqs[:, (i % 2) * 512:(i % 2) * 512 + 512],
                        start=(ct == 0), stop=(ct == 15)),
                        waits=[("act", c_ab)] + ([("dve", ln2_cd)] if i <= 1 else []), inc="pe"))
                cx_pe = p.v("pe")
                p.emit("scalar", lambda a: a.activation(
                    out=rowm[:1, 0:512], in_=psr[0][:1, :], func=AF.Sigmoid),
                    waits=[("pe", cx_pe)], inc="act")
                c_cx = p.emit("scalar", lambda a: a.activation(
                    out=rowm[:1, 512:1024], in_=psr[1][:1, :], func=AF.Sigmoid), inc="act")
                last = None
                for th in range(2):
                    c_bc = p.emit("tensor", lambda t, th=th: t.matmul(
                        ps[2][:, :], oner[:1, :], rowm[:1, th * 512:th * 512 + 512],
                        start=True, stop=True), waits=[("act", c_cx)] +
                        ([("dve", last)] if th == 1 else []), inc="pe")
                    for ct in range(16):
                        sl = ct * TO + th * 512
                        p.emit("vector", lambda v, sl=sl: v.tensor_tensor(
                            xqb[:, sl:sl + 512], x2b[:, sl:sl + 512], ps[2][:, :], ALU.mult),
                            waits=[("pe", c_bc)] if ct == 0 else [],
                            inc="dve" if ct == 15 else None)
                    last = p.v("dve")
                xq_done = p.v("dve")

                def rope_block(pp, th, ob, w_first):
                    p.emit("vector", lambda v: v.tensor_tensor(
                        tpos[0:64, 0:512], pp[0:64, :],
                        scs[64:128, th * 512:th * 512 + 512], ALU.mult), waits=w_first)
                    p.emit("vector", lambda v: v.tensor_tensor(
                        x0buf[0:64, 0:512], pp[64:128, :],
                        scs[0:64, th * 512:th * 512 + 512], ALU.mult))
                    p.emit("vector", lambda v: v.tensor_tensor(
                        kvout[0:64, ob + th * 512:ob + th * 512 + 512],
                        tpos[0:64, 0:512], x0buf[0:64, 0:512], ALU.subtract))
                    p.emit("vector", lambda v: v.tensor_tensor(
                        tpos[0:64, 0:512], pp[0:64, :],
                        scs[0:64, th * 512:th * 512 + 512], ALU.mult))
                    p.emit("vector", lambda v: v.tensor_tensor(
                        x0buf[0:64, 0:512], pp[64:128, :],
                        scs[64:128, th * 512:th * 512 + 512], ALU.mult))
                    return p.emit("vector", lambda v: v.tensor_tensor(
                        kvout[64:128, ob + th * 512:ob + th * 512 + 512],
                        tpos[0:64, 0:512], x0buf[0:64, 0:512], ALU.add), inc="dve")

                def proj_pass(W_h, rhs, rope_cts, outs, wait0, st_eng="sync"):
                    prev = [None] * 16
                    stc = [None] * 16
                    for ct in range(16):
                        po = (ct % 2) * 2048
                        ap = bass.AP(W_h, ct * 128, [[DM, 128], [128 * DM, 16], [1, 128]])
                        w_pan = ([("dve", wait0)] + list(wcast_done)) if ct < 2 else \
                            [("dve", prev[ct - 2])]
                        pan_c = p.dma("scalar", lambda sq, ap=ap, po=po: sq.dma_start(
                            out=kvpan[:, po:po + 2048], in_=ap), "kp", 2, waits=w_pan)

                        def mmp_th(th):
                            def f(t, th=th, po=po, rhs=rhs):
                                r = None
                                for kt in range(16):
                                    r = t.matmul(ps[th][:, :],
                                                 kvpan[:, po + kt * 128:po + kt * 128 + 128],
                                                 rhs[:, kt * TO + th * 512:
                                                     kt * TO + th * 512 + 512],
                                                 start=(kt == 0), stop=(kt == 15))
                                return r
                            return f
                        mmc_t = [p.emit("tensor", mmp_th(0),
                                        waits=[pan_c,
                                               ("dve", prev[ct - 1] if ct >= 1 else wait0)],
                                        inc="pe"),
                                 p.emit("tensor", mmp_th(1), inc="pe")]
                        mmc = mmc_t[1]
                        ob = (ct % 2) * TO
                        for th in range(2):
                            pp = ps[th]
                            w_first = [("pe", mmc_t[th])]
                            if th == 0 and ct >= 2:
                                w_first.append(stc[ct - 2])
                            if ct in rope_cts:
                                c_last = rope_block(pp, th, ob, w_first)
                            else:
                                c_last = p.emit("vector", lambda v, pp=pp, th=th, ob=ob:
                                                v.tensor_copy(
                                                    kvout[:, ob + th * 512:ob + th * 512 + 512],
                                                    pp[:, :]), waits=w_first, inc="dve")
                        prev[ct] = c_last
                        oo, ro = outs(ct)
                        stc[ct] = p.dma(st_eng, lambda sy, oo=oo, ro=ro, ob=ob: sy.dma_start(
                            out=oo[ro:ro + 128, :], in_=kvout[:, ob:ob + TO]),
                            "ko", 2, waits=[("dve", prev[ct])])
                    return prev, stc

                p.emit("vector", lambda v: None, waits=[scs_ld] + x2_st)
                prev_kv, st_kv = proj_pass(
                    WkvB_t, x2b, set(range(8)),
                    lambda ct: (kroT_o, ct * 128) if ct < 8 else (vT_o, (ct - 8) * 128),
                    x2_dve, st_eng="gpsimd")
                p.kv_marks = [st_kv[14], st_kv[15]]
                p.emit("vector", lambda v: None, waits=[st_kv[14], st_kv[15]])
                prev_q, st_q = proj_pass(
                    WqB_t, xqb, set(range(16)),
                    lambda ct: (qroT_o, ct * 128),
                    p.v("dve"))
                stC.close()

        _phase2()

        c_agk = p.emit("gpsimd", lambda g: g.collective_compute(
            "AllGather", mybir.AluOpType.bypass, replica_groups=rg,
            ins=[kag_in_h.ap().opt()], outs=[kag_h.ap().opt()]),
            waits=list(p.kv_marks))
        c_agv = p.emit("gpsimd", lambda g: g.collective_compute(
            "AllGather", mybir.AluOpType.bypass, replica_groups=rg,
            ins=[vag_in_h.ap().opt()], outs=[vag_h.ap().opt()]))
        allw = [(n, p.c[n]) for n in list(p.h.keys())
                if n not in ("m1", "m2", "m3")]
        p.emit("sync", lambda sy: None, waits=allw, inc="m2", inck=1)
        p.arm_barrier([("m2", 1)])
        if nph < 3:
            dbgo = nc.dram_tensor("dbgo", [1, 1], F32, kind="ExternalOutput")
            sdb = p.dma("gpsimd", lambda g: g.dma_start(
                out=dbgo.ap()[:, :], in_=a2a_out_h.ap()[0:1, 0:1]), "db", 1,
                waits=[("gp", p.c["gp"])])
            allw = [(n, p.c[n]) for n in list(p.h.keys())]
            p.emit("sync", lambda sy: None, waits=allw)
            with nc.Block() as block:
                p.replay(block)
            return nc
        def _phase3():
            q_h = qroT_i
            x2_h = x2T_i
            Wo_h = nc.dram_tensor("Wo", [DM, DM], F32, kind="ExternalInput")
            bo_h = nc.dram_tensor("bo2", [DM, 1], F32, kind="ExternalInput")
            ir_h = nc.dram_tensor("irow", [1, TO], F32, kind="ExternalInput")
            sj_h = nc.dram_tensor("SJ", [128, 128], F32, kind="ExternalInput")
            sg_h = nc.dram_tensor("SG", [1, 2 * SEQ], F32, kind="ExternalInput")
            sgb_h = nc.dram_tensor("SGb", [4, 2 * SEQ], BF16, kind="ExternalInput")
            khd = nc.dram_tensor("khd", [1, 2048], F32).ap()
            ucd = nc.dram_tensor("ucd", [1, 4], F32).ap()
            pen_h = nc.dram_tensor("PEND3", [128, NSL], F32, kind="ExternalInput")
            u0_h = nc.dram_tensor("u0", [128, 1], F32, kind="ExternalInput")
            hr_h = nc.dram_tensor("hrow", [1, 16], F32, kind="ExternalInput")
            cst_h = nc.dram_tensor("cstc", [1, 4], F32, kind="ExternalInput")
            x3T_o = x3T_i.ap()
            amx_o = arin_h.ap()
            x0d = nc.dram_tensor("x0d3", [DM, TO], F32,
                                 kind="ExternalOutput" if dbg else "Internal").ap()

            with ExitStack() as ctx:
                e = ctx.enter_context
                qb = e(nc.sbuf_tensor("f3qb", [128, 4 * TO], BF16))
                kf = e(nc.sbuf_tensor("f3kf", [128, SEQ], BF16))
                vt = e(nc.sbuf_tensor("f3vt", [128, SEQ], BF16))
                outT = e(nc.sbuf_tensor("f3outT", [128, 16 * TO], BF16))
                d3sb = e(nc.sbuf_tensor("f3d3sb", [128, NSL * 512], F32))
                sgf4 = e(nc.sbuf_tensor("f3sgf4", [4, SEQ], BF16))
                sgu4 = e(nc.sbuf_tensor("f3sgu4", [4, SEQ], BF16))
                kh4 = e(nc.sbuf_tensor("f3kh4", [4, 512], BF16))
                ucol4 = e(nc.sbuf_tensor("f3ucol4", [4, 1], F32))
                ub1 = e(nc.sbuf_tensor("f3ub1", [1, 1], BF16))
                khtmp = e(nc.sbuf_tensor("f3khtmp", [1, 2048], F32))
                sj = e(nc.sbuf_tensor("f3sj", [128, 128], F32))
                t0 = e(nc.sbuf_tensor("f3t0", [128, 512], F32))
                t1 = e(nc.sbuf_tensor("f3t1", [128, 512], F32))
                pT = e(nc.sbuf_tensor("f3pT", [128, 2 * 512], BF16))
                sqk = e(nc.sbuf_tensor("f3sqk", [128, 2 * 512], BF16))
                bcol = e(nc.sbuf_tensor("f3bcol", [128, 128], F32))
                ccol = e(nc.sbuf_tensor("f3ccol", [128, 12], F32))
                mxa = e(nc.sbuf_tensor("f3mxa", [128, 2], F32))
                rowm = e(nc.sbuf_tensor("f3rowm", [1, TO], F32))
                rowv = e(nc.sbuf_tensor("f3rowv", [1, TO], F32))
                rowr = e(nc.sbuf_tensor("f3rowr", [1, TO], F32))
                rowg = e(nc.sbuf_tensor("f3rowg", [1, TO], F32))
                srow = e(nc.sbuf_tensor("f3srow", [1, 64], F32))
                hrow = e(nc.sbuf_tensor("f3hrowS", [1, 16], F32))
                slo = e(nc.sbuf_tensor("f3slo", [1, 16], F32))
                cstc = e(nc.sbuf_tensor("f3cstcS", [1, 4], F32))
                onec = e(nc.sbuf_tensor("f3onec", [128, 1], F32))
                onecb = e(nc.sbuf_tensor("f3onecb", [128, 1], BF16))
                oner = e(nc.sbuf_tensor("f3oner", [1, 128], F32))
                epsc = e(nc.sbuf_tensor("f3epsc", [1, 1], F32))
                o64 = e(nc.sbuf_tensor("f3o64", [128, 64], F32))
                idn = e(nc.sbuf_tensor("f3idn", [128, 128], F32))
                wpan = e(nc.sbuf_tensor("f3wpan", [128, 2 * 2048], BF16))
                x2b = e(nc.sbuf_tensor("f3x2rb", [128, 4 * 512], F32))
                x0buf = e(nc.sbuf_tensor("f3x0buf", [128, 4 * 512], F32))
                ps = [e(nc.psum_tensor(f"f3ps{i}", [128, 512], F32)) for i in range(4)]
                psr = [e(nc.psum_tensor(f"f3psr{i}", [1, 512], F32)) for i in range(4)]

                p.emit("gpsimd", lambda g: g.memset(onec[:, :], 1.0), gpser=True)
                p.emit("gpsimd", lambda g: g.memset(onecb[:, :], 1.0), gpser=True)
                p.emit("gpsimd", lambda g: g.memset(oner[:1, :], 1.0), gpser=True)
                p.emit("gpsimd", lambda g: g.memset(epsc[:1, :], 1e-5), gpser=True)
                p.emit("gpsimd", lambda g: g.memset(o64[:, :], 1.0), gpser=True)
                p.emit("gpsimd", lambda g: g.memset(mxa[:, 0:2], 0.0), gpser=True)
                p.emit("gpsimd", lambda g: g.iota(idn[:, :], [[-1, 128]], base=0,
                                                  channel_multiplier=1,
                                                  allow_small_or_imprecise_dtypes=True),
                       gpser=True)
                p.emit("gpsimd", lambda g: g.iota(t1[:, :], [[-1, 512]], base=0,
                                                  channel_multiplier=1,
                                                  allow_small_or_imprecise_dtypes=True),
                       gpser=True)
                init_gp = p.v("gp")
                c_idn = p.emit("vector", lambda v: v.tensor_scalar(
                    idn[:, :], idn[:, :], 0.0, None, ALU.is_equal), waits=[("gp", init_gp)],
                    inc="dve")

                sj_ld = p.dma("sync", lambda sy: sy.dma_start(out=sj[:, :],
                                                              in_=sj_h.ap()[:, :]), "lsA", 4)
                hr_ld = p.dma("sync", lambda sy: sy.dma_start(out=hrow[:1, :],
                                                              in_=hr_h.ap()[:, :]), "lsA", 4)
                cs_ld = p.dma("sync", lambda sy: sy.dma_start(out=cstc[:1, :],
                                                              in_=cst_h.ap()[:, :]), "lsA", 4)
                u0_ld = p.dma("sync", lambda sy: sy.dma_start(out=ccol[:, 3:4],
                                                              in_=u0_h.ap()[:, :]), "lsA", 4)
                pend3 = e(nc.sbuf_tensor("f3pend3", [128, NSL], F32))
                idxs = e(nc.sbuf_tensor("f3idxs", [128, 64], U32))
                idx_ld = p.dma("sync", lambda sy: sy.dma_start(
                    out=idxs[:, :], in_=idxk_h.ap()[:, :]), "ls7", 1)
                idnb3 = e(nc.sbuf_tensor("f3idnb", [128, 128], BF16))
                idn3_ld = p.dma("sync", lambda sy: sy.dma_start(
                    out=idnb3[:, :], in_=idn_h.ap()[:, :]), "ls8", 1)
                vstg = e(nc.sbuf_tensor("f3vstg", [128, 2 * 1024], BF16))
                ps3b = ps[3].bitcast(BF16)
                pen_ld = p.dma("sync", lambda sy: sy.dma_start(out=pend3[:, :],
                                                               in_=pen_h.ap()[:, :]), "lsA", 4)

                c_l2 = p.emit("scalar", lambda a: a.activation(
                    out=cstc[:1, 1:2], in_=cstc[:1, 0:1], func=AF.Ln), waits=[cs_ld],
                    inc="act")
                c_sl = p.emit("vector", lambda v: v.tensor_scalar(
                    slo[:1, :], hrow[:1, :], cstc[0:1, 1:2], None, ALU.mult),
                    waits=[("act", c_l2), hr_ld], inc="dve")
                c_slo = p.emit("scalar", lambda a: a.activation(
                    out=slo[:1, :], in_=slo[:1, :], func=AF.Exp), waits=[("dve", c_sl)],
                    inc="act")

                for (qc, jt), sl in SSLOT.items():
                    off = float(128 * jt - 512 * qc)
                    if (qc, jt) in WRAP:
                        p.emit("vector", lambda v, sl=sl, off=off: v.tensor_scalar(
                            d3sb[:, sl * 512:sl * 512 + 512], t1[:, :], -1.0, 8192.0 - off,
                            ALU.mult, ALU.add), waits=[("gp", init_gp)] if sl == 0 else [])
                    else:
                        p.emit("vector", lambda v, sl=sl, off=off: v.tensor_scalar(
                            d3sb[:, sl * 512:sl * 512 + 512], t1[:, :], off, None,
                            ALU.add), waits=[("gp", init_gp)] if sl == 0 else [])
                        c_ab = p.emit("scalar", lambda a, sl=sl: a.activation(
                            out=d3sb[:, sl * 512:sl * 512 + 512],
                            in_=d3sb[:, sl * 512:sl * 512 + 512], func=AF.Abs),
                            waits=[("dve", p.v("dve"))], inc="act")
                        p.emit("vector", lambda v: None, waits=[("act", c_ab)])
                    p.emit("vector", lambda v, sl=sl: v.tensor_scalar(
                        t0[:, :], d3sb[:, sl * 512:sl * 512 + 512], 256.5, None, ALU.is_le))
                    p.emit("vector", lambda v, sl=sl: v.scalar_tensor_tensor(
                        d3sb[:, sl * 512:sl * 512 + 512], t0[:, :], 1e12,
                        d3sb[:, sl * 512:sl * 512 + 512], ALU.mult, ALU.add))
                for sl in range(NSL):
                    p.emit("vector", lambda v, sl=sl: v.tensor_scalar(
                        d3sb[:, sl * 512:sl * 512 + 512], d3sb[:, sl * 512:sl * 512 + 512],
                        pend3[:, sl:sl + 1], None, ALU.add),
                        waits=[pen_ld] if sl == 0 else [])
                d3_done = p.emit("vector", lambda v: v.tensor_copy(t0[:, 0:1], t0[:, 0:1]),
                                 inc="dve")
                p.emit("vector", lambda v: v.tensor_scalar(
                    khtmp[0:1, 0:512], t1[0:1, 0:512], -1.0, None, ALU.mult))
                p.emit("vector", lambda v: v.tensor_copy(
                    khtmp[0:1, 1024:1536], khtmp[0:1, 0:512]))
                p.emit("vector", lambda v: v.tensor_copy(pT[0:1, 0:512],
                                                         khtmp[0:1, 0:512]))
                p.emit("vector", lambda v: v.tensor_copy(khtmp[0:1, 0:512],
                                                         pT[0:1, 0:512]))
                p.emit("vector", lambda v: v.tensor_tensor(
                    khtmp[0:1, 512:1024], khtmp[0:1, 1024:1536], khtmp[0:1, 0:512],
                    ALU.subtract))
                p.emit("vector", lambda v: v.tensor_copy(
                    khtmp[0:1, 1024:1536], khtmp[0:1, 0:512]))
                c_kh = p.emit("vector", lambda v: v.tensor_copy(
                    khtmp[0:1, 1536:2048], khtmp[0:1, 512:1024]), inc="dve")
                st_kh = p.dma("sync", lambda sy: sy.dma_start(
                    out=khd[0:1, :], in_=khtmp[0:1, :]), "luc", 2,
                    waits=[("dve", c_kh)])
                ld_kh = p.dma("gpsimd", lambda g: g.dma_start(
                    out=kh4[0:4, :], in_=khd[0:1, :].rearrange(
                        "a (p t) -> (a p) t", p=4)), "luc", 2, waits=[st_kh])

                psr0_free = p.v("dve")
                prev_exp = None
                prev_evac = None
                prev_sum = None
                grp_done = None
                mq_done = {}
                q_ld = {}
                kn_last = None
                mq_last = None
                vse_prev = None
                out_written = {}
                head_pe_done = [None] * 8

                for kvh in range(8):
                    for qq in range(2):
                        qh = 2 * kvh + qq
                        slq = qh % 4
                        w_q = []
                        if qh >= 4:
                            w_q.append(("pe", head_pe_done[kvh - 2]))
                        q_ld[qh] = p.dma("gpsimd", lambda g, qh=qh, slq=slq: g.dma_start(
                            out=qb[:, slq * TO:(slq + 1) * TO],
                            in_=q_h.ap()[qh * 128:(qh + 1) * 128, :]), "lq", 4, waits=w_q)
                    w_kv = [("pe", head_pe_done[kvh - 1])] if kvh >= 1 else []
                    if kvh == 0:
                        w_kv = w_kv + [idx_ld, idn3_ld]
                    kld = None
                    for rp in range(8):
                        kld = p.dma("gpsimd", lambda g, kvh=kvh, rp=rp:
                                    g.indirect_dma_start(
                                        out=kf[:, rp * 1024:(rp + 1) * 1024],
                                        out_offset=None, in_=kag_h.ap()[:, :],
                                        in_offset=bass.IndirectOffsetOnAxis(
                                            ap=idxs[:, kvh * 8 + rp:kvh * 8 + rp + 1],
                                            axis=0)),
                                    "lk", 1, waits=w_kv if rp == 0 else [])
                    vld_e = None
                    prev_vtr = None
                    for rp in range(8):
                        sltg = (rp % 2) * 1024
                        w_g = list(w_kv) if rp == 0 else []
                        if prev_vtr is not None:
                            w_g.append(("pe", prev_vtr))
                        g_v = p.dma("gpsimd", lambda g, kvh=kvh, rp=rp, sltg=sltg:
                                    g.indirect_dma_start(
                                        out=vstg[:, sltg:sltg + 1024],
                                        out_offset=None, in_=vag_h.ap()[:, :],
                                        in_offset=bass.IndirectOffsetOnAxis(
                                            ap=idxs[:, kvh * 8 + rp:kvh * 8 + rp + 1],
                                            axis=0)),
                                    "lv", 2, waits=w_g)
                        for tt in range(8):
                            w_tr = [g_v, ("dve", p.v("dve"))] if tt == 0 else []
                            p.emit("tensor", lambda t, tt=tt, sltg=sltg: t.transpose(
                                ps3b[:, tt * 128:(tt + 1) * 128],
                                vstg[:, sltg + tt * 128:sltg + tt * 128 + 128],
                                idnb3[:, :]), waits=w_tr, inc="pe")
                        prev_vtr = p.v("pe")
                        for tt in range(8):
                            vld_e = p.emit("vector", lambda v, rp=rp, tt=tt: v.tensor_copy(
                                vt[:, (rp * 8 + tt) * 128:(rp * 8 + tt) * 128 + 128],
                                ps3b[:, tt * 128:(tt + 1) * 128]),
                                waits=[("pe", prev_vtr)] if tt == 0 else [], inc="dve")
                    vld = ("dve", vld_e)

                    def rowmax(src_tiles, dst_cell, waits0):
                        nonlocal psr0_free
                        last = None
                        for u, (ap_in, n) in enumerate(src_tiles):
                            sqc = p.emit("scalar", lambda a, ap_in=ap_in, u=u: a.activation(
                                out=sqk[:, (u % 2) * 512:(u % 2) * 512 + 512], in_=ap_in,
                                func=AF.Square),
                                waits=(waits0 if u == 0 else []) +
                                ([("pe", mm_h[u - 2])] if u >= 2 else []), inc="act")
                            if u == 0:
                                mm_h = [None] * len(src_tiles)
                            mm_h[u] = p.emit("tensor", lambda t, u=u: t.matmul(
                                psr[0][:1, :], onecb[:, :],
                                sqk[:, (u % 2) * 512:(u % 2) * 512 + 512],
                                start=True, stop=True),
                                waits=[("act", sqc), ("dve", psr0_free)], inc="pe")
                            p.emit("vector", lambda v: v.tensor_reduce(
                                srow[:1, 16:17], psr[0][:1, :], AX.X, ALU.max),
                                waits=[("pe", mm_h[u])])
                            if u == 0:
                                last = p.emit("vector", lambda v, dst_cell=dst_cell:
                                              v.tensor_copy(srow[:1, dst_cell:dst_cell + 1],
                                                            srow[:1, 16:17]), inc="dve")
                            else:
                                last = p.emit("vector", lambda v, dst_cell=dst_cell:
                                              v.tensor_tensor(
                                                  srow[:1, dst_cell:dst_cell + 1],
                                                  srow[:1, dst_cell:dst_cell + 1],
                                                  srow[:1, 16:17], ALU.max), inc="dve")
                            psr0_free = last
                        return last

                    for qq in range(2):
                        qh = 2 * kvh + qq
                        slq = qh % 4
                        tiles = [(qb[:, slq * TO + th * 512:slq * TO + th * 512 + 512], th)
                                 for th in range(2)]
                        mq_done[qh] = rowmax(tiles, 32 + qh, [q_ld[qh]])
                    ktiles = [(kf[:, cn * 512:cn * 512 + 512], cn) for cn in range(16)]
                    kn_last = rowmax(ktiles, 1, [kld])

                    def vsmm(t):
                        r = None
                        for jt in range(64):
                            r = t.matmul(ps[3][:, 0:1],
                                         vt[:, jt * 128:jt * 128 + 128],
                                         onecb[:, :], start=(jt == 0), stop=(jt == 63))
                        return r
                    w_vs = [vld]
                    if vse_prev is not None:
                        w_vs.append(("dve", vse_prev))
                    if grp_done is not None:
                        w_vs.append(("dve", grp_done))
                    c_vs = p.emit("tensor", vsmm, waits=w_vs, inc="pe")
                    c_vse = p.emit("vector", lambda v: v.tensor_scalar(
                        ccol[:, 1:2], ps[3][:, 0:1], 1.0 / SEQ, None, ALU.mult),
                        waits=[("pe", c_vs), u0_ld], inc="dve")
                    p.emit("vector", lambda v: v.tensor_tensor(
                        ccol[:, 4:5], ccol[:, 1:2], ccol[:, 3:4], ALU.mult))
                    vse_prev = p.emit("vector", lambda v: v.tensor_scalar(
                        ccol[:, 5:6], ccol[:, 3:4], -1.0, 1.0, ALU.mult, ALU.add), inc="dve")

                    for qc in range(2):
                        w_sg = [("pe", p.v("pe")), ("dve", p.v("dve"))]
                        sg_ld = p.dma("sync", lambda sy, qc=qc: sy.dma_start(
                            out=sgf4[0:4, :], in_=sgb_h.ap()[0:4, qc * SEQ:(qc + 1) * SEQ]),
                            "lsg", 1, waits=w_sg)
                        for qq in range(2):
                            qh = 2 * kvh + qq
                            slq = qh % 4
                            p.emit("vector", lambda v, qh=qh: v.tensor_tensor(
                                srow[:1, 2:3], srow[:1, 32 + qh:33 + qh], srow[:1, 1:2],
                                ALU.mult),
                                waits=[("dve", kn_last), ("dve", mq_done[qh]),
                                       ("act", c_slo)])
                            c_q1 = p.v("dve")
                            c_q2 = p.emit("scalar", lambda a: a.activation(
                                out=srow[:1, 3:4], in_=srow[:1, 2:3], func=AF.Sqrt),
                                waits=[("dve", c_q1)], inc="act")
                            p.emit("vector", lambda v, qh=qh: v.tensor_scalar(
                                srow[:1, 4:5], slo[0:1, qh:qh + 1], 257.0, None, ALU.mult),
                                waits=[("act", c_q2)])
                            p.emit("vector", lambda v: v.tensor_scalar(
                                srow[:1, 5:6], srow[:1, 3:4], -SCALE, None, ALU.mult))
                            p.emit("vector", lambda v: v.tensor_tensor(
                                srow[:1, 6:7], srow[:1, 4:5], srow[:1, 5:6], ALU.add))
                            p.emit("vector", lambda v, qh=qh: v.tensor_scalar(
                                srow[:1, 7:8], slo[0:1, qh:qh + 1], 1.0 / SCALE, None,
                                ALU.mult))
                            p.emit("vector", lambda v, qh=qh: v.tensor_scalar(
                                srow[:1, 8:9], slo[0:1, qh:qh + 1], -1.0, None, ALU.mult))
                            p.emit("vector", lambda v, qh=qh: v.tensor_copy(
                                srow[:1, 9:10], slo[0:1, qh:qh + 1]))
                            p.emit("vector", lambda v: v.tensor_copy(
                                ub1[0:1, 0:1], srow[0:1, 7:8]))
                            p.emit("vector", lambda v: v.tensor_copy(
                                srow[0:1, 20:21], ub1[0:1, 0:1]))
                            p.emit("vector", lambda v: v.tensor_tensor(
                                srow[0:1, 21:22], srow[0:1, 7:8], srow[0:1, 20:21],
                                ALU.subtract))
                            p.emit("vector", lambda v: v.tensor_copy(
                                srow[0:1, 24:25], srow[0:1, 20:21]))
                            p.emit("vector", lambda v: v.tensor_copy(
                                srow[0:1, 25:26], srow[0:1, 20:21]))
                            p.emit("vector", lambda v: v.tensor_copy(
                                srow[0:1, 26:27], srow[0:1, 21:22]))
                            c_u3 = p.emit("vector", lambda v: v.tensor_copy(
                                srow[0:1, 27:28], srow[0:1, 21:22]), inc="dve")
                            st_uc = p.dma("sync", lambda sy: sy.dma_start(
                                out=ucd[0:1, :], in_=srow[0:1, 24:28]), "luc", 2,
                                waits=[("dve", c_u3)])
                            ld_uc = p.dma("sync", lambda sy: sy.dma_start(
                                out=ucol4[0:4, :], in_=ucd[0:1, :].rearrange(
                                    "a (p t) -> (a p) t", p=4)), "luc", 2, waits=[st_uc])
                            c_R = p.emit("vector", lambda v: v.tensor_scalar(
                                sgu4[0:4, :], sgf4[0:4, :], ucol4[0:4, 0:1], None,
                                ALU.mult), waits=[ld_uc, sg_ld, ld_kh], inc="dve")
                            w_cb = [("dve", c_R)]
                            if grp_done is not None:
                                w_cb.append(("dve", grp_done))
                            c_cb2 = p.emit("tensor", lambda t: t.matmul(
                                ps[3][:, 1:5], oner[:1, :], srow[:1, 6:10], start=True,
                                stop=True), waits=w_cb, inc="pe")
                            c_cbe = p.emit("vector", lambda v: v.tensor_copy(
                                ccol[:, 6:10], ps[3][:, 1:5]), waits=[("pe", c_cb2)],
                                inc="dve")
                            c_bcl = p.emit("vector", lambda v: v.tensor_scalar(
                                bcol[:, :], sj[:, :], ccol[:, 9:10], ccol[:, 6:7],
                                ALU.mult, ALU.add), waits=[sj_ld], inc="dve")

                            sc_mm = [None] * 64
                            pT_done = [None] * 64
                            cons_pv = [None] * 64
                            cons_on = [None] * 64

                            def emit_consumers(ix, jt, nk):
                                sl = ix % 2
                                w_cons = [("act", pT_done[ix])]
                                if ix == 0:
                                    w_cons.append(vld)
                                    if prev_evac is not None:
                                        w_cons.append(("act", prev_evac))
                                cons_pv[ix] = p.emit("tensor", lambda t, sl=sl, jt=jt,
                                                     ix=ix, nk=nk: t.matmul(
                                    ps[2][:, :], vt[:, jt * 128:jt * 128 + 128],
                                    pT[:, sl * 512:sl * 512 + 512],
                                    start=(ix == 0), stop=(ix == nk - 1)), waits=w_cons,
                                    inc="pe")
                                w_on = []
                                if ix == 0 and prev_sum is not None:
                                    w_on.append(("dve", prev_sum))
                                cons_on[ix] = p.emit("tensor", lambda t, sl=sl, ix=ix,
                                                     nk=nk: t.matmul(
                                    psr[1][:1, :], onecb[:, :],
                                    pT[:, sl * 512:sl * 512 + 512],
                                    start=(ix == 0), stop=(ix == nk - 1)), waits=w_on,
                                    inc="pe")
                            kept = [jt for jt in range(64) if KEEP[(qc, jt, qh)]]
                            nk = len(kept)
                            for ix, jt in enumerate(kept):
                                sl = ix % 2
                                is_str = jt in STRADDLE[qc]
                                w_sc = [sg_ld] if ix == 0 else []
                                if ix == 0:
                                    w_sc += [("dve", c_bcl), ("dve", d3_done)]
                                if ix >= 2:
                                    w_sc.append(("act", pT_done[ix - 2]))
                                elif prev_exp is not None:
                                    w_sc.append(("act", prev_exp))

                                def scmm(t, jt=jt, sl=sl, slq=slq, qc=qc, is_str=is_str):
                                    m1 = t.matmul(
                                        ps[sl][:, :],
                                        kf[:, jt * 128:jt * 128 + 128],
                                        qb[:, slq * TO + qc * 512:slq * TO + qc * 512 + 512],
                                        start=True, stop=is_str)
                                    if is_str:
                                        return m1
                                    return t.matmul(
                                        ps[sl][:, :],
                                        sgu4[0:4, jt * 128:jt * 128 + 128],
                                        kh4[0:4, 0:512],
                                        start=False, stop=True)
                                sc_mm[ix] = p.emit("tensor", scmm, waits=w_sc, inc="pe")

                                w_pt = [("pe", sc_mm[ix])]
                                if ix >= 2:
                                    w_pt += [("pe", cons_on[ix - 2])]
                                if not is_str:
                                    pT_done[ix] = p.emit(
                                        "scalar", lambda a, sl=sl, qc=qc, jt=jt: a.activation(
                                            out=pT[:, sl * 512:sl * 512 + 512],
                                            in_=ps[sl][:, :], func=AF.Exp,
                                            bias=bcol[:, qc * 64 + jt:qc * 64 + jt + 1],
                                            scale=SCALE), waits=w_pt, inc="act")
                                else:
                                    ssl = SSLOT[(qc, jt)]
                                    p.emit("vector", lambda v, ssl=ssl: v.tensor_scalar(
                                        t0[:, :], d3sb[:, ssl * 512:ssl * 512 + 512],
                                        ccol[:, 8:9], ccol[:, 6:7], ALU.mult, ALU.add),
                                        waits=[("pe", sc_mm[ix])])
                                    c_ex = p.emit("vector",
                                                  lambda v, sl=sl: v.scalar_tensor_tensor(
                                                      t1[:, :], ps[sl][:, :], SCALE,
                                                      t0[:, :], ALU.mult, ALU.add),
                                                  waits=w_pt[1:] +
                                                  [("act", p.v("act"))], inc="dve")
                                    pT_done[ix] = p.emit(
                                        "scalar", lambda a, sl=sl: a.activation(
                                            out=pT[:, sl * 512:sl * 512 + 512],
                                            in_=t1[:, :], func=AF.Exp),
                                        waits=[("dve", c_ex)], inc="act")

                                if ix >= 1:
                                    emit_consumers(ix - 1, kept[ix - 1], nk)
                                if ix == nk - 1:
                                    emit_consumers(ix, jt, nk)
                            prev_exp = pT_done[nk - 1]
                            c_sum = p.emit("vector", lambda v: v.tensor_copy(
                                rowg[:1, 0:512], psr[1][:1, :]),
                                waits=[("pe", cons_on[nk - 1])], inc="dve")
                            prev_sum = p.emit("vector", lambda v: v.reciprocal(
                                rowg[:1, 0:512], rowg[:1, 0:512]), inc="dve")
                            c_nb = p.emit("tensor", lambda t: t.matmul(
                                ps[3][:, :], oner[:1, :], rowg[:1, 0:512], start=True,
                                stop=True), waits=[("dve", prev_sum)], inc="pe")
                            prev_evac = p.emit("scalar", lambda a: a.activation(
                                out=t1[:, :], in_=ps[2][:, :], func=AF.Identity),
                                waits=[("pe", cons_pv[nk - 1])], inc="act")
                            c_out = p.emit("vector", lambda v, qh=qh, qc=qc: v.tensor_tensor(
                                outT[:, qh * TO + qc * 512:qh * TO + qc * 512 + 512],
                                t1[:, :], ps[3][:, :], ALU.mult),
                                waits=[("act", prev_evac), ("pe", c_nb)], inc="dve")
                            if qc == 0:
                                p.emit("vector", lambda v, qh=qh: v.tensor_scalar(
                                    t0[:, 0:64], outT[:, qh * TO:qh * TO + 64],
                                    ccol[:, 5:6], None, ALU.mult),
                                    waits=[("dve", vse_prev)])
                                p.emit("vector", lambda v: v.tensor_scalar(
                                    t0[:, 64:128], o64[:, 0:64], ccol[:, 4:5], None,
                                    ALU.mult))
                                c_out = p.emit("vector", lambda v, qh=qh: v.tensor_tensor(
                                    outT[:, qh * TO:qh * TO + 64], t0[:, 0:64],
                                    t0[:, 64:128], ALU.add), inc="dve")
                            grp_done = p.v("dve")
                            out_written[qh] = grp_done
                    head_pe_done[kvh] = p.v("pe")
                att_done = p.v("dve")
                att_act = p.v("act")

                stats_hist = []
                sqc_hist = [None] * 32
                sth = [None] * 32
                evd = [None] * 32
                wo_mm = [None] * 32
                for ct in range(16):
                    po = (ct % 2) * 2048
                    ap = bass.AP(Wo_h, ct * 128, [[DM, 128], [128 * DM, 16], [1, 128]])
                    w_pan = [("pe", wo_mm[2 * ct - 3])] if ct >= 2 else []
                    pan_c = p.dma("gpsimd", lambda g, ap=ap, po=po: g.dma_start(
                        out=wpan[:, po:po + 2048], in_=ap), "wp", 2, waits=w_pan)
                    bo_c = p.dma("sync", lambda sy, ct=ct: sy.dma_start(
                        out=ccol[:, 0:1], in_=bo_h.ap()[ct * 128:(ct + 1) * 128, :]),
                        "bo", 1, waits=[("dve", evd[2 * ct - 1] if ct >= 1 else att_done)])
                    for th in range(2):
                        i = ct * 2 + th
                        x2_c = p.dma("sync", lambda sy, ct=ct, th=th, i=i: sy.dma_start(
                            out=x2b[:, (i % 4) * 512:(i % 4) * 512 + 512],
                            in_=x2_h.ap()[ct * 128:(ct + 1) * 128, th * 512:th * 512 + 512]),
                            "x2", 4, waits=[("dve", evd[i - 4])] if i >= 4 else [])

                        def mmw(t, po=po, th=th):
                            r = None
                            for kt in range(16):
                                r = t.matmul(ps[th][:, :],
                                             wpan[:, po + kt * 128:po + kt * 128 + 128],
                                             outT[:, kt * TO + th * 512:kt * TO + th * 512 + 512],
                                             start=(kt == 0), stop=(kt == 15))
                            return r
                        w_mm = [pan_c, ("dve", evd[i - 2] if i >= 2 else att_done)]
                        wo_mm[i] = p.emit("tensor", mmw, waits=w_mm, inc="pe")
                        w_ev = [("pe", wo_mm[i]), x2_c, bo_c]
                        if i >= 4:
                            w_ev.append(sth[i - 4])
                        if i >= 2:
                            w_ev.append(("act", sqc_hist[i - 2]))
                        elif att_act:
                            w_ev.append(("act", att_act))
                        p.emit("vector", lambda v, th=th: v.tensor_scalar(
                            t0[:, :], ps[th][:, :], ccol[:, 0:1], None, ALU.add), waits=w_ev)
                        c_ev = p.emit("vector", lambda v, i=i: v.tensor_tensor(
                            x0buf[:, (i % 4) * 512:(i % 4) * 512 + 512], t0[:, :],
                            x2b[:, (i % 4) * 512:(i % 4) * 512 + 512], ALU.add), inc="dve")
                        evd[i] = c_ev
                        sqc_hist[i] = p.emit("scalar", lambda a, i=i: a.activation(
                            out=sqk[:, (i % 2) * 512:(i % 2) * 512 + 512],
                            in_=x0buf[:, (i % 4) * 512:(i % 4) * 512 + 512], func=AF.Square),
                            waits=[("dve", c_ev)] +
                            ([("pe", stats_hist[i - 2])] if i >= 2 else []), inc="act")
                        p.emit("tensor", lambda t, th=th, ct=ct, i=i: t.matmul(
                            psr[th][:1, :], onec[:, :],
                            x0buf[:, (i % 4) * 512:(i % 4) * 512 + 512],
                            start=(ct == 0), stop=(ct == 15)),
                            waits=[("act", sqc_hist[i])] +
                            ([("dve", att_done)] if i <= 1 else []), inc="pe")
                        stats_hist.append(p.emit("tensor", lambda t, th=th, ct=ct, i=i: t.matmul(
                            psr[2 + th][:1, :], onecb[:, :],
                            sqk[:, (i % 2) * 512:(i % 2) * 512 + 512],
                            start=(ct == 0), stop=(ct == 15)), inc="pe"))
                        sth[i] = p.dma("sync", lambda sy, ct=ct, th=th, i=i: sy.dma_start(
                            out=x0d[ct * 128:(ct + 1) * 128, th * 512:th * 512 + 512],
                            in_=x0buf[:, (i % 4) * 512:(i % 4) * 512 + 512]), "so", 4,
                            waits=[("dve", c_ev)])
                stats_pe = p.v("pe")

                p.emit("vector", lambda v: v.tensor_scalar(rowm[:1, 0:512], psr[0][:1, :],
                                                           1.0 / DM, None, ALU.mult),
                       waits=[("pe", stats_pe)])
                p.emit("vector", lambda v: v.tensor_scalar(rowm[:1, 512:1024], psr[1][:1, :],
                                                           1.0 / DM, None, ALU.mult))
                p.emit("vector", lambda v: v.tensor_scalar(rowv[:1, 0:512], psr[2][:1, :],
                                                           1.0 / DM, None, ALU.mult))
                p.emit("vector", lambda v: v.tensor_scalar(rowv[:1, 512:1024], psr[3][:1, :],
                                                           1.0 / DM, None, ALU.mult))
                p.emit("vector", lambda v: v.tensor_tensor(rowg[:1, :], rowm[:1, :],
                                                           rowm[:1, :], ALU.mult))
                c_d = p.emit("vector", lambda v: v.tensor_tensor(rowv[:1, :], rowv[:1, :],
                                                                 rowg[:1, :], ALU.subtract),
                             inc="dve")
                p.emit("scalar", lambda a: a.activation(out=rowr[:1, :], in_=rowv[:1, :],
                                                        func=AF.Ln, bias=epsc[0:1, 0:1]),
                       waits=[("dve", c_d)])
                c_a = p.emit("scalar", lambda a: a.activation(out=rowr[:1, :], in_=rowr[:1, :],
                                                              func=AF.Exp, scale=-0.5),
                             inc="act")
                prev = [None] * 32
                last_dve = None
                for th in range(2):
                    p.emit("tensor", lambda t, th=th: t.matmul(
                        ps[2][:, :], oner[:1, :], rowm[:1, th * 512:th * 512 + 512],
                        start=True, stop=True), waits=[("act", c_a)] +
                        ([("dve", last_dve)] if th == 1 else []), inc="pe")
                    c_bc2 = p.emit("tensor", lambda t, th=th: t.matmul(
                        ps[3][:, :], oner[:1, :], rowr[:1, th * 512:th * 512 + 512],
                        start=True, stop=True), inc="pe")
                    for ct in range(16):
                        i = th * 16 + ct
                        xsl = (i % 2) * 512
                        w_rl = [sth[j] for j in range(28, 32)] if i == 0 else []
                        if i >= 2:
                            w_rl += [("dve", prev[i - 2][0]), prev[i - 2][1]]
                        rl = p.dma("scalar", lambda sq, ct=ct, th=th, xsl=xsl: sq.dma_start(
                            out=x0buf[:, xsl:xsl + 512],
                            in_=x0d[ct * 128:(ct + 1) * 128, th * 512:th * 512 + 512]),
                            "rl", 2, waits=w_rl)
                        p.emit("vector", lambda v, xsl=xsl: v.tensor_tensor(
                            t0[:, :], x0buf[:, xsl:xsl + 512], ps[2][:, :], ALU.subtract),
                            waits=[rl] + ([("pe", c_bc2)] if ct == 0 else []))
                        c_n = p.emit("vector", lambda v, xsl=xsl: v.tensor_tensor(
                            x0buf[:, xsl:xsl + 512], t0[:, :], ps[3][:, :], ALU.mult),
                            inc="dve")
                        p.emit("vector", lambda v, xsl=xsl: v.tensor_reduce(
                            mxa[:, 1:2], x0buf[:, xsl:xsl + 512], AX.X, ALU.max,
                            apply_absolute_value=True))
                        p.emit("vector", lambda v: v.tensor_tensor(
                            mxa[:, 0:1], mxa[:, 0:1], mxa[:, 1:2], ALU.max))
                        c_o = p.dma("sync", lambda sy, ct=ct, th=th, xsl=xsl: sy.dma_start(
                            out=x3T_o[ct * 128:(ct + 1) * 128, th * 512:th * 512 + 512],
                            in_=x0buf[:, xsl:xsl + 512]), "xo", 2, waits=[("dve", c_n)])
                        prev[i] = (c_n, c_o)
                    last_dve = p.v("dve")
                c_amx = p.v("dve")
                c_tr = p.emit("tensor", lambda t: t.transpose(psr[0][:1, 0:128], mxa[:, 0:1],
                                                              idn[:, :]),
                              waits=[("dve", c_amx), ("dve", c_idn)], inc="pe")
                c_am = p.emit("vector", lambda v: v.tensor_reduce(
                    srow[:1, 0:1], psr[0][:1, 0:128], AX.X, ALU.max), waits=[("pe", c_tr)],
                    inc="dve")
                p.dma("sync", lambda sy: sy.dma_start(out=amx_o[:, :], in_=srow[:1, 0:1]),
                      "am", 1, waits=[("dve", c_am)])

        _phase3()

        c_ar = p.emit("gpsimd", lambda g: g.collective_compute(
            "AllReduce", mybir.AluOpType.max, replica_groups=rg,
            ins=[arin_h.ap().opt()], outs=[arout_h.ap().opt()]),
            waits=[("am0", p.c["am0"])])
        allw = [(n, p.c[n]) for n in list(p.h.keys())
                if n not in ("m1", "m2", "m3")]
        p.emit("sync", lambda sy: None, waits=allw, inc="m3", inck=1)
        p.arm_barrier([("m3", 1)])
        if nph < 4:
            dbgo = nc.dram_tensor("dbgo", [1, 1], F32, kind="ExternalOutput")
            sdb = p.dma("gpsimd", lambda g: g.dma_start(
                out=dbgo.ap()[:, :], in_=a2a_out_h.ap()[0:1, 0:1]), "db", 1,
                waits=[("gp", p.c["gp"])])
            allw = [(n, p.c[n]) for n in list(p.h.keys())]
            p.emit("sync", lambda sy: None, waits=allw)
            with nc.Block() as block:
                p.replay(block)
            return nc
        def _phase4():
            x3T_h = x3T_i
            wT_h = nc.dram_tensor("wT", [DM, DM], BF16, kind="ExternalInput")
            am8_h = arout_h
            yT_o = nc.dram_tensor("yT", [DM, TO], F32, kind="ExternalOutput").ap()
            x0d = nc.dram_tensor("x0d4", [DM, TO], F32,
                                 kind="ExternalOutput" if dbg else "Internal").ap()
            x3T = x3T_h.ap()

            with ExitStack() as ctx:
                e = ctx.enter_context
                x3 = e(nc.sbuf_tensor("f4x3", [128, 16 * TO], F32))
                wsb = e(nc.sbuf_tensor("f4wsb", [128, 16 * DM], BF16))
                xq = e(nc.sbuf_tensor("f4xq", [128, 16 * TO], BF16))
                t0 = e(nc.sbuf_tensor("f4t0", [128, 512], F32))
                t1 = e(nc.sbuf_tensor("f4t1", [128, 512], F32))
                t2 = e(nc.sbuf_tensor("f4t2", [128, 512], F32))
                t3 = e(nc.sbuf_tensor("f4t3", [128, 512], F32))
                rowm = e(nc.sbuf_tensor("f4rowm", [1, TO], F32))
                rowv = e(nc.sbuf_tensor("f4rowv", [1, TO], F32))
                rowr = e(nc.sbuf_tensor("f4rowr", [1, TO], F32))
                rowg = e(nc.sbuf_tensor("f4rowg", [1, TO], F32))
                onec = e(nc.sbuf_tensor("f4onec", [128, 1], F32))
                onecb = e(nc.sbuf_tensor("f4onecb", [128, 1], BF16))
                oner = e(nc.sbuf_tensor("f4oner", [1, 128], F32))
                epsc = e(nc.sbuf_tensor("f4epsc", [1, 1], F32))
                idn = e(nc.sbuf_tensor("f4idn", [128, 128], F32))
                mxc = e(nc.sbuf_tensor("f4mxc", [128, 4], F32))
                srow = e(nc.sbuf_tensor("f4srow", [1, 16], F32))
                sqs = e(nc.sbuf_tensor("f4sqs", [128, 2 * 512], BF16))
                x0buf = e(nc.sbuf_tensor("f4x0buf", [128, 4 * 512], F32))
                ps = [e(nc.psum_tensor(f"f4ps{i}", [128, 512], F32)) for i in range(4)]
                psr = [e(nc.psum_tensor(f"f4psr{i}", [1, 512], F32)) for i in range(4)]


                p.emit("gpsimd", lambda g: g.memset(onec[:, :], 1.0), gpser=True)
                p.emit("gpsimd", lambda g: g.memset(onecb[:, :], 1.0), gpser=True)
                p.emit("gpsimd", lambda g: g.memset(oner[:1, :], 1.0), gpser=True)
                p.emit("gpsimd", lambda g: g.memset(epsc[:1, :], 1e-5), gpser=True)
                p.emit("gpsimd", lambda g: g.iota(idn[:, :], [[-1, 128]], base=0,
                                                  channel_multiplier=1,
                                                  allow_small_or_imprecise_dtypes=True),
                       gpser=True)
                init_gp = p.v("gp")
                c_idn = p.emit("vector", lambda v: v.tensor_scalar(
                    idn[:, :], idn[:, :], 0.0, None, ALU.is_equal), waits=[("gp", init_gp)],
                    inc="dve")

                x3_ld = []
                for kt in range(16):
                    x3_ld.append(p.dma("sync", lambda sy, kt=kt: sy.dma_start(
                        out=x3[:, kt * TO:(kt + 1) * TO], in_=x3T[kt * 128:(kt + 1) * 128, :]),
                        "lx", 4))
                w_ld = []
                for kt in range(16):
                    w_ld.append(p.dma("scalar", lambda sq, kt=kt: sq.dma_start(
                        out=wsb[:, kt * DM:(kt + 1) * DM],
                        in_=wT_h.ap()[kt * 128:(kt + 1) * 128, :]), "lw", 4))
                c_amz = p.emit("gpsimd", lambda g: g.memset(srow[:1, 0:8], 0.0))
                am_ld = p.dma("sync", lambda sy: sy.dma_start(out=srow[:1, 0:1],
                                                              in_=am8_h.ap()[:, :]), "la", 1,
                              waits=[("gp", c_amz)])

                for kt in range(16):
                    p.emit("vector", lambda v, kt=kt: v.tensor_reduce(
                        mxc[:, 1:2], wsb[:, kt * DM:(kt + 1) * DM], AX.X, ALU.max,
                        apply_absolute_value=True), waits=[w_ld[kt]])
                    if kt == 0:
                        p.emit("vector", lambda v: v.tensor_copy(mxc[:, 0:1], mxc[:, 1:2]))
                    else:
                        p.emit("vector", lambda v: v.tensor_tensor(
                            mxc[:, 0:1], mxc[:, 0:1], mxc[:, 1:2], ALU.max))
                c_wmax = p.emit("vector", lambda v: None, inc="dve")
                c_tr = p.emit("tensor", lambda t: t.transpose(psr[0][:1, 0:128], mxc[:, 0:1],
                                                              idn[:, :]),
                              waits=[("dve", c_wmax), ("dve", c_idn)], inc="pe")
                p.emit("vector", lambda v: v.tensor_reduce(
                    srow[:1, 8:9], psr[0][:1, 0:128], AX.X, ALU.max), waits=[("pe", c_tr)])
                p.emit("vector", lambda v: v.tensor_reduce(
                    srow[:1, 9:10], srow[:1, 0:8], AX.X, ALU.max), waits=[am_ld])
                p.emit("vector", lambda v: v.tensor_scalar(
                    srow[:1, 10:12], srow[:1, 8:10], 1.0 / 448.0, 1e-12, ALU.mult, ALU.max))
                p.emit("vector", lambda v: v.reciprocal(srow[:1, 12:14], srow[:1, 10:12]))
                c_sc = p.emit("vector", lambda v: v.tensor_tensor(
                    srow[:1, 14:15], srow[:1, 10:11], srow[:1, 11:12], ALU.mult), inc="dve")
                c_bc = p.emit("tensor", lambda t: t.matmul(ps[0][:, 0:3], oner[:1, :],
                                                           srow[:1, 12:15], start=True,
                                                           stop=True),
                              waits=[("dve", c_sc)], inc="pe")
                c_sccol = p.emit("vector", lambda v: v.tensor_copy(mxc[:, 1:4], ps[0][:, 0:3]),
                                 waits=[("pe", c_bc)], inc="dve")

                def e4m3(dst, src, scol, extra_waits, eng="vector"):
                    g = (lambda fn, w=(): p.emit(eng, fn, waits=w,
                                                 gpser=(eng == "gpsimd")))
                    g(lambda v: v.tensor_scalar(t0[:, :], src, scol, None, ALU.mult),
                      extra_waits)
                    g(lambda v: v.tensor_scalar(t1[:, :], t0[:, :], C1, None, ALU.mult))
                    g(lambda v: v.tensor_tensor(t2[:, :], t1[:, :], t0[:, :], ALU.subtract))
                    g(lambda v: v.tensor_tensor(t1[:, :], t1[:, :], t2[:, :], ALU.subtract))
                    g(lambda v: v.tensor_scalar(t2[:, :], t0[:, :], C2, -C2, ALU.add, ALU.add))
                    g(lambda v: v.tensor_tensor(t3[:, :], t0[:, :], t0[:, :], ALU.mult))
                    g(lambda v: v.tensor_scalar(t3[:, :], t3[:, :], float(2 ** -12), None,
                                                ALU.is_lt))
                    g(lambda v: v.tensor_tensor(t2[:, :], t2[:, :], t1[:, :], ALU.subtract))
                    g(lambda v: v.tensor_tensor(t3[:, :], t3[:, :], t2[:, :], ALU.mult))
                    return p.emit(eng, lambda v: v.tensor_tensor(dst, t1[:, :], t3[:, :],
                                                                 ALU.add),
                                  gpser=(eng == "gpsimd"))

                g0 = e(nc.sbuf_tensor("f4g0", [128, 512], F32))
                g1 = e(nc.sbuf_tensor("f4g1", [128, 512], F32))
                g2 = e(nc.sbuf_tensor("f4g2", [128, 512], F32))
                g3 = e(nc.sbuf_tensor("f4g3", [128, 512], F32))

                def e4m3_gp(dst, src, scol, extra_waits):
                    g = (lambda fn, w=(): p.emit("gpsimd", fn, waits=w, gpser=True))
                    g(lambda v: v.tensor_scalar(g0[:, :], src, scol, None, ALU.mult),
                      extra_waits)
                    g(lambda v: v.tensor_scalar(g1[:, :], g0[:, :], C1, None, ALU.mult))
                    g(lambda v: v.tensor_tensor(g2[:, :], g1[:, :], g0[:, :], ALU.subtract))
                    g(lambda v: v.tensor_tensor(g1[:, :], g1[:, :], g2[:, :], ALU.subtract))
                    g(lambda v: v.tensor_scalar(g2[:, :], g0[:, :], C2, -C2, ALU.add, ALU.add))
                    g(lambda v: v.tensor_tensor(g3[:, :], g0[:, :], g0[:, :], ALU.mult))
                    g(lambda v: v.tensor_scalar(g3[:, :], g3[:, :], float(2 ** -12), None,
                                                ALU.is_lt))
                    g(lambda v: v.tensor_tensor(g2[:, :], g2[:, :], g1[:, :], ALU.subtract))
                    g(lambda v: v.tensor_tensor(g3[:, :], g3[:, :], g2[:, :], ALU.mult))
                    return g(lambda v: v.tensor_tensor(dst, g1[:, :], g3[:, :], ALU.add))

                xq_done = [None] * 32
                for i in range(32):
                    kt, th = i // 2, i % 2
                    w0 = [("dve", c_sccol), x3_ld[kt]] if i <= 3 else [x3_ld[kt]]
                    xq_done[i] = e4m3(xq[:, kt * TO + th * 512:kt * TO + th * 512 + 512],
                                      x3[:, kt * TO + th * 512:kt * TO + th * 512 + 512],
                                      mxc[:, 2:3], w0)
                wq_done = [None] * 64
                for i in range(64):
                    kt, c4 = i // 4, i % 4
                    w0 = [("dve", c_sccol), w_ld[kt]] if True else []
                    wq_done[i] = e4m3(wsb[:, kt * DM + c4 * 512:kt * DM + c4 * 512 + 512],
                                      wsb[:, kt * DM + c4 * 512:kt * DM + c4 * 512 + 512],
                                      mxc[:, 1:2], w0)
                xq_all = p.v("dve")
                wq_all = p.v("dve")

                stats_hist = []
                sth = [None] * 32
                evd = [None] * 32
                sqc_hist = [None] * 32
                for ct in range(16):
                    def mm_th(th):
                        def f(t, th=th, ct=ct):
                            r = None
                            for kt in range(16):
                                r = t.matmul(ps[th][:, :],
                                             wsb[:, kt * DM + ct * 128:kt * DM + ct * 128 + 128],
                                             xq[:, kt * TO + th * 512:kt * TO + th * 512 + 512],
                                             start=(kt == 0), stop=(kt == 15))
                            return r
                        return f
                    w_mm = [("dve", xq_all), ("dve", wq_all)] if ct == 0 else \
                        [("dve", evd[2 * ct - 1])]
                    mmc = [p.emit("tensor", mm_th(0), waits=w_mm, inc="pe"),
                           p.emit("tensor", mm_th(1), inc="pe")]
                    for th in range(2):
                        i = ct * 2 + th
                        xsl = (i % 4) * 512
                        w_ev = [("pe", mmc[th])]
                        if i >= 4:
                            w_ev.append(sth[i - 4])
                        if i >= 2:
                            w_ev.append(("act", sqc_hist[i - 2]))
                        p.emit("vector", lambda v, th=th, xsl=xsl: v.tensor_scalar(
                            sqs[:, (th % 2) * 512:(th % 2) * 512 + 512], ps[th][:, :],
                            mxc[:, 3:4], None, ALU.mult), waits=w_ev)
                        c_ev = p.emit("vector", lambda v, ct=ct, th=th, xsl=xsl: v.tensor_tensor(
                            x0buf[:, xsl:xsl + 512],
                            x3[:, ct * TO + th * 512:ct * TO + th * 512 + 512],
                            sqs[:, (th % 2) * 512:(th % 2) * 512 + 512], ALU.add), inc="dve")
                        evd[i] = c_ev
                        sq_c = p.emit("scalar", lambda a, xsl=xsl, i=i: a.activation(
                            out=sqs[:, (i % 2) * 512:(i % 2) * 512 + 512],
                            in_=x0buf[:, xsl:xsl + 512], func=AF.Square),
                            waits=[("dve", c_ev)] + ([("pe", stats_hist[i - 2])] if i >= 2 else []),
                            inc="act")
                        sqc_hist[i] = sq_c
                        p.emit("tensor", lambda t, th=th, ct=ct, xsl=xsl: t.matmul(
                            psr[th][:1, :], onec[:, :], x0buf[:, xsl:xsl + 512],
                            start=(ct == 0), stop=(ct == 15)), waits=[("act", sq_c)], inc="pe")
                        stats_hist.append(p.emit("tensor", lambda t, th=th, ct=ct, i=i: t.matmul(
                            psr[2 + th][:1, :], onecb[:, :],
                            sqs[:, (i % 2) * 512:(i % 2) * 512 + 512],
                            start=(ct == 0), stop=(ct == 15)), inc="pe"))
                        sth[i] = p.dma("sync", lambda sy, ct=ct, th=th, xsl=xsl: sy.dma_start(
                            out=x0d[ct * 128:(ct + 1) * 128, th * 512:th * 512 + 512],
                            in_=x0buf[:, xsl:xsl + 512]), "so", 4, waits=[("dve", c_ev)])
                stats_pe = p.v("pe")


                p.emit("vector", lambda v: v.tensor_scalar(rowm[:1, 0:512], psr[0][:1, :],
                                                           1.0 / DM, None, ALU.mult),
                       waits=[("pe", stats_pe)])
                p.emit("vector", lambda v: v.tensor_scalar(rowm[:1, 512:1024], psr[1][:1, :],
                                                           1.0 / DM, None, ALU.mult))
                p.emit("vector", lambda v: v.tensor_scalar(rowv[:1, 0:512], psr[2][:1, :],
                                                           1.0 / DM, None, ALU.mult))
                p.emit("vector", lambda v: v.tensor_scalar(rowv[:1, 512:1024], psr[3][:1, :],
                                                           1.0 / DM, None, ALU.mult))
                p.emit("vector", lambda v: v.tensor_tensor(rowg[:1, :], rowm[:1, :],
                                                           rowm[:1, :], ALU.mult))
                c_d = p.emit("vector", lambda v: v.tensor_tensor(rowv[:1, :], rowv[:1, :],
                                                                 rowg[:1, :], ALU.subtract),
                             inc="dve")
                p.emit("scalar", lambda a: a.activation(out=rowr[:1, :], in_=rowv[:1, :],
                                                        func=AF.Ln, bias=epsc[0:1, 0:1]),
                       waits=[("dve", c_d)])
                c_a = p.emit("scalar", lambda a: a.activation(out=rowr[:1, :], in_=rowr[:1, :],
                                                              func=AF.Exp, scale=-0.5),
                             inc="act")
                prev = [None] * 32
                last_dve = None
                for th in range(2):
                    p.emit("tensor", lambda t, th=th: t.matmul(
                        ps[2][:, :], oner[:1, :], rowm[:1, th * 512:th * 512 + 512],
                        start=True, stop=True), waits=[("act", c_a)] +
                        ([("dve", last_dve)] if th == 1 else []), inc="pe")
                    c_bc2 = p.emit("tensor", lambda t, th=th: t.matmul(
                        ps[3][:, :], oner[:1, :], rowr[:1, th * 512:th * 512 + 512],
                        start=True, stop=True), inc="pe")
                    for ct in range(16):
                        i = th * 16 + ct
                        xsl = (i % 2) * 512
                        w_rl = [sth[j] for j in range(28, 32)] if i == 0 else []
                        if i >= 2:
                            w_rl += [("dve", prev[i - 2][0]), prev[i - 2][1]]
                        rl = p.dma("scalar", lambda sq, ct=ct, th=th, xsl=xsl: sq.dma_start(
                            out=x0buf[:, xsl:xsl + 512],
                            in_=x0d[ct * 128:(ct + 1) * 128, th * 512:th * 512 + 512]),
                            "rl", 2, waits=w_rl)
                        p.emit("vector", lambda v, xsl=xsl: v.tensor_tensor(
                            t0[:, :], x0buf[:, xsl:xsl + 512], ps[2][:, :], ALU.subtract),
                            waits=[rl] + ([("pe", c_bc2)] if ct == 0 else []))
                        c_n = p.emit("vector", lambda v, xsl=xsl: v.tensor_tensor(
                            x0buf[:, xsl:xsl + 512], t0[:, :], ps[3][:, :], ALU.mult),
                            inc="dve")
                        c_o = p.dma("sync", lambda sy, ct=ct, th=th, xsl=xsl: sy.dma_start(
                            out=yT_o[ct * 128:(ct + 1) * 128, th * 512:th * 512 + 512],
                            in_=x0buf[:, xsl:xsl + 512]), "xo", 2, waits=[("dve", c_n)])
                        prev[i] = (c_n, c_o)
                    last_dve = p.v("dve")

        _phase4()

        allw = [(n, p.c[n]) for n in list(p.h.keys())]
        p.emit("sync", lambda sy: None, waits=allw)
        assert len(p.h) <= 100, f"too many sems: {len(p.h)}"
        with nc.Block() as block:
            p.replay(block)
    return nc


def kernel(**inputs):
    z = {k: np.asarray(v) for k, v in inputs.items()}
    hid = z["L0_attn_hidden_states"].reshape(SEQ, DM)
    hidT = np.ascontiguousarray(hid.T)
    w1T = np.ascontiguousarray(z["L0_ffn_w1"].T)
    w2T = np.ascontiguousarray(z["L0_ffn_w2"].T)
    w3T = np.ascontiguousarray(z["L0_ffn_w3"].T)
    half = 64
    freqs = 1.0 / 10000.0 ** (np.arange(0, half, dtype=np.float64) / half)
    f = np.outer(np.arange(SEQ, dtype=np.float64), freqs)
    sc = np.concatenate([np.sin(f).T.astype(np.float32),
                         np.cos(f).T.astype(np.float32)], axis=0)
    wT = np.ascontiguousarray(_bf16(z["L1_ffn_weight"]).T)
    nc = build_fused(Prog)
    in_maps = []
    p_ = np.arange(128, dtype=np.uint32)
    for c in range(NCORES):
        m = host_prep_l1(z, c)
        m["hidT"] = hidT
        sl = slice(c * TPC, (c + 1) * TPC)
        m["hT"] = np.ascontiguousarray(hidT[:, sl])
        m.update(Wout=z["L0_attn_W_out"], w1T=w1T, w2T=w2T, w3T=w3T,
                 Wkv=z["L1_attn_Wkv"], Wq=z["L1_attn_Wq"],
                 sc=np.ascontiguousarray(sc[:, sl]))
        SJ, SG, PEN, u0, irow, hrow, cstc = host_prep_l3(c)
        m.update(Wo=z["L1_attn_Wo"], bo2=z["L1_attn_bo"].reshape(DM, 1),
                 irow=irow, SJ=SJ, SG=SG, PEND3=PEN, u0=u0, hrow=hrow,
                 cstc=cstc, wT=wT)
        m["SGb"] = np.ascontiguousarray(
            np.broadcast_to(SG, (4, SG.shape[1]))).astype(BF)
        idx = np.zeros((128, 64), np.uint32)
        for kvh in range(8):
            for rp in range(8):
                idx[:, kvh * 8 + rp] = 1024 * ((c + rp) % 8) + kvh * 128 + p_
        m["idxk"] = idx
        in_maps.append(m)
    res = run_bass_kernel_spmd(nc, in_maps, core_ids=list(range(NCORES)))
    out = np.concatenate([np.asarray(res.results[c]["yT"]).T
                          for c in range(NCORES)], axis=0)
    return np.ascontiguousarray(out.astype(np.float32)).reshape(1, SEQ, DM)
